# version 30
# baseline (speedup 1.0000x reference)
import contextlib
import numpy as np

import concourse.bass as bass
import concourse.mybir as mybir
from concourse.bass_utils import run_bass_kernel_spmd

F32 = mybir.dt.float32
BF16 = mybir.dt.bfloat16
AF = mybir.ActivationFunctionType
OP = mybir.AluOpType
AX = mybir.AxisListType

C_T = float(np.float32(np.float32(0.5009765625) * np.float32(1.0 + 2.0 ** -20)))
V45C = float(2.0 ** 45)
V29C = float(2.0 ** 29)
K0 = 1.000244140625
NEG_BIG = -1e38
LN_EPS = 1e-5
RMS_EPS = 1.1920929e-7
ENGINES = ("tensor", "vector", "scalar", "gpsimd", "sync")


def full_cfg():
    return dict(H=2048, T=2048, I=8192, R=256, TC=512)


class Emitter:

    KQ = 8

    def __init__(self):
        self.ops = []
        self.last_writer = {}
        self.readers = {}
        self.qcount = {}
        self.qlast = {}

    def op(self, eng, fn, r=(), w=(), dma=False, tag="", extra_deps=()):
        idx = len(self.ops)
        deps = set(extra_deps)
        same = False

        def classify(j):
            nonlocal same
            rec = self.ops[j]
            if rec["eng"] == eng and rec["dma"] == dma and not dma:
                same = True
            else:
                deps.add(j)

        for reg in r:
            for wj in self.last_writer.get(reg, ()):
                classify(wj)
        for reg in w:
            for wj in self.last_writer.get(reg, ()):
                classify(wj)
            for rd in self.readers.get(reg, ()):
                classify(rd)
        self.ops.append(dict(eng=eng, fn=fn, dma=dma, deps=deps, tag=tag,
                             need_inc=False, drain_before=same))
        if dma:
            qi = self.qcount.get(eng, 0)
            self.qcount[eng] = qi + 1
            self.qlast[(eng, qi % self.KQ)] = idx
        for reg in r:
            self.readers.setdefault(reg, []).append(idx)
        for reg in w:
            if dma and self.last_writer.get(reg):
                self.last_writer[reg].append(idx)
            else:
                self.last_writer[reg] = [idx]
            self.readers[reg] = []
        return idx

    def barrier(self, tag="bar"):
        last = {}
        for i, rec in enumerate(self.ops):
            if not rec["dma"]:
                last[rec["eng"]] = i
        deps = tuple(last.values()) + tuple(self.qlast.values())
        for e in ENGINES:
            self.op(e, lambda en: en.nop(), extra_deps=deps, tag=tag + "_" + e)
        self.last_writer = {}
        self.readers = {}

    def finalize(self, block, sems):
        for rec in self.ops:
            for d in rec["deps"]:
                self.ops[d]["need_inc"] = True
        for rec in self.ops:
            if rec["dma"]:
                rec["need_inc"] = True
        counts = {}
        qidx = {}
        for rec in self.ops:
            if rec["dma"]:
                q = rec["eng"]
                i = qidx.get(q, 0)
                qidx[q] = i + 1
                sub = i % self.KQ
                key = (q, True, sub)
                counts[key] = counts.get(key, 0) + 16
                rec["sem_key"] = key
                rec["sem_val"] = counts[key]
            elif rec["need_inc"]:
                key = (rec["eng"], False)
                counts[key] = counts.get(key, 0) + 1
                rec["sem_key"] = key
                rec["sem_val"] = counts[key]
            else:
                rec["sem_key"] = rec["sem_val"] = None
        for rec in self.ops:
            waits = {}
            for d in rec["deps"]:
                dep = self.ops[d]
                waits[dep["sem_key"]] = max(waits.get(dep["sem_key"], 0),
                                            dep["sem_val"])
            if not rec["dma"]:
                waits.pop((rec["eng"], False), None)
            rec["waits"] = waits
        self.final_counts = counts

        def run(eng_name, e):
            seen = {}
            for rec in self.ops:
                if rec["eng"] != eng_name:
                    continue
                for key in sorted(rec["waits"], key=str):
                    v = rec["waits"][key]
                    if seen.get(key, -1) < v:
                        e.wait_ge(sems[key], v)
                        seen[key] = v
                if rec["dma"] and rec["sem_val"] > 16:
                    k2, v2 = rec["sem_key"], rec["sem_val"] - 16
                    if seen.get(k2, -1) < v2:
                        e.wait_ge(sems[k2], v2)
                        seen[k2] = v2
                if rec["drain_before"] and eng_name in ("vector", "scalar", "gpsimd"):
                    e.drain()
                inst = rec["fn"](e)
                if rec["sem_key"] is not None:
                    if inst is None:
                        inst = e.nop()
                    inst.then_inc(sems[rec["sem_key"]], 16 if rec["dma"] else 1)

        @block.tensor
        def _(e):
            run("tensor", e)

        @block.vector
        def _(e):
            run("vector", e)

        @block.scalar
        def _(e):
            run("scalar", e)

        @block.gpsimd
        def _(e):
            run("gpsimd", e)

        @block.sync
        def _(e):
            run("sync", e)
            for key in sorted(self.final_counts, key=str):
                if key[1] is True:
                    e.wait_ge(sems[key], self.final_counts[key])


def build_nc(cfg):
    H, T, I, R, TC = cfg["H"], cfg["T"], cfg["I"], cfg["R"], cfg["TC"]
    HC = H // 128
    TN = T // TC
    IC = I // 128
    RC = R // 128
    JB = T // 128
    TCB = 128
    GRP = 64
    QW = 512 if H >= 1024 else H
    assert TC <= 512

    nc = bass.Bass("TRN2", target_bir_lowering=False)
    x_in = nc.dram_tensor("x_in", [H, T], F32, kind="ExternalInput").ap()
    y_out = nc.dram_tensor("y_out", [H, T], F32, kind="ExternalOutput").ap()
    w_in = {}
    for name, shape in [
        ("wkT", [H, H]), ("wvT", [H, H]), ("wrT", [H, H]), ("woT", [H, H]),
        ("ln_w", [H]), ("ln_b", [H]), ("mix_k", [H]), ("mix_v", [H]),
        ("mix_r", [H]), ("td", [H]), ("cw", [H]),
        ("w_gate", [H, I]), ("w_up", [H, I]), ("w_down", [I, H]),
        ("ws", [H, R]), ("wa", [H, R]), ("woT1", [H, H]), ("bo", [H]),
        ("w_fc", [I, H]), ("w_proj", [H, I]),
    ]:
        w_in[name] = nc.dram_tensor(name, shape, F32, kind="ExternalInput").ap()
    tri_in = nc.dram_tensor("tri", [128, 128], F32, kind="ExternalInput").ap()
    qfc = nc.dram_tensor("qfc", [I, H], BF16, kind="Internal").ap()
    qpr = nc.dram_tensor("qpr", [H, I], BF16, kind="Internal").ap()
    PRECONV = ["wkT", "wvT", "wrT", "woT", "w_gate", "w_up", "w_down", "woT1",
               "ws", "wa"]
    wbf = {n: nc.dram_tensor("bf_" + n, list(w_in[n].shape), BF16,
                             kind="Internal").ap() for n in PRECONV}

    em = Emitter()
    es = contextlib.ExitStack()

    def sb(name, shape, dtype=F32):
        return es.enter_context(nc.sbuf_tensor(name, shape, dtype))

    res = [sb(f"res{c}", [128, T], BF16) for c in range(HC)]
    pc = {n: sb("pc_" + n, [128, HC]) for n in
          ["ln_w", "ln_b", "mix_k", "mix_v", "mix_r", "cw", "wdec", "bo", "ecw"]}
    ones_bf = sb("ones_bf", [128, 1], BF16)
    ones1_f = sb("ones1_f", [1, 128], F32)
    cst = sb("cst", [128, 8])
    C_HALF, C_NHALF, C_ELN, C_ERMS, C_N1, C_S1, C_S2, C_I16 = range(8)
    dcol = sb("dcol", [128, 2])
    stN = sb("stN", [128, HC])
    stD = sb("stD", [128, HC])
    xprev = sb("xprev", [128, HC])
    rw = {n: sb("rw_" + n, [1, TC]) for n in
          ["s1", "s2", "mu", "var", "l0", "rr", "tmp"]}
    rw["r0"] = rw["s1"]
    rw["vy"] = rw["tmp"]
    rw["l1"] = rw["s2"]
    mu_b = sb("mu_b", [128, TC], BF16)
    rr_b = sb("rr_b", [128, TC])
    rc_b = sb("rc_b", [128, TC], BF16)
    amask = sb("amask", [128, 128], BF16)
    tA = sb("tA", [128, TC])
    tB = sb("tB", [128, TC])
    wp = [sb(f"wp{i}", [128, 512], BF16) for i in range(8)]
    cv = [None, None]
    tC = [None]

    psum = [es.enter_context(nc.psum_tensor(f"ps{i}", [128, 512], F32))
            for i in range(6)]
    prow = es.enter_context(nc.psum_tensor("prow", [1, 512], F32))
    prow2 = es.enter_context(nc.psum_tensor("prow2", [1, 512], F32))

    sems = {}
    for eng in ENGINES:
        sems[(eng, False)] = es.enter_context(nc.semaphore("s_" + eng))
    for q in ("sync", "gpsimd"):
        for sub in range(8):
            sems[(q, True, sub)] = es.enter_context(nc.semaphore(f"sd_{q}{sub}"))

    V, S, G, TE, SY = "vector", "scalar", "gpsimd", "tensor", "sync"

    from concourse.hw_specs import get_activation_tables
    _tab_names = list(get_activation_tables(nc.m.arch).keys())

    def preload_set(e, name):
        idx = _tab_names.index(name)
        return e.add_instruction(mybir.InstLoadActFuncSet(
            name=e.bass.get_next_instruction_name(), act_func_set_id=idx,
            ins=[], outs=[]))

    for c in range(HC):
        em.op(G, lambda e, c=c: e.dma_start(out=res[c][:], in_=x_in[c * 128:(c + 1) * 128, :]),
              r=[("x_in",)], w=[("res", c)], dma=True, tag="ld_x")
    for name in ["ln_w", "ln_b", "mix_k", "mix_v", "mix_r", "cw", "bo"]:
        ap = w_in[name].rearrange("(c p) -> p c", p=128)
        em.op(SY, lambda e, n=name, a=ap: e.dma_start(out=pc[n][:], in_=a, allow_slow_non_contiguous=True),
              r=[("win", name)], w=[("pc", name)], dma=True)
    em.op(SY, lambda e: e.dma_start(out=pc["wdec"][:],
                                    in_=w_in["td"].rearrange("(c p) -> p c", p=128),
                                    allow_slow_non_contiguous=True),
          r=[("win", "td")], w=[("pc", "wdec")], dma=True)
    for i, v in [(C_HALF, 0.5), (C_NHALF, -0.5), (C_ELN, LN_EPS),
                 (C_ERMS, RMS_EPS), (C_N1, -1.0), (C_I16, 1.0 / 16.0)]:
        em.op(V, lambda e, i=i, v=v: e.memset(cst[:, i:i + 1], v), w=[("cst",)])
    em.op(V, lambda e: e.memset(ones_bf[:], 1.0), w=[("ones_bf",)])
    em.op(V, lambda e: e.memset(ones1_f[:], 1.0), w=[("ones1",)])
    em.op(V, lambda e: e.memset(stN[:], 0.0), w=[("stN",)])
    em.op(V, lambda e: e.memset(stD[:], 0.0), w=[("stD",)])
    em.op(V, lambda e: e.memset(xprev[:], 0.0), w=[("xprev",)])
    em.op(V, lambda e: e.memset(dcol[:, 0:1], K0), w=[("dcol",)])
    em.op(V, lambda e: e.tensor_scalar(out=cst[:, C_S1:C_S1 + 1], in0=dcol[:, 0:1],
                                       scalar1=V45C, scalar2=None, op0=OP.mult),
          r=[("dcol",)], w=[("cst",)])
    em.op(V, lambda e: e.tensor_tensor(out=cst[:, C_S2:C_S2 + 1], in0=cst[:, C_S1:C_S1 + 1],
                                       in1=dcol[:, 0:1], op=OP.add),
          r=[("dcol",), ("cst",)], w=[("cst",)])
    em.op(V, lambda e: e.tensor_tensor(out=cst[:, C_S1:C_S1 + 1], in0=cst[:, C_S2:C_S2 + 1],
                                       in1=dcol[:, 0:1], op=OP.subtract),
          r=[("dcol",), ("cst",)], w=[("cst",)])
    em.op(V, lambda e: e.tensor_tensor(out=dcol[:, 1:2], in0=cst[:, C_S2:C_S2 + 1],
                                       in1=cst[:, C_S1:C_S1 + 1], op=OP.subtract),
          r=[("cst",)], w=[("dcol",)])
    em.op(V, lambda e: e.tensor_scalar(out=dcol[:, 0:1], in0=dcol[:, 1:2], scalar1=-1.0,
                                       scalar2=1.0, op0=OP.mult, op1=OP.add),
          r=[("dcol",)], w=[("dcol",)], tag="D")
    em.op(S, lambda e: preload_set(e, "natural_log_exp_and_others"))
    em.op(S, lambda e: e.activation(out=pc["wdec"][:], in_=pc["wdec"][:], func=AF.Exp),
          r=[("pc", "wdec")], w=[("pc", "wdec")])
    em.op(S, lambda e: e.activation(out=pc["wdec"][:], in_=pc["wdec"][:], func=AF.Exp,
                                    scale=cst[:, C_N1:C_N1 + 1]),
          r=[("pc", "wdec"), ("cst",)], w=[("pc", "wdec")], tag="wdec")
    em.op(S, lambda e: e.activation(out=pc["ecw"][:], in_=pc["cw"][:], func=AF.Exp),
          r=[("pc", "cw")], w=[("pc", "ecw")], tag="ecw")
    em.op(G, lambda e: e.dma_start(out=amask[:], in_=tri_in[:]),
          r=[("tri_in",)], w=[("amask",)], dma=True, tag="ld_tri")

    def load_piece(b, src, skey, row0, col0, ncols=512, nrows=128):
        name = skey[1]
        bsrc = wbf[{"wg": "w_gate", "wu": "w_up", "wd": "w_down"}.get(name, name)]
        rkeys = [("wbf", name, (row0 + d) // 128, cp)
                 for d in range(0, nrows, 128)
                 for cp in range(col0 // 1024, (col0 + ncols - 1) // 1024 + 1)]
        em.op(SY, lambda e: e.dma_start(out=wp[b][0:nrows, 0:ncols],
                                        in_=bsrc[row0:row0 + nrows, col0:col0 + ncols]),
              r=rkeys, w=[("wp", b)], dma=True, tag="wld")

    def bcast_row(row, rkey, width, dst, dkey, bank, bidx):
        for t0 in range(0, width, 512):
            n = min(512, width - t0)
            em.op(TE, lambda e, t0=t0, n=n: e.matmul(bank[0:128, 0:n], ones1_f[:],
                                                     row[0:1, t0:t0 + n],
                                                     start=True, stop=True),
                  r=[rkey, ("ones1",)], w=[("psum", bidx)], tag="bc")
            em.op(S, lambda e, t0=t0, n=n: e.copy(out=dst[:, t0:t0 + n],
                                                  in_=bank[0:128, 0:n]),
                  r=[("psum", bidx)], w=[dkey], tag="bc_e")

    def ln_rows(src_of, src_keys, sq_tiles, sq_keys, n_feat, w, eps_idx,
                fused=False, rr0=False):
        nch = len(src_keys)
        for c in range(nch):
            em.op(S, lambda e, c=c: e.activation(out=sq_tiles[c][:, 0:w],
                                                 in_=src_of(c, 0, w), func=AF.Square),
                  r=[src_keys[c]], w=[sq_keys[c]], tag="sq")
        for t0 in range(0, w, 512):
            n = min(512, w - t0)
            for c in range(nch):
                em.op(TE, lambda e, c=c, t0=t0, n=n, st=(c == 0): e.matmul(
                    prow[0:1, 0:n], ones_bf[:], src_of(c, t0, t0 + n),
                    start=st, stop=(c == nch - 1)),
                    r=[("ones_bf",), src_keys[c]], w=[("prow",)], tag="sum1")
            for c in range(nch):
                em.op(TE, lambda e, c=c, t0=t0, n=n, st=(c == 0): e.matmul(
                    prow2[0:1, 0:n], ones_bf[:], sq_tiles[c][:, t0:t0 + n],
                    start=st, stop=(c == nch - 1)),
                    r=[("ones_bf",), sq_keys[c]], w=[("prow2",)], tag="sum2")
            em.op(V, lambda e, t0=t0, n=n: e.tensor_copy(out=rw["s1"][:, t0:t0 + n],
                                                         in_=prow[0:1, 0:n]),
                  r=[("prow",)], w=[("rw", "s1")])
            em.op(V, lambda e, t0=t0, n=n: e.tensor_copy(out=rw["s2"][:, t0:t0 + n],
                                                         in_=prow2[0:1, 0:n]),
                  r=[("prow2",)], w=[("rw", "s2")])
        inv = 1.0 / n_feat
        em.op(V, lambda e: e.tensor_scalar(out=rw["mu"][:, 0:w], in0=rw["s1"][:, 0:w],
                                           scalar1=inv, scalar2=None, op0=OP.mult),
              r=[("rw", "s1")], w=[("rw", "mu")])
        em.op(V, lambda e: e.tensor_tensor(out=rw["tmp"][:, 0:w], in0=rw["mu"][:, 0:w],
                                           in1=rw["mu"][:, 0:w], op=OP.mult),
              r=[("rw", "mu")], w=[("rw", "tmp")])
        em.op(V, lambda e: e.tensor_scalar(out=rw["var"][:, 0:w], in0=rw["s2"][:, 0:w],
                                           scalar1=inv, scalar2=None, op0=OP.mult),
              r=[("rw", "s2")], w=[("rw", "var")])
        em.op(V, lambda e: e.tensor_tensor(out=rw["var"][:, 0:w], in0=rw["var"][:, 0:w],
                                           in1=rw["tmp"][:, 0:w], op=OP.subtract),
              r=[("rw", "var"), ("rw", "tmp")], w=[("rw", "var")])
        em.op(S, lambda e: e.activation(out=rw["l0"][:, 0:w], in_=rw["var"][:, 0:w],
                                        func=AF.Ln, bias=cst[0:1, eps_idx:eps_idx + 1]),
              r=[("rw", "var"), ("cst",)], w=[("rw", "l0")])
        if fused:
            em.op(S, lambda e: e.activation(out=rw["r0"][:, 0:w], in_=rw["l0"][:, 0:w],
                                            func=AF.Exp, scale=cst[0:1, C_N1:C_N1 + 1]),
                  r=[("rw", "l0"), ("cst",), ("rw", "s1")], w=[("rw", "s1")])
            em.op(V, lambda e: e.tensor_tensor(out=rw["vy"][:, 0:w], in0=rw["var"][:, 0:w],
                                               in1=rw["r0"][:, 0:w], op=OP.mult),
                  r=[("rw", "var"), ("rw", "s1"), ("rw", "tmp")], w=[("rw", "tmp")])
            em.op(S, lambda e: e.activation(out=rw["l1"][:, 0:w], in_=rw["vy"][:, 0:w],
                                            func=AF.Ln, bias=cst[0:1, eps_idx:eps_idx + 1]),
                  r=[("rw", "tmp"), ("cst",)], w=[("rw", "s2")])
            em.op(V, lambda e: e.tensor_tensor(out=rw["tmp"][:, 0:w], in0=rw["l1"][:, 0:w],
                                               in1=rw["l0"][:, 0:w], op=OP.add),
                  r=[("rw", "s2"), ("rw", "l0")], w=[("rw", "tmp")])
            em.op(S, lambda e: e.activation(out=rw["rr"][:, 0:w], in_=rw["tmp"][:, 0:w],
                                            func=AF.Exp, scale=cst[0:1, C_NHALF:C_NHALF + 1]),
                  r=[("rw", "tmp"), ("cst",)], w=[("rw", "rr")])
        else:
            em.op(S, lambda e: e.activation(out=rw["rr"][:, 0:w], in_=rw["l0"][:, 0:w],
                                            func=AF.Exp, scale=cst[0:1, C_NHALF:C_NHALF + 1]),
                  r=[("rw", "l0"), ("cst",)], w=[("rw", "rr")])
        bcast_row(rw["mu"][:], ("rw", "mu"), w, mu_b, ("mu_b",), psum[4], 4)
        bcast_row(rw["rr"][:], ("rw", "rr"), w, rr_b, ("rr_b",), psum[5], 5)
        if rr0:
            em.op(S, lambda e: e.activation(out=rw["tmp"][:, 0:w], in_=rw["l0"][:, 0:w],
                                            func=AF.Exp, scale=cst[0:1, C_NHALF:C_NHALF + 1]),
                  r=[("rw", "l0"), ("cst",)], w=[("rw", "tmp")])
            bcast_row(rw["tmp"][:], ("rw", "tmp"), w, rc_b, ("rc_b",), psum[4], 4)

    def res_src(tj):
        def f(c, a, b):
            return res[c][:, tj * TC + a: tj * TC + b]
        return f, [("res", c) for c in range(HC)]

    def ln_out_add(y_tiles, y_keys, sq_tiles, sq_keys, tj, off, w, final=False,
                   fin_bufs=None):
        ln_rows(lambda c, a, b: y_tiles[c][:, a:b], y_keys,
                sq_tiles, sq_keys, H, w, C_ELN)
        for c in range(HC):
            tsl = slice(tj * TC + off, tj * TC + off + w)
            em.op(V, lambda e, c=c: e.tensor_tensor(out=tA[:, 0:w],
                                                    in0=y_tiles[c][:, 0:w],
                                                    in1=mu_b[:, 0:w], op=OP.subtract),
                  r=[y_keys[c], ("mu_b",)], w=[("tA",)])
            em.op(V, lambda e: e.tensor_tensor(out=tB[:, 0:w], in0=tA[:, 0:w],
                                               in1=rr_b[:, 0:w], op=OP.mult),
                  r=[("tA",), ("rr_b",)], w=[("tB",)])
            if final:
                fb = fin_bufs[c % len(fin_bufs)]
                fk = ("finb", c % len(fin_bufs))
                em.op(V, lambda e, c=c, tsl=tsl, fb=fb: e.tensor_tensor(
                    out=fb[:, 0:w], in0=res[c][:, tsl], in1=tB[:, 0:w], op=OP.add),
                    r=[("res", c), ("tB",)], w=[fk])
                em.op(SY, lambda e, c=c, tsl=tsl, fb=fb: e.dma_start(
                    out=y_out[c * 128:(c + 1) * 128, tsl], in_=fb[:, 0:w]),
                    r=[fk], w=[("y_out",)], dma=True, tag="store_y")
            else:
                em.op(V, lambda e, c=c, tsl=tsl: e.tensor_tensor(
                    out=res[c][:, tsl], in0=res[c][:, tsl], in1=tB[:, 0:w], op=OP.add),
                    r=[("res", c), ("tB",)], w=[("res", c)], tag="res_upd")

    def preconv_all():
        i = 0
        for n in PRECONV:
            srcd = w_in[n]
            rows, cols = srcd.shape
            key = {"w_gate": "wg", "w_up": "wu", "w_down": "wd"}.get(n, n)
            for r0 in range(0, rows, 128):
                for c0 in range(0, cols, 512):
                    cw = min(512, cols - c0)
                    b = i % 2
                    i += 1
                    em.op(G, lambda e, b=b, srcd=srcd, r0=r0, c0=c0, cw=cw: e.dma_start(
                        out=cv[b][:, 0:cw], in_=srcd[r0:r0 + 128, c0:c0 + cw]),
                        r=[("win", n)], w=[("cv", b)], dma=True, tag="pc_ld")
                    em.op(G, lambda e, b=b, n=n, r0=r0, c0=c0, cw=cw: e.dma_start(
                        out=wbf[n][r0:r0 + 128, c0:c0 + cw], in_=cv[b][:, 0:cw]),
                        r=[("cv", b)], w=[("wbf", key, r0 // 128, c0 // 1024)],
                        dma=True, tag="pc_st")

    def make_quant(qes, which, pfx=""):
        qsb = lambda n, sh, dt=F32: qes.enter_context(nc.sbuf_tensor(pfx + n, sh, dt))
        NG = QW // GRP
        S2 = 1
        qw = [qsb("qw0", [128, QW])] * 2
        qc1 = [qsb(f"qc1{i}", [128, QW]) for i in range(S2)]
        qd1 = [qsb(f"qd1{i}", [128, QW]) for i in range(S2)]
        qV = qd1
        qwb = [qsb(f"qwb{i}", [128, QW], BF16) for i in range(S2)]
        qg1 = [qsb(f"qg1{i}", [128, QW], BF16) for i in range(S2)]
        qout = [qsb(f"qout{i}", [128, QW], BF16) for i in range(S2)]
        qg2 = qout
        qs_ = [qsb(f"qs{i}", [128, NG]) for i in range(S2)]
        qm = [qsb(f"qm{i}", [128, NG]) for i in range(S2)]
        qt_ = [qsb(f"qt{i}", [128, NG]) for i in range(S2)]
        qnt = [qsb(f"qnt{i}", [128, NG]) for i in range(S2)]
        qsc = [qsb(f"qsc{i}", [128, NG], BF16) for i in range(S2)]
        qtm = [qsb(f"qtm{i}", [128, NG]) for i in range(S2)]

        def bcg(t):
            a = t[:]
            return bass.AP(a.tensor, a.offset, [a.ap[0], [1, NG], [0, GRP]])

        def chunk(src_, skey, dst, dkey, r0, c0, qwi):
            s = 0
            qwt, qwk = qw[0], (pfx, "qw", 0)
            em.op(SY, lambda e: e.dma_start(out=qwt[:], in_=src_[r0:r0 + 128, c0:c0 + QW]),
                  r=[skey], w=[qwk], dma=True)
            em.op(V, lambda e: e.scalar_tensor_tensor(out=qc1[s][:], in0=qwt[:], scalar=V45C,
                                                      in1=qwt[:], op0=OP.mult, op1=OP.add),
                  r=[qwk], w=[(pfx, "qc1", s)])
            em.op(V, lambda e: e.tensor_tensor(out=qd1[s][:], in0=qc1[s][:], in1=qwt[:],
                                               op=OP.subtract),
                  r=[(pfx, "qc1", s), qwk], w=[(pfx, "qd1", s)])
            em.op(V, lambda e: e.tensor_tensor(out=qV[s][:], in0=qc1[s][:], in1=qd1[s][:],
                                               op=OP.subtract),
                  r=[(pfx, "qc1", s), (pfx, "qd1", s)], w=[(pfx, "qd1", s)])
            em.op(V, lambda e: e.scalar_tensor_tensor(out=qwb[s][:], in0=qwt[:],
                                                      scalar=dcol[:, 0:1], in1=qV[s][:],
                                                      op0=OP.mult, op1=OP.add),
                  r=[qwk, (pfx, "qd1", s), ("dcol",)], w=[(pfx, "qwb", s)], tag="q_wb")
            em.op(V, lambda e: e.tensor_reduce(
                out=qs_[s][:], in_=qwb[s][:].rearrange("p (g k) -> p g k", k=GRP),
                axis=AX.X, op=OP.add, apply_absolute_value=True),
                r=[(pfx, "qwb", s)], w=[(pfx, "qs_", s)], tag="q_red")
            em.op(V, lambda e: e.tensor_scalar(out=qs_[s][:], in0=qs_[s][:], scalar1=1.0 / GRP,
                                               scalar2=None, op0=OP.mult),
                  r=[(pfx, "qs_", s)], w=[(pfx, "qs_", s)])
            em.op(V, lambda e: e.scalar_tensor_tensor(out=qm[s][:], in0=qs_[s][:], scalar=V29C,
                                                      in1=qs_[s][:], op0=OP.mult, op1=OP.add),
                  r=[(pfx, "qs_", s)], w=[(pfx, "qm", s)])
            em.op(V, lambda e: e.tensor_tensor(out=qtm[s][:], in0=qm[s][:], in1=qs_[s][:],
                                               op=OP.subtract),
                  r=[(pfx, "qm", s), (pfx, "qs_", s)], w=[(pfx, "qtm", s)])
            em.op(V, lambda e: e.tensor_tensor(out=qm[s][:], in0=qm[s][:], in1=qtm[s][:],
                                               op=OP.subtract),
                  r=[(pfx, "qm", s), (pfx, "qtm", s)], w=[(pfx, "qm", s)])
            em.op(V, lambda e: e.scalar_tensor_tensor(out=qm[s][:], in0=qs_[s][:],
                                                      scalar=dcol[:, 0:1], in1=qm[s][:],
                                                      op0=OP.mult, op1=OP.add),
                  r=[(pfx, "qs_", s), (pfx, "qm", s), ("dcol",)], w=[(pfx, "qm", s)])
            em.op(V, lambda e: e.tensor_scalar(out=qm[s][:], in0=qm[s][:], scalar1=1e-8,
                                               scalar2=None, op0=OP.max),
                  r=[(pfx, "qm", s)], w=[(pfx, "qm", s)])
            em.op(V, lambda e: e.scalar_tensor_tensor(out=qtm[s][:], in0=qm[s][:], scalar=V45C,
                                                      in1=qm[s][:], op0=OP.mult, op1=OP.add),
                  r=[(pfx, "qm", s)], w=[(pfx, "qtm", s)])
            em.op(V, lambda e: e.tensor_tensor(out=qt_[s][:], in0=qtm[s][:], in1=qm[s][:],
                                               op=OP.subtract),
                  r=[(pfx, "qtm", s), (pfx, "qm", s)], w=[(pfx, "qt_", s)])
            em.op(V, lambda e: e.tensor_tensor(out=qt_[s][:], in0=qtm[s][:], in1=qt_[s][:],
                                               op=OP.subtract),
                  r=[(pfx, "qtm", s), (pfx, "qt_", s)], w=[(pfx, "qt_", s)])
            em.op(V, lambda e: e.scalar_tensor_tensor(out=qsc[s][:], in0=qm[s][:],
                                                      scalar=dcol[:, 0:1], in1=qt_[s][:],
                                                      op0=OP.mult, op1=OP.add),
                  r=[(pfx, "qm", s), (pfx, "qt_", s), ("dcol",)], w=[(pfx, "qsc", s)], tag="q_sc")
            em.op(V, lambda e: e.tensor_scalar(out=qt_[s][:], in0=qsc[s][:], scalar1=C_T,
                                               scalar2=None, op0=OP.mult),
                  r=[(pfx, "qsc", s)], w=[(pfx, "qt_", s)])
            em.op(V, lambda e: e.tensor_scalar(out=qnt[s][:], in0=qt_[s][:], scalar1=-1.0,
                                               scalar2=None, op0=OP.mult),
                  r=[(pfx, "qt_", s)], w=[(pfx, "qnt", s)])
            em.op(V, lambda e: e.tensor_tensor(
                out=qg1[s][:].rearrange("p (g k) -> p g k", k=GRP),
                in0=qwb[s][:].rearrange("p (g k) -> p g k", k=GRP),
                in1=bcg(qt_[s]), op=OP.is_gt),
                r=[(pfx, "qwb", s), (pfx, "qt_", s)], w=[(pfx, "qg1", s)], tag="q_g1")
            em.op(V, lambda e: e.tensor_tensor(
                out=qg2[s][:].rearrange("p (g k) -> p g k", k=GRP),
                in0=qwb[s][:].rearrange("p (g k) -> p g k", k=GRP),
                in1=bcg(qnt[s]), op=OP.is_lt),
                r=[(pfx, "qwb", s), (pfx, "qnt", s)], w=[(pfx, "qout", s)], tag="q_g2")
            em.op(V, lambda e: e.tensor_tensor(out=qg1[s][:], in0=qg1[s][:], in1=qg2[s][:],
                                               op=OP.subtract),
                  r=[(pfx, "qg1", s), (pfx, "qout", s)], w=[(pfx, "qg1", s)])
            em.op(V, lambda e: e.tensor_tensor(
                out=qout[s][:].rearrange("p (g k) -> p g k", k=GRP),
                in0=qg1[s][:].rearrange("p (g k) -> p g k", k=GRP),
                in1=bcg(qsc[s]), op=OP.mult),
                r=[(pfx, "qg1", s), (pfx, "qsc", s)], w=[(pfx, "qout", s)], tag="q_qs")
            em.op(SY, lambda e: e.dma_start(out=dst[r0:r0 + 128, c0:c0 + QW],
                                            in_=qout[s][:]),
                  r=[(pfx, "qout", s)], w=[dkey], dma=True, tag="q_st")

        chunks = []
        if "fc" in which:
            for r0 in range(0, I, 128):
                for c0 in range(0, H, QW):
                    chunks.append((w_in["w_fc"], ("win", "w_fc"), qfc, ("qfc",), r0, c0))
        if "pr" in which:
            for r0 in range(0, H, 128):
                for c0 in range(0, I, QW):
                    chunks.append((w_in["w_proj"], ("win", "w_proj"), qpr, ("qpr",), r0, c0))
        state = {"i": 0}

        def pump(n):
            while n > 0 and state["i"] < len(chunks):
                chunk(*chunks[state["i"]], qwi=state["i"] % 2)
                state["i"] += 1
                n -= 1
        return pump, len(chunks)

    def l0_attn():
        with contextlib.ExitStack() as pes:
            psb = lambda n, sh, dt=BF16: pes.enter_context(nc.sbuf_tensor(n, sh, dt))
            xk = [psb(f"xk{c}", [128, TC]) for c in range(HC)]
            xv = [psb(f"xv{c}", [128, TC]) for c in range(HC)]
            xr = [psb(f"xr{c}", [128, TC]) for c in range(HC)]
            rp = [psb(f"rp{c}", [128, TC]) for c in range(HC)]
            wkv = [psb(f"wkv{c}", [128, TC]) for c in range(HC)]
            rwk = xk
            ybf = xr
            ek4 = [pes.enter_context(nc.sbuf_tensor(f"ek4_{m}", [128, TC], F32))
                   for m in range(2)]
            vv4 = [pes.enter_context(nc.sbuf_tensor(f"vv4_{m}", [128, TC], F32))
                   for m in range(2)]
            xl1 = pes.enter_context(nc.sbuf_tensor("xl1", [128, TC + 1], BF16))
            cv[0] = pes.enter_context(nc.sbuf_tensor("cvp0", [128, 512], BF16))
            cv[1] = pes.enter_context(nc.sbuf_tensor("cvp1", [128, 512], BF16))
            preconv_all()
            scN = pes.enter_context(nc.sbuf_tensor("scN", [128, TC + 1], F32))
            scD = pes.enter_context(nc.sbuf_tensor("scD", [128, TC + 1], F32))
            tC[0] = pes.enter_context(nc.sbuf_tensor("tCa", [128, TC], F32))
            tD_ = tC[0]
            em.op(S, lambda e: preload_set(e, "natural_log_exp_and_others"), tag="tbl")
            for tj in range(TN):
                tsl = slice(tj * TC, (tj + 1) * TC)
                src, skeys = res_src(tj)
                ln_rows(src, skeys, rp, [("rp", c) for c in range(HC)], H, TC,
                        C_ELN, fused=True, rr0=True)
                for c in range(HC):
                    em.op(V, lambda e, c=c, tsl=tsl: e.tensor_tensor(
                        out=tA[:], in0=res[c][:, tsl], in1=mu_b[:], op=OP.subtract),
                        r=[("res", c), ("mu_b",)], w=[("tA",)])
                    em.op(V, lambda e: e.tensor_tensor(out=tB[:], in0=tA[:], in1=rr_b[:],
                                                       op=OP.mult),
                          r=[("tA",), ("rr_b",)], w=[("tB",)])
                    em.op(V, lambda e, c=c: e.tensor_copy(out=xl1[:, 0:1],
                                                          in_=xprev[:, c:c + 1]),
                          r=[("xprev",)], w=[("xl1",)])
                    em.op(V, lambda e, c=c: e.tensor_scalar(
                        out=xl1[:, 1:TC + 1], in0=tB[:], scalar1=pc["ln_w"][:, c:c + 1],
                        scalar2=pc["ln_b"][:, c:c + 1], op0=OP.mult, op1=OP.add),
                        r=[("tB",), ("pc", "ln_w"), ("pc", "ln_b")], w=[("xl1",)], tag="xl1")
                    em.op(V, lambda e, c=c: e.tensor_copy(out=xprev[:, c:c + 1],
                                                          in_=xl1[:, TC:TC + 1]),
                          r=[("xl1",)], w=[("xprev",)])
                    em.op(V, lambda e: e.tensor_tensor(out=tC[0][:], in0=xl1[:, 1:TC + 1],
                                                       in1=xl1[:, 0:TC], op=OP.subtract),
                          r=[("xl1",)], w=[("tC",)])
                    for mix, dst, dk in (("mix_k", xk, "xk"), ("mix_v", xv, "xv"),
                                         ("mix_r", xr, "xr")):
                        em.op(V, lambda e, c=c, mix=mix, dst=dst: e.scalar_tensor_tensor(
                            out=dst[c][:], in0=tC[0][:], scalar=pc[mix][:, c:c + 1],
                            in1=xl1[:, 0:TC], op0=OP.mult, op1=OP.add),
                            r=[("tC",), ("xl1",), ("pc", mix)], w=[(dk, c)], tag="xi")
                nm = min(2, HC)
                for cq in range(max(1, HC // nm)):
                    for hc in range(HC):
                        b = hc % 4
                        load_piece(b, w_in["wkT"], ("win", "wkT"), hc * 128, cq * nm * 128,
                                   ncols=nm * 128)
                        for m in range(nm):
                            em.op(TE, lambda e, hc=hc, m=m, b=b, st=(hc == 0): e.matmul(
                                psum[m][:, 0:TC], wp[b][:, m * 128:(m + 1) * 128],
                                xk[hc][:, 0:TC], start=st, stop=(hc == HC - 1)),
                                r=[("wp", b), ("xk", hc)], w=[("psum", m)], tag="mm_k")
                    for m in range(nm):
                        em.op(S, lambda e, m=m: e.activation(out=ek4[m][:], in_=psum[m][:, 0:TC],
                                                             func=AF.Exp),
                              r=[("psum", m)], w=[("ek4", m)], tag="ek")
                    for hc in range(HC):
                        b = 4 + hc % 4
                        load_piece(b, w_in["wvT"], ("win", "wvT"), hc * 128, cq * nm * 128,
                                   ncols=nm * 128)
                        for m in range(nm):
                            em.op(TE, lambda e, hc=hc, m=m, b=b, st=(hc == 0): e.matmul(
                                psum[m][:, 0:TC], wp[b][:, m * 128:(m + 1) * 128],
                                xv[hc][:, 0:TC], start=st, stop=(hc == HC - 1)),
                                r=[("wp", b), ("xv", hc)], w=[("psum", m)], tag="mm_v")
                    for m in range(nm):
                        em.op(S, lambda e, m=m: e.copy(out=vv4[m][:], in_=psum[m][:, 0:TC]),
                              r=[("psum", m)], w=[("vv4", m)], tag="v_ev")
                    for hc in range(HC):
                        b = hc % 4
                        load_piece(b, w_in["wrT"], ("win", "wrT"), hc * 128, cq * nm * 128,
                                   ncols=nm * 128)
                        for m in range(nm):
                            em.op(TE, lambda e, hc=hc, m=m, b=b, st=(hc == 0): e.matmul(
                                psum[m][:, 0:TC], wp[b][:, m * 128:(m + 1) * 128],
                                xr[hc][:, 0:TC], start=st, stop=(hc == HC - 1)),
                                r=[("wp", b), ("xr", hc)], w=[("psum", m)], tag="mm_r")
                    for m in range(nm):
                        co = cq * nm + m
                        em.op(S, lambda e, m=m, co=co: e.copy(out=rp[co][:],
                                                              in_=psum[m][:, 0:TC]),
                              r=[("psum", m)], w=[("rp", co)], tag="rp_ev")
                    for m in range(nm):
                        co = cq * nm + m
                        em.op(V, lambda e, m=m: e.tensor_tensor(out=tD_[:], in0=ek4[m][:],
                                                                in1=vv4[m][:], op=OP.mult),
                              r=[("ek4", m), ("vv4", m)], w=[("tC",)])
                        em.op(V, lambda e, co=co: e.tensor_copy(out=scN[:, 0:1],
                                                                in_=stN[:, co:co + 1]),
                              r=[("stN",)], w=[("scN",)])
                        em.op(V, lambda e, co=co: e.tensor_tensor_scan(
                            out=scN[:, 1:TC + 1],
                            data0=bass.AP(pc["wdec"][:].tensor,
                                          pc["wdec"][:].offset + co,
                                          [pc["wdec"][:].ap[0], [0, TC]]),
                            data1=tD_[:],
                            initial=stN[:, co:co + 1], op0=OP.mult, op1=OP.add),
                            r=[("pc", "wdec"), ("tC",), ("stN",)], w=[("scN",)], tag="scanN")
                        em.op(V, lambda e, co=co: e.tensor_copy(out=stN[:, co:co + 1],
                                                                in_=scN[:, TC:TC + 1]),
                              r=[("scN",)], w=[("stN",)])
                        em.op(V, lambda e, co=co: e.tensor_copy(out=scD[:, 0:1],
                                                                in_=stD[:, co:co + 1]),
                              r=[("stD",)], w=[("scD",)])
                        em.op(V, lambda e, co=co, m=m: e.tensor_tensor_scan(
                            out=scD[:, 1:TC + 1],
                            data0=bass.AP(pc["wdec"][:].tensor,
                                          pc["wdec"][:].offset + co,
                                          [pc["wdec"][:].ap[0], [0, TC]]),
                            data1=ek4[m][:],
                            initial=stD[:, co:co + 1], op0=OP.mult, op1=OP.add),
                            r=[("pc", "wdec"), ("ek4", m), ("stD",)], w=[("scD",)], tag="scanD")
                        em.op(V, lambda e, co=co: e.tensor_copy(out=stD[:, co:co + 1],
                                                                in_=scD[:, TC:TC + 1]),
                              r=[("scD",)], w=[("stD",)])
                        em.op(V, lambda e, co=co: e.tensor_scalar(
                            out=tA[:], in0=tD_[:], scalar1=pc["ecw"][:, co:co + 1],
                            scalar2=None, op0=OP.mult),
                            r=[("tC",), ("pc", "ecw")], w=[("tA",)], tag="ecurv")
                        em.op(V, lambda e: e.tensor_tensor(out=tA[:], in0=tA[:],
                                                           in1=scN[:, 0:TC], op=OP.add),
                              r=[("tA",), ("scN",)], w=[("tA",)])
                        em.op(V, lambda e, m=m, co=co: e.tensor_scalar(
                            out=tB[:], in0=ek4[m][:], scalar1=pc["ecw"][:, co:co + 1],
                            scalar2=None, op0=OP.mult),
                            r=[("ek4", m), ("pc", "ecw")], w=[("tB",)], tag="ecur")
                        em.op(V, lambda e: e.tensor_tensor(out=tB[:], in0=tB[:],
                                                           in1=scD[:, 0:TC], op=OP.add),
                              r=[("tB",), ("scD",)], w=[("tB",)])
                        em.op(V, lambda e: e.reciprocal(out=tC[0][:], in_=tB[:]),
                              r=[("tB",)], w=[("tC",)], tag="recip")
                        em.op(V, lambda e, co=co: e.tensor_tensor(out=wkv[co][:], in0=tA[:],
                                                                  in1=tC[0][:], op=OP.mult),
                              r=[("tA",), ("tC",)], w=[("wkv", co)], tag="wkv")
                em.op(S, lambda e: preload_set(e, "exp_and_others"), tag="tbl_t")
                for co in range(HC):
                    em.op(S, lambda e, co=co: e.activation(out=rp[co][:], in_=rp[co][:],
                                                           func=AF.Tanh,
                                                           scale=cst[:, C_HALF:C_HALF + 1]),
                          r=[("rp", co), ("cst",)], w=[("rp", co)], tag="tanh")
                    em.op(V, lambda e, co=co: e.tensor_scalar(out=tA[:], in0=rp[co][:],
                                                              scalar1=0.5, scalar2=0.5,
                                                              op0=OP.mult, op1=OP.add),
                          r=[("rp", co)], w=[("tA",)])
                    em.op(V, lambda e, co=co: e.tensor_tensor(out=rwk[co][:], in0=tA[:],
                                                              in1=wkv[co][:], op=OP.mult),
                          r=[("tA",), ("wkv", co)], w=[("xk", co)], tag="rwkv")
                em.op(S, lambda e: preload_set(e, "natural_log_exp_and_others"), tag="tbl_b")
                for cq in range(max(1, HC // 4)):
                    nm = min(4, HC)
                    for hc in range(HC):
                        b = 4 + hc % 4
                        load_piece(b, w_in["woT"], ("win", "woT"), hc * 128, cq * nm * 128,
                                   ncols=nm * 128)
                        for m in range(nm):
                            em.op(TE, lambda e, hc=hc, m=m, b=b, st=(hc == 0): e.matmul(
                                psum[m][:, 0:TC], wp[b][:, m * 128:(m + 1) * 128],
                                rwk[hc][:, 0:TC], start=st, stop=(hc == HC - 1)),
                                r=[("wp", b), ("xk", hc)], w=[("psum", m)], tag="mm_wo")
                    for m in range(nm):
                        co = cq * nm + m
                        em.op(V, lambda e, co=co, tsl=tsl: e.tensor_tensor(
                            out=tA[:], in0=res[co][:, tsl], in1=mu_b[:], op=OP.subtract),
                            r=[("res", co), ("mu_b",)], w=[("tA",)], tag="ln0a")
                        em.op(V, lambda e: e.tensor_tensor(out=tB[:], in0=tA[:],
                                                           in1=rc_b[:], op=OP.mult),
                              r=[("tA",), ("rc_b",)], w=[("tB",)], tag="ln0b")
                        em.op(V, lambda e, m=m, co=co: e.tensor_tensor(
                            out=ybf[co][:], in0=tB[:], in1=psum[m][:, 0:TC], op=OP.add),
                            r=[("tB",), ("psum", m)], w=[("xr", co)], tag="y0")
                ln_out_add(ybf, [("xr", c) for c in range(HC)], rp,
                           [("rp", c) for c in range(HC)], tj, 0, TC)
        em.barrier("l0a")

    def l0_ffn():
        with contextlib.ExitStack() as pes:
            psb = lambda n, sh, dt=BF16: pes.enter_context(nc.sbuf_tensor(n, sh, dt))
            xf = [psb(f"xf{c}", [128, TC]) for c in range(HC)]
            hsp = [psb(f"hsp{i}", [128, TC]) for i in range(IC)]
            sg4 = [psb(f"sg{m}", [128, TC]) for m in range(2)]
            ybf = xf
            tD_ = pes.enter_context(nc.sbuf_tensor("tDf", [128, TC], F32))
            tCf = pes.enter_context(nc.sbuf_tensor("tCf", [128, TC], F32))
            tE = pes.enter_context(nc.sbuf_tensor("tE", [128, TC], F32))
            tF = pes.enter_context(nc.sbuf_tensor("tF", [128, TC], F32))
            tG = pes.enter_context(nc.sbuf_tensor("tG", [128, TC], F32))
            tH = pes.enter_context(nc.sbuf_tensor("tH", [128, TC], F32))
            scN = pes.enter_context(nc.sbuf_tensor("scN2", [128, TC], F32))
            scD = pes.enter_context(nc.sbuf_tensor("scD2", [128, TC], F32))
            wb_t = pes.enter_context(nc.sbuf_tensor("wb_t2", [128, TC], F32))
            t2q = pes.enter_context(nc.sbuf_tensor("t2q", [128, TC], F32))
            for tj in range(TN):
                tsl = slice(tj * TC, (tj + 1) * TC)
                src, skeys = res_src(tj)
                ln_rows(src, skeys, hsp, [("hsp", c) for c in range(HC)], H, TC, C_ELN)
                for c in range(HC):
                    em.op(V, lambda e, c=c, tsl=tsl: e.tensor_tensor(
                        out=tA[:], in0=res[c][:, tsl], in1=mu_b[:], op=OP.subtract),
                        r=[("res", c), ("mu_b",)], w=[("tA",)])
                    em.op(V, lambda e, c=c: e.tensor_tensor(out=xf[c][:], in0=tA[:],
                                                            in1=rr_b[:], op=OP.mult),
                          r=[("tA",), ("rr_b",)], w=[("xf", c)], tag="xffn")
                em.op(S, lambda e: preload_set(e, "silu_and_others"), tag="tbl_s")
                for icq in range(IC // 4):
                    for hc in range(HC):
                        b = hc % 4
                        load_piece(b, w_in["w_gate"], ("win", "wg"), hc * 128, icq * 512)
                        for m in range(4):
                            em.op(TE, lambda e, hc=hc, m=m, b=b, st=(hc == 0): e.matmul(
                                psum[m][:, 0:TC], wp[b][:, m * 128:(m + 1) * 128],
                                xf[hc][:, 0:TC], start=st, stop=(hc == HC - 1)),
                                r=[("wp", b), ("xf", hc)], w=[("psum", m)], tag="mm_g")
                    for m in (0, 1):
                        em.op(S, lambda e, m=m: e.activation(out=sg4[m][:], in_=psum[m][:, 0:TC],
                                                             func=AF.Silu),
                              r=[("psum", m)], w=[("sg", m)], tag="silu")
                    ub = [4, 5, 0, 1]
                    for hc in range(HC):
                        b = 4 + hc % 4
                        load_piece(b, w_in["w_up"], ("win", "wu"), hc * 128, icq * 512)
                        for m in range(4):
                            em.op(TE, lambda e, hc=hc, m=m, b=b, st=(hc == 0): e.matmul(
                                psum[ub[m]][:, 0:TC], wp[b][:, m * 128:(m + 1) * 128],
                                xf[hc][:, 0:TC], start=st, stop=(hc == HC - 1)),
                                r=[("wp", b), ("xf", hc)], w=[("psum", ub[m])], tag="mm_u")
                    hts = (tA, tB, tCf, tD_)
                    ats = (tE, tF, tG, tH)
                    tkeys = [("tA",), ("tB",), ("tCf",), ("tD_",)]
                    akeys = [("tE",), ("tF",), ("tG",), ("tH",)]
                    for m in range(4):
                        if m == 2:
                            em.op(S, lambda e: e.activation(out=sg4[0][:], in_=psum[2][:, 0:TC],
                                                            func=AF.Silu),
                                  r=[("psum", 2)], w=[("sg", 0)], tag="silu")
                            em.op(S, lambda e: e.activation(out=sg4[1][:], in_=psum[3][:, 0:TC],
                                                            func=AF.Silu),
                                  r=[("psum", 3)], w=[("sg", 1)], tag="silu")
                        em.op(V, lambda e, m=m: e.tensor_tensor(out=hts[m][:], in0=sg4[m % 2][:],
                                                                in1=psum[ub[m]][:, 0:TC],
                                                                op=OP.mult),
                              r=[("sg", m % 2), ("psum", ub[m])], w=[tkeys[m]], tag="h")
                        em.op(S, lambda e, m=m: e.activation(out=ats[m][:], in_=hts[m][:],
                                                             func=AF.Abs),
                              r=[tkeys[m]], w=[akeys[m]], tag="abs")
                    em.op(V, lambda e: e.tensor_tensor(out=wb_t[:], in0=ats[0][:],
                                                       in1=ats[1][:], op=OP.max),
                          r=[akeys[0], akeys[1]], w=[("wb_t",)])
                    em.op(V, lambda e: e.tensor_tensor(out=scN[:], in0=ats[2][:],
                                                       in1=ats[3][:], op=OP.max),
                          r=[akeys[2], akeys[3]], w=[("scN",)])
                    em.op(V, lambda e: e.tensor_tensor(out=t2q[:], in0=wb_t[:],
                                                       in1=scN[:], op=OP.min),
                          r=[("wb_t",), ("scN",)], w=[("t2q",)])
                    em.op(V, lambda e: e.tensor_tensor(out=wb_t[:], in0=ats[0][:],
                                                       in1=ats[1][:], op=OP.min),
                          r=[akeys[0], akeys[1]], w=[("wb_t",)])
                    em.op(V, lambda e: e.tensor_tensor(out=scN[:], in0=ats[2][:],
                                                       in1=ats[3][:], op=OP.min),
                          r=[akeys[2], akeys[3]], w=[("scN",)])
                    em.op(V, lambda e: e.tensor_tensor(out=scD[:], in0=wb_t[:],
                                                       in1=scN[:], op=OP.max),
                          r=[("wb_t",), ("scN",)], w=[("scD",)])
                    em.op(V, lambda e: e.tensor_tensor(out=wb_t[:], in0=t2q[:],
                                                       in1=scD[:], op=OP.max),
                          r=[("t2q",), ("scD",)], w=[("wb_t",)], tag="t2")
                    for m in range(4):
                        it = icq * 4 + m
                        em.op(V, lambda e, m=m: e.tensor_tensor(out=ats[m][:], in0=ats[m][:],
                                                                in1=wb_t[:], op=OP.is_ge),
                              r=[akeys[m], ("wb_t",)], w=[akeys[m]], tag="mask")
                        em.op(V, lambda e, m=m, it=it: e.tensor_tensor(
                            out=hsp[it][:], in0=ats[m][:], in1=hts[m][:], op=OP.mult),
                            r=[akeys[m], tkeys[m]], w=[("hsp", it)], tag="hsp")
                em.op(S, lambda e: preload_set(e, "natural_log_exp_and_others"), tag="tbl_b2")
                for oc in range(HC):
                    for icq in range(IC // 4):
                        b = icq % 4
                        a = wbf["w_down"][icq * 512:(icq + 1) * 512,
                                          oc * 128:(oc + 1) * 128].rearrange(
                            "(kc p) n -> p kc n", p=128)
                        rkeys = [("wbf", "wd", icq * 4 + d, (oc * 128) // 1024)
                                 for d in range(4)]
                        em.op(SY, lambda e, a=a, b=b: e.dma_start(
                            out=wp[b][:].rearrange("p (kc n) -> p kc n", n=128), in_=a),
                            r=rkeys, w=[("wp", b)], dma=True, tag="wd_ld")
                        for k4 in range(4):
                            ic = icq * 4 + k4
                            em.op(TE, lambda e, b=b, k4=k4, ic=ic, st=(ic == 0): e.matmul(
                                psum[2][:, 0:TC], wp[b][:, k4 * 128:(k4 + 1) * 128],
                                hsp[ic][:, 0:TC], start=st, stop=(ic == IC - 1)),
                                r=[("wp", b), ("hsp", ic)], w=[("psum", 2)], tag="mm_d")
                    em.op(S, lambda e, oc=oc: e.copy(out=ybf[oc][:], in_=psum[2][:, 0:TC]),
                          r=[("psum", 2)], w=[("xf", oc)], tag="yffn")
                ln_out_add(ybf, [("xf", c) for c in range(HC)], hsp,
                           [("hsp", c) for c in range(HC)], tj, 0, TC)
        em.barrier("l0f")

    def l1_attn():
        with contextlib.ExitStack() as pes:
            psb = lambda n, sh, dt=BF16: pes.enter_context(nc.sbuf_tensor(n, sh, dt))
            stb = [psb(f"stb{r}", [128, T]) for r in range(RC)]
            acb = [psb(f"acb{r}", [128, T]) for r in range(RC)]
            xtok = [psb(f"xt{j}", [128, H]) for j in range(JB)]
            em.op(S, lambda e: preload_set(e, "natural_log_exp_and_others"), tag="tblA")
            aes = contextlib.ExitStack()
            xf = [aes.enter_context(nc.sbuf_tensor(f"xa{c}", [128, TC], BF16))
                  for c in range(HC)]
            for tj in range(TN):
                tsl = slice(tj * TC, (tj + 1) * TC)
                src, skeys = res_src(tj)
                ln_rows(src, skeys, xf, [("xfa", c) for c in range(HC)], H, TC, C_ELN)
                for c in range(HC):
                    em.op(V, lambda e, c=c, tsl=tsl: e.tensor_tensor(
                        out=tA[:], in0=res[c][:, tsl], in1=mu_b[:], op=OP.subtract),
                        r=[("res", c), ("mu_b",)], w=[("tA",)])
                    em.op(V, lambda e, c=c: e.tensor_tensor(out=xf[c][:], in0=tA[:],
                                                            in1=rr_b[:], op=OP.mult),
                          r=[("tA",), ("rr_b",)], w=[("xfa", c)], tag="xl3")
                for rc in range(RC):
                    gsl = slice(tj * TC, (tj + 1) * TC)
                    for hc in range(HC):
                        b = hc % 4
                        load_piece(b, w_in["ws"], ("win", "ws"), hc * 128, rc * 128,
                                   ncols=128)
                        em.op(TE, lambda e, hc=hc, b=b, st=(hc == 0): e.matmul(
                            psum[0][:, 0:TC], wp[b][:, 0:128],
                            xf[hc][:, 0:TC], start=st, stop=(hc == HC - 1)),
                            r=[("wp", b), ("xfa", hc)], w=[("psum", 0)], tag="mm_s")
                    em.op(S, lambda e, gsl=gsl, rc=rc: e.copy(out=stb[rc][:, gsl],
                                                              in_=psum[0][:, 0:TC]),
                          r=[("psum", 0)], w=[("stb", rc)])
                    for hc in range(HC):
                        b = 4 + hc % 4
                        load_piece(b, w_in["wa"], ("win", "wa"), hc * 128, rc * 128,
                                   ncols=128)
                        em.op(TE, lambda e, hc=hc, b=b, st=(hc == 0): e.matmul(
                            psum[1][:, 0:TC], wp[b][:, 0:128],
                            xf[hc][:, 0:TC], start=st, stop=(hc == HC - 1)),
                            r=[("wp", b), ("xfa", hc)], w=[("psum", 1)], tag="mm_a")
                    em.op(S, lambda e, gsl=gsl, rc=rc: e.copy(out=acb[rc][:, gsl],
                                                              in_=psum[1][:, 0:TC]),
                          r=[("psum", 1)], w=[("acb", rc)])
                for c in range(HC):
                    for q in range(TC // 128):
                        jb = (tj * TC) // 128 + q
                        em.op(SY, lambda e, c=c, q=q, jb=jb: e.dma_start(
                            out=xtok[jb][:, c * 128:(c + 1) * 128],
                            in_=xf[c][:, q * 128:(q + 1) * 128], transpose=True),
                            r=[("xfa", c)], w=[("xtok", jb)], dma=True, tag="xtok")
            L1B = cfg.get("l1b_upto", 5)
            aes.close()
            em.barrier("l1ab")
            Ep = [psb(f"Ep{j}", [128, TCB]) for j in range(JB)]
            anb = [psb(f"anb{c}", [128, TCB]) for c in range(HC)]
            ybf = [psb(f"yba{c}", [128, TCB]) for c in range(HC)]
            xsq = Ep
            for iq in range(0 if cfg.get("l1a_only_A") else JB):
                isl = slice(iq * 128, (iq + 1) * 128)
                for jb in range(iq + 1):
                    for rc in range(RC):
                        em.op(TE, lambda e, jb=jb, rc=rc, isl=isl, st=(rc == 0): e.matmul(
                            psum[0][:, 0:TCB], acb[rc][:, jb * 128:(jb + 1) * 128],
                            stb[rc][:, isl], start=st, stop=(rc == RC - 1)),
                            r=[("acb", rc), ("stb", rc)], w=[("psum", 0)], tag="mm_sc")
                    if jb == iq:
                        em.op(V, lambda e: e.tensor_tensor(out=tA[:, 0:TCB],
                                                           in0=psum[0][:, 0:TCB],
                                                           in1=amask[:], op=OP.add),
                              r=[("psum", 0), ("amask",)], w=[("tA",)], tag="mask")
                        em.op(S, lambda e, jb=jb: e.activation(out=Ep[jb][:], in_=tA[:, 0:TCB],
                                                               func=AF.Exp,
                                                               scale=cst[:, C_I16:C_I16 + 1]),
                              r=[("tA",), ("cst",)], w=[("Ep", jb)], tag="expS")
                    else:
                        em.op(S, lambda e, jb=jb: e.activation(out=Ep[jb][:],
                                                               in_=psum[0][:, 0:TCB],
                                                               func=AF.Exp,
                                                               scale=cst[:, C_I16:C_I16 + 1]),
                              r=[("psum", 0), ("cst",)], w=[("Ep", jb)], tag="expS")
                if L1B < 2:
                    continue
                for jb in range(iq + 1):
                    em.op(TE, lambda e, jb=jb, iq=iq, st=(jb == 0): e.matmul(
                        prow[0:1, 0:TCB], ones_bf[:], Ep[jb][:],
                        start=st, stop=(jb == iq)),
                        r=[("ones_bf",), ("Ep", jb)], w=[("prow",)], tag="rs")
                em.op(V, lambda e: e.tensor_copy(out=rw["s1"][:, 0:TCB], in_=prow[0:1, 0:TCB]),
                      r=[("prow",)], w=[("rw", "s1")])
                em.op(V, lambda e: e.reciprocal(out=rw["tmp"][:, 0:TCB],
                                                in_=rw["s1"][:, 0:TCB]),
                      r=[("rw", "s1")], w=[("rw", "tmp")], tag="rcp")
                bcast_row(rw["tmp"][:], ("rw", "tmp"), TCB, rc_b, ("rc_b",), psum[4], 4)
                if L1B < 3:
                    continue
                for hc in range(HC):
                    for jb in range(iq + 1):
                        em.op(TE, lambda e, jb=jb, hc=hc, iq=iq, st=(jb == 0): e.matmul(
                            psum[1][:, 0:TCB], xtok[jb][:, hc * 128:(hc + 1) * 128],
                            Ep[jb][:], start=st, stop=(jb == iq)),
                            r=[("xtok", jb), ("Ep", jb)], w=[("psum", 1)], tag="mm_av")
                    em.op(V, lambda e, hc=hc: e.tensor_tensor(out=anb[hc][:],
                                                              in0=psum[1][:, 0:TCB],
                                                              in1=rc_b[:, 0:TCB], op=OP.mult),
                          r=[("psum", 1), ("rc_b",)], w=[("anb", hc)], tag="attn")
                if L1B < 4:
                    continue
                nm = min(2, HC)
                for cq in range(max(1, HC // nm)):
                    for hc in range(HC):
                        b = 4 + hc % 4
                        load_piece(b, w_in["woT1"], ("win", "woT1"), hc * 128,
                                   cq * nm * 128, ncols=nm * 128)
                        for m in range(nm):
                            em.op(TE, lambda e, hc=hc, m=m, b=b, st=(hc == 0): e.matmul(
                                psum[m][:, 0:TCB], wp[b][:, m * 128:(m + 1) * 128],
                                anb[hc][:], start=st, stop=(hc == HC - 1)),
                                r=[("wp", b), ("anb", hc)], w=[("psum", m)], tag="mm_o1")
                    for m in range(nm):
                        oc = cq * nm + m
                        em.op(V, lambda e, m=m, oc=oc: e.tensor_scalar(
                            out=ybf[oc][:], in0=psum[m][:, 0:TCB],
                            scalar1=pc["bo"][:, oc:oc + 1], scalar2=None, op0=OP.add),
                            r=[("psum", m), ("pc", "bo")], w=[("yba", oc)], tag="y_att")
                if L1B < 5:
                    continue
                tj, off = (iq * 128) // TC, (iq * 128) % TC
                ln_out_add(ybf, [("yba", c) for c in range(HC)], xsq,
                           [("Ep", c) for c in range(HC)], tj, off, TCB)
        em.barrier("l1a")

    def l1_ffn():
        with contextlib.ExitStack() as pes:
            psb = lambda n, sh, dt=BF16: pes.enter_context(nc.sbuf_tensor(n, sh, dt))
            xf = [psb(f"x5{c}", [128, TC]) for c in range(HC)]
            fbufs = [pes.enter_context(nc.sbuf_tensor(f"fb{i}", [128, TC], F32))
                     for i in range(2)]
            hbf = [psb(f"hb{i}", [128, TC]) for i in range(IC)]
            ybf = xf
            xsq = [psb(f"q5{c}", [128, TC]) for c in range(HC)]
            for tj in range(TN):
                tsl = slice(tj * TC, (tj + 1) * TC)
                src, skeys = res_src(tj)
                ln_rows(src, skeys, xsq, [("xsq5", c) for c in range(HC)], H, TC, C_ELN)
                for c in range(HC):
                    em.op(V, lambda e, c=c, tsl=tsl: e.tensor_tensor(
                        out=tA[:], in0=res[c][:, tsl], in1=mu_b[:], op=OP.subtract),
                        r=[("res", c), ("mu_b",)], w=[("tA",)])
                    em.op(V, lambda e, c=c: e.tensor_tensor(out=xf[c][:], in0=tA[:],
                                                            in1=rr_b[:], op=OP.mult),
                          r=[("tA",), ("rr_b",)], w=[("x5", c)], tag="x5")
                for icq in range(IC // 4):
                    for hc in range(HC):
                        b = hc % 4
                        em.op(SY, lambda e, icq=icq, hc=hc, b=b: e.dma_start(
                            out=wp[b][:], in_=qfc[icq * 512:(icq + 1) * 512,
                                                 hc * 128:(hc + 1) * 128], transpose=True),
                            r=[("qfc",)], w=[("wp", b)], dma=True, tag="qfc_tr")
                        for m in range(4):
                            em.op(TE, lambda e, hc=hc, m=m, b=b, st=(hc == 0): e.matmul(
                                psum[m][:, 0:TC], wp[b][:, m * 128:(m + 1) * 128],
                                xf[hc][:, 0:TC], start=st, stop=(hc == HC - 1)),
                                r=[("wp", b), ("x5", hc)], w=[("psum", m)], tag="mm_fc")
                    for m in range(4):
                        ic = icq * 4 + m
                        em.op(S, lambda e, m=m: e.activation(out=tA[:], in_=psum[m][:, 0:TC],
                                                             func=AF.Relu),
                              r=[("psum", m)], w=[("tA",)], tag="relu")
                        em.op(V, lambda e, ic=ic: e.tensor_tensor(out=hbf[ic][:], in0=tA[:],
                                                                  in1=tA[:], op=OP.mult),
                              r=[("tA",)], w=[("hb", ic)], tag="hsq")
                for t0 in range(0, TC, 512):
                    n = min(512, TC - t0)
                    for ic in range(IC):
                        em.op(S, lambda e, ic=ic, t0=t0, n=n: e.activation(
                            out=xsq[ic % HC][:, t0:t0 + n], in_=hbf[ic][:, t0:t0 + n],
                            func=AF.Square),
                            r=[("hb", ic)], w=[("xsq5", ic % HC)], tag="h2")
                        em.op(TE, lambda e, ic=ic, t0=t0, n=n, st=(ic == 0): e.matmul(
                            prow[0:1, 0:n], ones_bf[:], xsq[ic % HC][:, t0:t0 + n],
                            start=st, stop=(ic == IC - 1)),
                            r=[("ones_bf",), ("xsq5", ic % HC)], w=[("prow",)], tag="rsum")
                    em.op(V, lambda e, t0=t0, n=n: e.tensor_copy(out=rw["s1"][:, t0:t0 + n],
                                                                 in_=prow[0:1, 0:n]),
                          r=[("prow",)], w=[("rw", "s1")])
                em.op(V, lambda e: e.tensor_scalar(out=rw["var"][:], in0=rw["s1"][:],
                                                   scalar1=1.0 / I, scalar2=None,
                                                   op0=OP.mult),
                      r=[("rw", "s1")], w=[("rw", "var")])
                em.op(S, lambda e: e.activation(out=rw["l0"][:], in_=rw["var"][:], func=AF.Ln,
                                                bias=cst[0:1, C_ERMS:C_ERMS + 1]),
                      r=[("rw", "var"), ("cst",)], w=[("rw", "l0")])
                em.op(S, lambda e: e.activation(out=rw["rr"][:], in_=rw["l0"][:], func=AF.Exp,
                                                scale=cst[0:1, C_NHALF:C_NHALF + 1]),
                      r=[("rw", "l0"), ("cst",)], w=[("rw", "rr")], tag="rms")
                bcast_row(rw["rr"][:], ("rw", "rr"), TC, rr_b, ("rr_b",), psum[5], 5)
                for ocq in range(max(1, HC // 4)):
                    nm = min(4, HC)
                    for icc in range(IC):
                        b = 4 + icc % 4
                        em.op(SY, lambda e, ocq=ocq, icc=icc, b=b, nm=nm: e.dma_start(
                            out=wp[b][0:128, 0:nm * 128],
                            in_=qpr[ocq * nm * 128:(ocq + 1) * nm * 128,
                                    icc * 128:(icc + 1) * 128], transpose=True),
                            r=[("qpr",)], w=[("wp", b)], dma=True, tag="qpr_tr")
                        for m in range(nm):
                            em.op(TE, lambda e, m=m, b=b, icc=icc, st=(icc == 0): e.matmul(
                                psum[m][:, 0:TC], wp[b][:, m * 128:(m + 1) * 128],
                                hbf[icc][:, 0:TC], start=st, stop=(icc == IC - 1)),
                                r=[("wp", b), ("hb", icc)], w=[("psum", m)], tag="mm_pr")
                    for m in range(nm):
                        oc = ocq * nm + m
                        em.op(V, lambda e, m=m, oc=oc: e.tensor_tensor(
                            out=ybf[oc][:], in0=psum[m][:, 0:TC], in1=rr_b[:, 0:TC],
                            op=OP.mult),
                            r=[("psum", m), ("rr_b",)], w=[("x5", oc)], tag="y5")
                ln_out_add(ybf, [("x5", c) for c in range(HC)], xsq,
                           [("xsq5", c) for c in range(HC)], tj, 0, TC, final=True,
                           fin_bufs=fbufs)
        em.barrier("l1f")

    phases = cfg.get("phases", "qaf1AF")
    qpump, _nq = make_quant(es, "fcpr", "Q")
    if "q" in phases:
        qpump(10 ** 9)
        em.barrier("q")
    if "a" in phases:
        l0_attn()
    if "f" in phases:
        l0_ffn()
    if "A" in phases:
        l1_attn()
    if "F" in phases:
        l1_ffn()

    with nc.Block() as block:
        em.finalize(block, sems)
    es.close()
    return nc


def make_wmap(inputs, I):
    jj = np.arange(I)
    g = jj // 4
    m = jj % 4
    newpos = (g // 128) * 512 + m * 128 + (g % 128)
    perm = np.empty(I, dtype=np.int64)
    perm[newpos] = jj

    def C(a):
        return np.ascontiguousarray(a, dtype=np.float32)

    return {
        "wkT": C(inputs["L0_attn_Wk"].T), "wvT": C(inputs["L0_attn_Wv"].T),
        "wrT": C(inputs["L0_attn_Wr"].T), "woT": C(inputs["L0_attn_Wo"].T),
        "ln_w": C(inputs["L0_attn_ln_w"]), "ln_b": C(inputs["L0_attn_ln_b"]),
        "mix_k": C(inputs["L0_attn_mix_k"]), "mix_v": C(inputs["L0_attn_mix_v"]),
        "mix_r": C(inputs["L0_attn_mix_r"]), "td": C(inputs["L0_attn_time_decay"]),
        "cw": C(inputs["L0_attn_current_token_weight"]),
        "w_gate": C(inputs["L0_ffn_w_gate"][:, perm]),
        "w_up": C(inputs["L0_ffn_w_up"][:, perm]),
        "w_down": C(inputs["L0_ffn_w_down"][perm, :]),
        "ws": C(inputs["L1_attn_Ws"]), "wa": C(inputs["L1_attn_Wa"]),
        "woT1": C(inputs["L1_attn_Wo"].T), "bo": C(inputs["L1_attn_bo"]),
        "w_fc": C(inputs["L1_ffn_W_fc"]), "w_proj": C(inputs["L1_ffn_W_proj"]),
        "tri": np.where(np.arange(128)[None, :] < np.arange(128)[:, None],
                        np.float32(NEG_BIG), np.float32(0.0)),
    }


def kernel(**inputs):
    cfg = full_cfg()
    H, T, I = cfg["H"], cfg["T"], cfg["I"]
    B = inputs["L0_attn_hidden_state"].shape[0]
    x = np.ascontiguousarray(inputs["L0_attn_hidden_state"])
    wmap = make_wmap(inputs, I)
    nc = build_nc(cfg)
    in_maps = []
    for b in range(B):
        mm = dict(wmap)
        mm["x_in"] = np.ascontiguousarray(x[b].T, dtype=np.float32)
        in_maps.append(mm)
    res_ = run_bass_kernel_spmd(nc, in_maps, core_ids=list(range(B)))
    out = np.empty((B, T, H), dtype=np.float32)
    for b in range(B):
        out[b] = res_.results[b]["y_out"].T
    return out


# version 31
# speedup vs baseline: 1.1291x; 1.1291x over previous
import contextlib
import numpy as np

import concourse.bass as bass
import concourse.mybir as mybir
from concourse.bass_utils import run_bass_kernel_spmd

F32 = mybir.dt.float32
BF16 = mybir.dt.bfloat16
AF = mybir.ActivationFunctionType
OP = mybir.AluOpType
AX = mybir.AxisListType

C_T = float(np.float32(np.float32(0.5009765625) * np.float32(1.0 + 2.0 ** -20)))
V45C = float(2.0 ** 45)
V29C = float(2.0 ** 29)
K0 = 1.000244140625
NEG_BIG = -1e38
LN_EPS = 1e-5
RMS_EPS = 1.1920929e-7
ENGINES = ("tensor", "vector", "scalar", "gpsimd", "sync")


def full_cfg():
    return dict(H=2048, T=2048, I=8192, R=256, TC=512)


class Emitter:

    KQ = 8

    def __init__(self):
        self.ops = []
        self.last_writer = {}
        self.readers = {}
        self.qcount = {}
        self.qlast = {}

    def op(self, eng, fn, r=(), w=(), dma=False, tag="", extra_deps=()):
        idx = len(self.ops)
        deps = set(extra_deps)
        same = False

        def classify(j):
            nonlocal same
            rec = self.ops[j]
            if rec["eng"] == eng and rec["dma"] == dma and not dma:
                same = True
            else:
                deps.add(j)

        for reg in r:
            for wj in self.last_writer.get(reg, ()):
                classify(wj)
        for reg in w:
            for wj in self.last_writer.get(reg, ()):
                classify(wj)
            for rd in self.readers.get(reg, ()):
                classify(rd)
        self.ops.append(dict(eng=eng, fn=fn, dma=dma, deps=deps, tag=tag,
                             need_inc=False, drain_before=same))
        if dma:
            qi = self.qcount.get(eng, 0)
            self.qcount[eng] = qi + 1
            self.qlast[(eng, qi % self.KQ)] = idx
        for reg in r:
            self.readers.setdefault(reg, []).append(idx)
        for reg in w:
            if dma and self.last_writer.get(reg):
                self.last_writer[reg].append(idx)
            else:
                self.last_writer[reg] = [idx]
            self.readers[reg] = []
        return idx

    def barrier(self, tag="bar"):
        last = {}
        for i, rec in enumerate(self.ops):
            if not rec["dma"]:
                last[rec["eng"]] = i
        deps = tuple(last.values()) + tuple(self.qlast.values())
        for e in ENGINES:
            self.op(e, lambda en: en.nop(), extra_deps=deps, tag=tag + "_" + e)
        self.last_writer = {}
        self.readers = {}

    def finalize(self, block, sems):
        for rec in self.ops:
            for d in rec["deps"]:
                self.ops[d]["need_inc"] = True
        for rec in self.ops:
            if rec["dma"]:
                rec["need_inc"] = True
        counts = {}
        qidx = {}
        for rec in self.ops:
            if rec["dma"]:
                q = rec["eng"]
                i = qidx.get(q, 0)
                qidx[q] = i + 1
                sub = i % self.KQ
                key = (q, True, sub)
                counts[key] = counts.get(key, 0) + 16
                rec["sem_key"] = key
                rec["sem_val"] = counts[key]
            elif rec["need_inc"]:
                key = (rec["eng"], False)
                counts[key] = counts.get(key, 0) + 1
                rec["sem_key"] = key
                rec["sem_val"] = counts[key]
            else:
                rec["sem_key"] = rec["sem_val"] = None
        for rec in self.ops:
            waits = {}
            for d in rec["deps"]:
                dep = self.ops[d]
                waits[dep["sem_key"]] = max(waits.get(dep["sem_key"], 0),
                                            dep["sem_val"])
            if not rec["dma"]:
                waits.pop((rec["eng"], False), None)
            rec["waits"] = waits
        self.final_counts = counts

        def run(eng_name, e):
            seen = {}
            for rec in self.ops:
                if rec["eng"] != eng_name:
                    continue
                for key in sorted(rec["waits"], key=str):
                    v = rec["waits"][key]
                    if seen.get(key, -1) < v:
                        e.wait_ge(sems[key], v)
                        seen[key] = v
                if rec["dma"] and rec["sem_val"] > 16:
                    k2, v2 = rec["sem_key"], rec["sem_val"] - 16
                    if seen.get(k2, -1) < v2:
                        e.wait_ge(sems[k2], v2)
                        seen[k2] = v2
                if rec["drain_before"] and eng_name in ("vector", "scalar", "gpsimd"):
                    e.drain()
                inst = rec["fn"](e)
                if rec["sem_key"] is not None:
                    if inst is None:
                        inst = e.nop()
                    inst.then_inc(sems[rec["sem_key"]], 16 if rec["dma"] else 1)

        @block.tensor
        def _(e):
            run("tensor", e)

        @block.vector
        def _(e):
            run("vector", e)

        @block.scalar
        def _(e):
            run("scalar", e)

        @block.gpsimd
        def _(e):
            run("gpsimd", e)

        @block.sync
        def _(e):
            run("sync", e)
            for key in sorted(self.final_counts, key=str):
                if key[1] is True:
                    e.wait_ge(sems[key], self.final_counts[key])


def build_nc(cfg):
    H, T, I, R, TC = cfg["H"], cfg["T"], cfg["I"], cfg["R"], cfg["TC"]
    HC = H // 128
    TN = T // TC
    IC = I // 128
    RC = R // 128
    JB = T // 128
    TCB = 128
    GRP = 64
    QW = 1024 if H >= 1024 else H
    assert TC <= 512

    nc = bass.Bass("TRN2", target_bir_lowering=False)
    x_in = nc.dram_tensor("x_in", [H, T], F32, kind="ExternalInput").ap()
    y_out = nc.dram_tensor("y_out", [H, T], F32, kind="ExternalOutput").ap()
    w_in = {}
    for name, shape in [
        ("wkT", [H, H]), ("wvT", [H, H]), ("wrT", [H, H]), ("woT", [H, H]),
        ("ln_w", [H]), ("ln_b", [H]), ("mix_k", [H]), ("mix_v", [H]),
        ("mix_r", [H]), ("td", [H]), ("cw", [H]),
        ("w_gate", [H, I]), ("w_up", [H, I]), ("w_down", [I, H]),
        ("ws", [H, R]), ("wa", [H, R]), ("woT1", [H, H]), ("bo", [H]),
        ("w_fc", [I, H]), ("w_proj", [H, I]),
    ]:
        w_in[name] = nc.dram_tensor(name, shape, F32, kind="ExternalInput").ap()
    tri_in = nc.dram_tensor("tri", [128, 128], F32, kind="ExternalInput").ap()
    qfc = nc.dram_tensor("qfc", [I, H], BF16, kind="Internal").ap()
    qpr = nc.dram_tensor("qpr", [H, I], BF16, kind="Internal").ap()
    PRECONV = ["wkT", "wvT", "wrT", "woT", "w_gate", "w_up", "w_down", "woT1",
               "ws", "wa"]
    wbf = {n: nc.dram_tensor("bf_" + n, list(w_in[n].shape), BF16,
                             kind="Internal").ap() for n in PRECONV}

    em = Emitter()
    es = contextlib.ExitStack()

    def sb(name, shape, dtype=F32):
        return es.enter_context(nc.sbuf_tensor(name, shape, dtype))

    res = [sb(f"res{c}", [128, T], BF16) for c in range(HC)]
    pc = {n: sb("pc_" + n, [128, HC]) for n in
          ["ln_w", "ln_b", "mix_k", "mix_v", "mix_r", "cw", "wdec", "bo", "ecw"]}
    ones_bf = sb("ones_bf", [128, 1], BF16)
    ones1_f = sb("ones1_f", [1, 128], F32)
    cst = sb("cst", [128, 8])
    C_HALF, C_NHALF, C_ELN, C_ERMS, C_N1, C_S1, C_S2, C_I16 = range(8)
    dcol = sb("dcol", [128, 2])
    stN = sb("stN", [128, HC])
    stD = sb("stD", [128, HC])
    xprev = sb("xprev", [128, HC])
    rw = {n: sb("rw_" + n, [1, TC]) for n in
          ["s1", "s2", "mu", "var", "l0", "rr", "tmp"]}
    rw["r0"] = rw["s1"]
    rw["vy"] = rw["tmp"]
    rw["l1"] = rw["s2"]
    mu_b = sb("mu_b", [128, TC], BF16)
    rr_b = sb("rr_b", [128, TC])
    rc_b = sb("rc_b", [128, TC], BF16)
    amask = sb("amask", [128, 128], BF16)
    tA = sb("tA", [128, TC])
    tB = sb("tB", [128, TC])
    wp = [sb(f"wp{i}", [128, 512], BF16) for i in range(8)]
    cv = [None, None]
    tC = [None]

    psum = [es.enter_context(nc.psum_tensor(f"ps{i}", [128, 512], F32))
            for i in range(6)]
    prow = es.enter_context(nc.psum_tensor("prow", [1, 512], F32))
    prow2 = es.enter_context(nc.psum_tensor("prow2", [1, 512], F32))

    sems = {}
    for eng in ENGINES:
        sems[(eng, False)] = es.enter_context(nc.semaphore("s_" + eng))
    for q in ("sync", "gpsimd"):
        for sub in range(8):
            sems[(q, True, sub)] = es.enter_context(nc.semaphore(f"sd_{q}{sub}"))

    V, S, G, TE, SY = "vector", "scalar", "gpsimd", "tensor", "sync"

    from concourse.hw_specs import get_activation_tables
    _tab_names = list(get_activation_tables(nc.m.arch).keys())

    def preload_set(e, name):
        idx = _tab_names.index(name)
        return e.add_instruction(mybir.InstLoadActFuncSet(
            name=e.bass.get_next_instruction_name(), act_func_set_id=idx,
            ins=[], outs=[]))

    for c in range(HC):
        em.op(G, lambda e, c=c: e.dma_start(out=res[c][:], in_=x_in[c * 128:(c + 1) * 128, :]),
              r=[("x_in",)], w=[("res", c)], dma=True, tag="ld_x")
    for name in ["ln_w", "ln_b", "mix_k", "mix_v", "mix_r", "cw", "bo"]:
        ap = w_in[name].rearrange("(c p) -> p c", p=128)
        em.op(SY, lambda e, n=name, a=ap: e.dma_start(out=pc[n][:], in_=a, allow_slow_non_contiguous=True),
              r=[("win", name)], w=[("pc", name)], dma=True)
    em.op(SY, lambda e: e.dma_start(out=pc["wdec"][:],
                                    in_=w_in["td"].rearrange("(c p) -> p c", p=128),
                                    allow_slow_non_contiguous=True),
          r=[("win", "td")], w=[("pc", "wdec")], dma=True)
    for i, v in [(C_HALF, 0.5), (C_NHALF, -0.5), (C_ELN, LN_EPS),
                 (C_ERMS, RMS_EPS), (C_N1, -1.0), (C_I16, 1.0 / 16.0)]:
        em.op(V, lambda e, i=i, v=v: e.memset(cst[:, i:i + 1], v), w=[("cst",)])
    em.op(V, lambda e: e.memset(ones_bf[:], 1.0), w=[("ones_bf",)])
    em.op(V, lambda e: e.memset(ones1_f[:], 1.0), w=[("ones1",)])
    em.op(V, lambda e: e.memset(stN[:], 0.0), w=[("stN",)])
    em.op(V, lambda e: e.memset(stD[:], 0.0), w=[("stD",)])
    em.op(V, lambda e: e.memset(xprev[:], 0.0), w=[("xprev",)])
    em.op(V, lambda e: e.memset(dcol[:, 0:1], K0), w=[("dcol",)])
    em.op(V, lambda e: e.tensor_scalar(out=cst[:, C_S1:C_S1 + 1], in0=dcol[:, 0:1],
                                       scalar1=V45C, scalar2=None, op0=OP.mult),
          r=[("dcol",)], w=[("cst",)])
    em.op(V, lambda e: e.tensor_tensor(out=cst[:, C_S2:C_S2 + 1], in0=cst[:, C_S1:C_S1 + 1],
                                       in1=dcol[:, 0:1], op=OP.add),
          r=[("dcol",), ("cst",)], w=[("cst",)])
    em.op(V, lambda e: e.tensor_tensor(out=cst[:, C_S1:C_S1 + 1], in0=cst[:, C_S2:C_S2 + 1],
                                       in1=dcol[:, 0:1], op=OP.subtract),
          r=[("dcol",), ("cst",)], w=[("cst",)])
    em.op(V, lambda e: e.tensor_tensor(out=dcol[:, 1:2], in0=cst[:, C_S2:C_S2 + 1],
                                       in1=cst[:, C_S1:C_S1 + 1], op=OP.subtract),
          r=[("cst",)], w=[("dcol",)])
    em.op(V, lambda e: e.tensor_scalar(out=dcol[:, 0:1], in0=dcol[:, 1:2], scalar1=-1.0,
                                       scalar2=1.0, op0=OP.mult, op1=OP.add),
          r=[("dcol",)], w=[("dcol",)], tag="D")
    em.op(S, lambda e: preload_set(e, "natural_log_exp_and_others"))
    em.op(S, lambda e: e.activation(out=pc["wdec"][:], in_=pc["wdec"][:], func=AF.Exp),
          r=[("pc", "wdec")], w=[("pc", "wdec")])
    em.op(S, lambda e: e.activation(out=pc["wdec"][:], in_=pc["wdec"][:], func=AF.Exp,
                                    scale=cst[:, C_N1:C_N1 + 1]),
          r=[("pc", "wdec"), ("cst",)], w=[("pc", "wdec")], tag="wdec")
    em.op(S, lambda e: e.activation(out=pc["ecw"][:], in_=pc["cw"][:], func=AF.Exp),
          r=[("pc", "cw")], w=[("pc", "ecw")], tag="ecw")
    em.op(G, lambda e: e.dma_start(out=amask[:], in_=tri_in[:]),
          r=[("tri_in",)], w=[("amask",)], dma=True, tag="ld_tri")

    def load_piece(b, src, skey, row0, col0, ncols=512, nrows=128):
        name = skey[1]
        bsrc = wbf[{"wg": "w_gate", "wu": "w_up", "wd": "w_down"}.get(name, name)]
        rkeys = [("wbf", name, (row0 + d) // 128, cp)
                 for d in range(0, nrows, 128)
                 for cp in range(col0 // 1024, (col0 + ncols - 1) // 1024 + 1)]
        em.op(SY, lambda e: e.dma_start(out=wp[b][0:nrows, 0:ncols],
                                        in_=bsrc[row0:row0 + nrows, col0:col0 + ncols]),
              r=rkeys, w=[("wp", b)], dma=True, tag="wld")

    def bcast_row(row, rkey, width, dst, dkey, bank, bidx):
        for t0 in range(0, width, 512):
            n = min(512, width - t0)
            em.op(TE, lambda e, t0=t0, n=n: e.matmul(bank[0:128, 0:n], ones1_f[:],
                                                     row[0:1, t0:t0 + n],
                                                     start=True, stop=True),
                  r=[rkey, ("ones1",)], w=[("psum", bidx)], tag="bc")
            em.op(S, lambda e, t0=t0, n=n: e.copy(out=dst[:, t0:t0 + n],
                                                  in_=bank[0:128, 0:n]),
                  r=[("psum", bidx)], w=[dkey], tag="bc_e")

    def ln_rows(src_of, src_keys, sq_tiles, sq_keys, n_feat, w, eps_idx,
                fused=False, rr0=False):
        nch = len(src_keys)
        for c in range(nch):
            em.op(S, lambda e, c=c: e.activation(out=sq_tiles[c][:, 0:w],
                                                 in_=src_of(c, 0, w), func=AF.Square),
                  r=[src_keys[c]], w=[sq_keys[c]], tag="sq")
        for t0 in range(0, w, 512):
            n = min(512, w - t0)
            for c in range(nch):
                em.op(TE, lambda e, c=c, t0=t0, n=n, st=(c == 0): e.matmul(
                    prow[0:1, 0:n], ones_bf[:], src_of(c, t0, t0 + n),
                    start=st, stop=(c == nch - 1)),
                    r=[("ones_bf",), src_keys[c]], w=[("prow",)], tag="sum1")
            for c in range(nch):
                em.op(TE, lambda e, c=c, t0=t0, n=n, st=(c == 0): e.matmul(
                    prow2[0:1, 0:n], ones_bf[:], sq_tiles[c][:, t0:t0 + n],
                    start=st, stop=(c == nch - 1)),
                    r=[("ones_bf",), sq_keys[c]], w=[("prow2",)], tag="sum2")
            em.op(V, lambda e, t0=t0, n=n: e.tensor_copy(out=rw["s1"][:, t0:t0 + n],
                                                         in_=prow[0:1, 0:n]),
                  r=[("prow",)], w=[("rw", "s1")])
            em.op(V, lambda e, t0=t0, n=n: e.tensor_copy(out=rw["s2"][:, t0:t0 + n],
                                                         in_=prow2[0:1, 0:n]),
                  r=[("prow2",)], w=[("rw", "s2")])
        inv = 1.0 / n_feat
        em.op(V, lambda e: e.tensor_scalar(out=rw["mu"][:, 0:w], in0=rw["s1"][:, 0:w],
                                           scalar1=inv, scalar2=None, op0=OP.mult),
              r=[("rw", "s1")], w=[("rw", "mu")])
        em.op(V, lambda e: e.tensor_tensor(out=rw["tmp"][:, 0:w], in0=rw["mu"][:, 0:w],
                                           in1=rw["mu"][:, 0:w], op=OP.mult),
              r=[("rw", "mu")], w=[("rw", "tmp")])
        em.op(V, lambda e: e.tensor_scalar(out=rw["var"][:, 0:w], in0=rw["s2"][:, 0:w],
                                           scalar1=inv, scalar2=None, op0=OP.mult),
              r=[("rw", "s2")], w=[("rw", "var")])
        em.op(V, lambda e: e.tensor_tensor(out=rw["var"][:, 0:w], in0=rw["var"][:, 0:w],
                                           in1=rw["tmp"][:, 0:w], op=OP.subtract),
              r=[("rw", "var"), ("rw", "tmp")], w=[("rw", "var")])
        em.op(S, lambda e: e.activation(out=rw["l0"][:, 0:w], in_=rw["var"][:, 0:w],
                                        func=AF.Ln, bias=cst[0:1, eps_idx:eps_idx + 1]),
              r=[("rw", "var"), ("cst",)], w=[("rw", "l0")])
        if fused:
            em.op(S, lambda e: e.activation(out=rw["r0"][:, 0:w], in_=rw["l0"][:, 0:w],
                                            func=AF.Exp, scale=cst[0:1, C_N1:C_N1 + 1]),
                  r=[("rw", "l0"), ("cst",), ("rw", "s1")], w=[("rw", "s1")])
            em.op(V, lambda e: e.tensor_tensor(out=rw["vy"][:, 0:w], in0=rw["var"][:, 0:w],
                                               in1=rw["r0"][:, 0:w], op=OP.mult),
                  r=[("rw", "var"), ("rw", "s1"), ("rw", "tmp")], w=[("rw", "tmp")])
            em.op(S, lambda e: e.activation(out=rw["l1"][:, 0:w], in_=rw["vy"][:, 0:w],
                                            func=AF.Ln, bias=cst[0:1, eps_idx:eps_idx + 1]),
                  r=[("rw", "tmp"), ("cst",)], w=[("rw", "s2")])
            em.op(V, lambda e: e.tensor_tensor(out=rw["tmp"][:, 0:w], in0=rw["l1"][:, 0:w],
                                               in1=rw["l0"][:, 0:w], op=OP.add),
                  r=[("rw", "s2"), ("rw", "l0")], w=[("rw", "tmp")])
            em.op(S, lambda e: e.activation(out=rw["rr"][:, 0:w], in_=rw["tmp"][:, 0:w],
                                            func=AF.Exp, scale=cst[0:1, C_NHALF:C_NHALF + 1]),
                  r=[("rw", "tmp"), ("cst",)], w=[("rw", "rr")])
        else:
            em.op(S, lambda e: e.activation(out=rw["rr"][:, 0:w], in_=rw["l0"][:, 0:w],
                                            func=AF.Exp, scale=cst[0:1, C_NHALF:C_NHALF + 1]),
                  r=[("rw", "l0"), ("cst",)], w=[("rw", "rr")])
        bcast_row(rw["mu"][:], ("rw", "mu"), w, mu_b, ("mu_b",), psum[4], 4)
        bcast_row(rw["rr"][:], ("rw", "rr"), w, rr_b, ("rr_b",), psum[5], 5)
        if rr0:
            em.op(S, lambda e: e.activation(out=rw["tmp"][:, 0:w], in_=rw["l0"][:, 0:w],
                                            func=AF.Exp, scale=cst[0:1, C_NHALF:C_NHALF + 1]),
                  r=[("rw", "l0"), ("cst",)], w=[("rw", "tmp")])
            bcast_row(rw["tmp"][:], ("rw", "tmp"), w, rc_b, ("rc_b",), psum[4], 4)

    def res_src(tj):
        def f(c, a, b):
            return res[c][:, tj * TC + a: tj * TC + b]
        return f, [("res", c) for c in range(HC)]

    def ln_out_add(y_tiles, y_keys, sq_tiles, sq_keys, tj, off, w, final=False,
                   fin_bufs=None):
        ln_rows(lambda c, a, b: y_tiles[c][:, a:b], y_keys,
                sq_tiles, sq_keys, H, w, C_ELN)
        for c in range(HC):
            tsl = slice(tj * TC + off, tj * TC + off + w)
            em.op(V, lambda e, c=c: e.tensor_tensor(out=tA[:, 0:w],
                                                    in0=y_tiles[c][:, 0:w],
                                                    in1=mu_b[:, 0:w], op=OP.subtract),
                  r=[y_keys[c], ("mu_b",)], w=[("tA",)])
            em.op(V, lambda e: e.tensor_tensor(out=tB[:, 0:w], in0=tA[:, 0:w],
                                               in1=rr_b[:, 0:w], op=OP.mult),
                  r=[("tA",), ("rr_b",)], w=[("tB",)])
            if final:
                fb = fin_bufs[c % len(fin_bufs)]
                fk = ("finb", c % len(fin_bufs))
                em.op(V, lambda e, c=c, tsl=tsl, fb=fb: e.tensor_tensor(
                    out=fb[:, 0:w], in0=res[c][:, tsl], in1=tB[:, 0:w], op=OP.add),
                    r=[("res", c), ("tB",)], w=[fk])
                em.op(SY, lambda e, c=c, tsl=tsl, fb=fb: e.dma_start(
                    out=y_out[c * 128:(c + 1) * 128, tsl], in_=fb[:, 0:w]),
                    r=[fk], w=[("y_out",)], dma=True, tag="store_y")
            else:
                em.op(V, lambda e, c=c, tsl=tsl: e.tensor_tensor(
                    out=res[c][:, tsl], in0=res[c][:, tsl], in1=tB[:, 0:w], op=OP.add),
                    r=[("res", c), ("tB",)], w=[("res", c)], tag="res_upd")

    def preconv_all():
        i = 0
        for n in PRECONV:
            srcd = w_in[n]
            rows, cols = srcd.shape
            key = {"w_gate": "wg", "w_up": "wu", "w_down": "wd"}.get(n, n)
            for r0 in range(0, rows, 128):
                for c0 in range(0, cols, 512):
                    cw = min(512, cols - c0)
                    b = i % 2
                    i += 1
                    em.op(G, lambda e, b=b, srcd=srcd, r0=r0, c0=c0, cw=cw: e.dma_start(
                        out=cv[b][:, 0:cw], in_=srcd[r0:r0 + 128, c0:c0 + cw]),
                        r=[("win", n)], w=[("cv", b)], dma=True, tag="pc_ld")
                    em.op(G, lambda e, b=b, n=n, r0=r0, c0=c0, cw=cw: e.dma_start(
                        out=wbf[n][r0:r0 + 128, c0:c0 + cw], in_=cv[b][:, 0:cw]),
                        r=[("cv", b)], w=[("wbf", key, r0 // 128, c0 // 1024)],
                        dma=True, tag="pc_st")

    def make_quant(qes, which, pfx=""):
        qsb = lambda n, sh, dt=F32: qes.enter_context(nc.sbuf_tensor(pfx + n, sh, dt))
        NG = QW // GRP
        S2 = 2
        qw = [qsb(f"qw{i}", [128, QW]) for i in range(2)]
        qc1 = [qsb(f"qc1{i}", [128, QW]) for i in range(S2)]
        qd1 = [qsb(f"qd1{i}", [128, QW]) for i in range(S2)]
        qV = qd1
        qwb = [qsb(f"qwb{i}", [128, QW], BF16) for i in range(S2)]
        qg1 = [qsb(f"qg1{i}", [128, QW], BF16) for i in range(S2)]
        qout = [qsb(f"qout{i}", [128, QW], BF16) for i in range(S2)]
        qg2 = qout
        qs_ = [qsb(f"qs{i}", [128, NG]) for i in range(S2)]
        qm = [qsb(f"qm{i}", [128, NG]) for i in range(S2)]
        qt_ = [qsb(f"qt{i}", [128, NG]) for i in range(S2)]
        qnt = [qsb(f"qnt{i}", [128, NG]) for i in range(S2)]
        qsc = [qsb(f"qsc{i}", [128, NG], BF16) for i in range(S2)]
        qtm = [qsb(f"qtm{i}", [128, NG]) for i in range(S2)]

        def bcg(t):
            a = t[:]
            return bass.AP(a.tensor, a.offset, [a.ap[0], [1, NG], [0, GRP]])

        def chunk(src_, skey, dst, dkey, r0, c0, qwi):
            s = qwi
            qwt, qwk = qw[qwi], (pfx, "qw", qwi)
            em.op(SY, lambda e: e.dma_start(out=qwt[:], in_=src_[r0:r0 + 128, c0:c0 + QW]),
                  r=[skey], w=[qwk], dma=True)
            em.op(V, lambda e: e.scalar_tensor_tensor(out=qc1[s][:], in0=qwt[:], scalar=V45C,
                                                      in1=qwt[:], op0=OP.mult, op1=OP.add),
                  r=[qwk], w=[(pfx, "qc1", s)])
            em.op(V, lambda e: e.tensor_tensor(out=qd1[s][:], in0=qc1[s][:], in1=qwt[:],
                                               op=OP.subtract),
                  r=[(pfx, "qc1", s), qwk], w=[(pfx, "qd1", s)])
            em.op(V, lambda e: e.tensor_tensor(out=qV[s][:], in0=qc1[s][:], in1=qd1[s][:],
                                               op=OP.subtract),
                  r=[(pfx, "qc1", s), (pfx, "qd1", s)], w=[(pfx, "qd1", s)])
            em.op(V, lambda e: e.scalar_tensor_tensor(out=qwb[s][:], in0=qwt[:],
                                                      scalar=dcol[:, 0:1], in1=qV[s][:],
                                                      op0=OP.mult, op1=OP.add),
                  r=[qwk, (pfx, "qd1", s), ("dcol",)], w=[(pfx, "qwb", s)], tag="q_wb")
            em.op(V, lambda e: e.tensor_reduce(
                out=qs_[s][:], in_=qwb[s][:].rearrange("p (g k) -> p g k", k=GRP),
                axis=AX.X, op=OP.add, apply_absolute_value=True),
                r=[(pfx, "qwb", s)], w=[(pfx, "qs_", s)], tag="q_red")
            em.op(V, lambda e: e.tensor_scalar(out=qs_[s][:], in0=qs_[s][:], scalar1=1.0 / GRP,
                                               scalar2=None, op0=OP.mult),
                  r=[(pfx, "qs_", s)], w=[(pfx, "qs_", s)])
            em.op(V, lambda e: e.scalar_tensor_tensor(out=qm[s][:], in0=qs_[s][:], scalar=V29C,
                                                      in1=qs_[s][:], op0=OP.mult, op1=OP.add),
                  r=[(pfx, "qs_", s)], w=[(pfx, "qm", s)])
            em.op(V, lambda e: e.tensor_tensor(out=qtm[s][:], in0=qm[s][:], in1=qs_[s][:],
                                               op=OP.subtract),
                  r=[(pfx, "qm", s), (pfx, "qs_", s)], w=[(pfx, "qtm", s)])
            em.op(V, lambda e: e.tensor_tensor(out=qm[s][:], in0=qm[s][:], in1=qtm[s][:],
                                               op=OP.subtract),
                  r=[(pfx, "qm", s), (pfx, "qtm", s)], w=[(pfx, "qm", s)])
            em.op(V, lambda e: e.scalar_tensor_tensor(out=qm[s][:], in0=qs_[s][:],
                                                      scalar=dcol[:, 0:1], in1=qm[s][:],
                                                      op0=OP.mult, op1=OP.add),
                  r=[(pfx, "qs_", s), (pfx, "qm", s), ("dcol",)], w=[(pfx, "qm", s)])
            em.op(V, lambda e: e.tensor_scalar(out=qm[s][:], in0=qm[s][:], scalar1=1e-8,
                                               scalar2=None, op0=OP.max),
                  r=[(pfx, "qm", s)], w=[(pfx, "qm", s)])
            em.op(V, lambda e: e.scalar_tensor_tensor(out=qtm[s][:], in0=qm[s][:], scalar=V45C,
                                                      in1=qm[s][:], op0=OP.mult, op1=OP.add),
                  r=[(pfx, "qm", s)], w=[(pfx, "qtm", s)])
            em.op(V, lambda e: e.tensor_tensor(out=qt_[s][:], in0=qtm[s][:], in1=qm[s][:],
                                               op=OP.subtract),
                  r=[(pfx, "qtm", s), (pfx, "qm", s)], w=[(pfx, "qt_", s)])
            em.op(V, lambda e: e.tensor_tensor(out=qt_[s][:], in0=qtm[s][:], in1=qt_[s][:],
                                               op=OP.subtract),
                  r=[(pfx, "qtm", s), (pfx, "qt_", s)], w=[(pfx, "qt_", s)])
            em.op(V, lambda e: e.scalar_tensor_tensor(out=qsc[s][:], in0=qm[s][:],
                                                      scalar=dcol[:, 0:1], in1=qt_[s][:],
                                                      op0=OP.mult, op1=OP.add),
                  r=[(pfx, "qm", s), (pfx, "qt_", s), ("dcol",)], w=[(pfx, "qsc", s)], tag="q_sc")
            em.op(V, lambda e: e.tensor_scalar(out=qt_[s][:], in0=qsc[s][:], scalar1=C_T,
                                               scalar2=None, op0=OP.mult),
                  r=[(pfx, "qsc", s)], w=[(pfx, "qt_", s)])
            em.op(V, lambda e: e.tensor_scalar(out=qnt[s][:], in0=qt_[s][:], scalar1=-1.0,
                                               scalar2=None, op0=OP.mult),
                  r=[(pfx, "qt_", s)], w=[(pfx, "qnt", s)])
            em.op(V, lambda e: e.tensor_tensor(
                out=qg1[s][:].rearrange("p (g k) -> p g k", k=GRP),
                in0=qwb[s][:].rearrange("p (g k) -> p g k", k=GRP),
                in1=bcg(qt_[s]), op=OP.is_gt),
                r=[(pfx, "qwb", s), (pfx, "qt_", s)], w=[(pfx, "qg1", s)], tag="q_g1")
            em.op(V, lambda e: e.tensor_tensor(
                out=qg2[s][:].rearrange("p (g k) -> p g k", k=GRP),
                in0=qwb[s][:].rearrange("p (g k) -> p g k", k=GRP),
                in1=bcg(qnt[s]), op=OP.is_lt),
                r=[(pfx, "qwb", s), (pfx, "qnt", s)], w=[(pfx, "qout", s)], tag="q_g2")
            em.op(V, lambda e: e.tensor_tensor(out=qg1[s][:], in0=qg1[s][:], in1=qg2[s][:],
                                               op=OP.subtract),
                  r=[(pfx, "qg1", s), (pfx, "qout", s)], w=[(pfx, "qg1", s)])
            em.op(V, lambda e: e.tensor_tensor(
                out=qout[s][:].rearrange("p (g k) -> p g k", k=GRP),
                in0=qg1[s][:].rearrange("p (g k) -> p g k", k=GRP),
                in1=bcg(qsc[s]), op=OP.mult),
                r=[(pfx, "qg1", s), (pfx, "qsc", s)], w=[(pfx, "qout", s)], tag="q_qs")
            em.op(SY, lambda e: e.dma_start(out=dst[r0:r0 + 128, c0:c0 + QW],
                                            in_=qout[s][:]),
                  r=[(pfx, "qout", s)], w=[dkey], dma=True, tag="q_st")

        chunks = []
        if "fc" in which:
            for r0 in range(0, I, 128):
                for c0 in range(0, H, QW):
                    chunks.append((w_in["w_fc"], ("win", "w_fc"), qfc, ("qfc",), r0, c0))
        if "pr" in which:
            for r0 in range(0, H, 128):
                for c0 in range(0, I, QW):
                    chunks.append((w_in["w_proj"], ("win", "w_proj"), qpr, ("qpr",), r0, c0))
        state = {"i": 0}

        def pump(n):
            while n > 0 and state["i"] < len(chunks):
                chunk(*chunks[state["i"]], qwi=state["i"] % 2)
                state["i"] += 1
                n -= 1
        return pump, len(chunks)

    def l0_attn():
        with contextlib.ExitStack() as pes:
            psb = lambda n, sh, dt=BF16: pes.enter_context(nc.sbuf_tensor(n, sh, dt))
            xk = [psb(f"xk{c}", [128, TC]) for c in range(HC)]
            xv = [psb(f"xv{c}", [128, TC]) for c in range(HC)]
            xr = [psb(f"xr{c}", [128, TC]) for c in range(HC)]
            rp = [psb(f"rp{c}", [128, TC]) for c in range(HC)]
            wkv = [psb(f"wkv{c}", [128, TC]) for c in range(HC)]
            rwk = xk
            ybf = xr
            ek4 = [pes.enter_context(nc.sbuf_tensor(f"ek4_{m}", [128, TC], F32))
                   for m in range(4)]
            vv4 = [pes.enter_context(nc.sbuf_tensor(f"vv4_{m}", [128, TC], F32))
                   for m in range(4)]
            xl1 = pes.enter_context(nc.sbuf_tensor("xl1", [128, TC + 1], BF16))
            cv[0] = pes.enter_context(nc.sbuf_tensor("cvp0", [128, 512], BF16))
            cv[1] = pes.enter_context(nc.sbuf_tensor("cvp1", [128, 512], BF16))
            preconv_all()
            scN = pes.enter_context(nc.sbuf_tensor("scN", [128, TC + 1], F32))
            scD = pes.enter_context(nc.sbuf_tensor("scD", [128, TC + 1], F32))
            tC[0] = pes.enter_context(nc.sbuf_tensor("tCa", [128, TC], F32))
            tD_ = tC[0]
            em.op(S, lambda e: preload_set(e, "natural_log_exp_and_others"), tag="tbl")
            for tj in range(TN):
                tsl = slice(tj * TC, (tj + 1) * TC)
                src, skeys = res_src(tj)
                ln_rows(src, skeys, rp, [("rp", c) for c in range(HC)], H, TC,
                        C_ELN, fused=True, rr0=True)
                for c in range(HC):
                    em.op(V, lambda e, c=c, tsl=tsl: e.tensor_tensor(
                        out=tA[:], in0=res[c][:, tsl], in1=mu_b[:], op=OP.subtract),
                        r=[("res", c), ("mu_b",)], w=[("tA",)])
                    em.op(V, lambda e: e.tensor_tensor(out=tB[:], in0=tA[:], in1=rr_b[:],
                                                       op=OP.mult),
                          r=[("tA",), ("rr_b",)], w=[("tB",)])
                    em.op(V, lambda e, c=c: e.tensor_copy(out=xl1[:, 0:1],
                                                          in_=xprev[:, c:c + 1]),
                          r=[("xprev",)], w=[("xl1",)])
                    em.op(V, lambda e, c=c: e.tensor_scalar(
                        out=xl1[:, 1:TC + 1], in0=tB[:], scalar1=pc["ln_w"][:, c:c + 1],
                        scalar2=pc["ln_b"][:, c:c + 1], op0=OP.mult, op1=OP.add),
                        r=[("tB",), ("pc", "ln_w"), ("pc", "ln_b")], w=[("xl1",)], tag="xl1")
                    em.op(V, lambda e, c=c: e.tensor_copy(out=xprev[:, c:c + 1],
                                                          in_=xl1[:, TC:TC + 1]),
                          r=[("xl1",)], w=[("xprev",)])
                    em.op(V, lambda e: e.tensor_tensor(out=tC[0][:], in0=xl1[:, 1:TC + 1],
                                                       in1=xl1[:, 0:TC], op=OP.subtract),
                          r=[("xl1",)], w=[("tC",)])
                    for mix, dst, dk in (("mix_k", xk, "xk"), ("mix_v", xv, "xv"),
                                         ("mix_r", xr, "xr")):
                        em.op(V, lambda e, c=c, mix=mix, dst=dst: e.scalar_tensor_tensor(
                            out=dst[c][:], in0=tC[0][:], scalar=pc[mix][:, c:c + 1],
                            in1=xl1[:, 0:TC], op0=OP.mult, op1=OP.add),
                            r=[("tC",), ("xl1",), ("pc", mix)], w=[(dk, c)], tag="xi")
                nm = min(4, HC)
                for cq in range(max(1, HC // nm)):
                    for hc in range(HC):
                        b = hc % 4
                        load_piece(b, w_in["wkT"], ("win", "wkT"), hc * 128, cq * nm * 128,
                                   ncols=nm * 128)
                        for m in range(nm):
                            em.op(TE, lambda e, hc=hc, m=m, b=b, st=(hc == 0): e.matmul(
                                psum[m][:, 0:TC], wp[b][:, m * 128:(m + 1) * 128],
                                xk[hc][:, 0:TC], start=st, stop=(hc == HC - 1)),
                                r=[("wp", b), ("xk", hc)], w=[("psum", m)], tag="mm_k")
                    for m in range(nm):
                        em.op(S, lambda e, m=m: e.activation(out=ek4[m][:], in_=psum[m][:, 0:TC],
                                                             func=AF.Exp),
                              r=[("psum", m)], w=[("ek4", m)], tag="ek")
                    for hc in range(HC):
                        b = 4 + hc % 4
                        load_piece(b, w_in["wvT"], ("win", "wvT"), hc * 128, cq * nm * 128,
                                   ncols=nm * 128)
                        for m in range(nm):
                            em.op(TE, lambda e, hc=hc, m=m, b=b, st=(hc == 0): e.matmul(
                                psum[m][:, 0:TC], wp[b][:, m * 128:(m + 1) * 128],
                                xv[hc][:, 0:TC], start=st, stop=(hc == HC - 1)),
                                r=[("wp", b), ("xv", hc)], w=[("psum", m)], tag="mm_v")
                    for m in range(nm):
                        em.op(S, lambda e, m=m: e.copy(out=vv4[m][:], in_=psum[m][:, 0:TC]),
                              r=[("psum", m)], w=[("vv4", m)], tag="v_ev")
                    for hc in range(HC):
                        b = hc % 4
                        load_piece(b, w_in["wrT"], ("win", "wrT"), hc * 128, cq * nm * 128,
                                   ncols=nm * 128)
                        for m in range(nm):
                            em.op(TE, lambda e, hc=hc, m=m, b=b, st=(hc == 0): e.matmul(
                                psum[m][:, 0:TC], wp[b][:, m * 128:(m + 1) * 128],
                                xr[hc][:, 0:TC], start=st, stop=(hc == HC - 1)),
                                r=[("wp", b), ("xr", hc)], w=[("psum", m)], tag="mm_r")
                    for m in range(nm):
                        co = cq * nm + m
                        em.op(S, lambda e, m=m, co=co: e.copy(out=rp[co][:],
                                                              in_=psum[m][:, 0:TC]),
                              r=[("psum", m)], w=[("rp", co)], tag="rp_ev")
                    for m in range(nm):
                        co = cq * nm + m
                        em.op(V, lambda e, m=m: e.tensor_tensor(out=tD_[:], in0=ek4[m][:],
                                                                in1=vv4[m][:], op=OP.mult),
                              r=[("ek4", m), ("vv4", m)], w=[("tC",)])
                        em.op(V, lambda e, co=co: e.tensor_copy(out=scN[:, 0:1],
                                                                in_=stN[:, co:co + 1]),
                              r=[("stN",)], w=[("scN",)])
                        em.op(V, lambda e, co=co: e.tensor_tensor_scan(
                            out=scN[:, 1:TC + 1],
                            data0=bass.AP(pc["wdec"][:].tensor,
                                          pc["wdec"][:].offset + co,
                                          [pc["wdec"][:].ap[0], [0, TC]]),
                            data1=tD_[:],
                            initial=stN[:, co:co + 1], op0=OP.mult, op1=OP.add),
                            r=[("pc", "wdec"), ("tC",), ("stN",)], w=[("scN",)], tag="scanN")
                        em.op(V, lambda e, co=co: e.tensor_copy(out=stN[:, co:co + 1],
                                                                in_=scN[:, TC:TC + 1]),
                              r=[("scN",)], w=[("stN",)])
                        em.op(V, lambda e, co=co: e.tensor_copy(out=scD[:, 0:1],
                                                                in_=stD[:, co:co + 1]),
                              r=[("stD",)], w=[("scD",)])
                        em.op(V, lambda e, co=co, m=m: e.tensor_tensor_scan(
                            out=scD[:, 1:TC + 1],
                            data0=bass.AP(pc["wdec"][:].tensor,
                                          pc["wdec"][:].offset + co,
                                          [pc["wdec"][:].ap[0], [0, TC]]),
                            data1=ek4[m][:],
                            initial=stD[:, co:co + 1], op0=OP.mult, op1=OP.add),
                            r=[("pc", "wdec"), ("ek4", m), ("stD",)], w=[("scD",)], tag="scanD")
                        em.op(V, lambda e, co=co: e.tensor_copy(out=stD[:, co:co + 1],
                                                                in_=scD[:, TC:TC + 1]),
                              r=[("scD",)], w=[("stD",)])
                        em.op(V, lambda e, co=co: e.tensor_scalar(
                            out=tA[:], in0=tD_[:], scalar1=pc["ecw"][:, co:co + 1],
                            scalar2=None, op0=OP.mult),
                            r=[("tC",), ("pc", "ecw")], w=[("tA",)], tag="ecurv")
                        em.op(V, lambda e: e.tensor_tensor(out=tA[:], in0=tA[:],
                                                           in1=scN[:, 0:TC], op=OP.add),
                              r=[("tA",), ("scN",)], w=[("tA",)])
                        em.op(V, lambda e, m=m, co=co: e.tensor_scalar(
                            out=tB[:], in0=ek4[m][:], scalar1=pc["ecw"][:, co:co + 1],
                            scalar2=None, op0=OP.mult),
                            r=[("ek4", m), ("pc", "ecw")], w=[("tB",)], tag="ecur")
                        em.op(V, lambda e: e.tensor_tensor(out=tB[:], in0=tB[:],
                                                           in1=scD[:, 0:TC], op=OP.add),
                              r=[("tB",), ("scD",)], w=[("tB",)])
                        em.op(V, lambda e: e.reciprocal(out=tC[0][:], in_=tB[:]),
                              r=[("tB",)], w=[("tC",)], tag="recip")
                        em.op(V, lambda e, co=co: e.tensor_tensor(out=wkv[co][:], in0=tA[:],
                                                                  in1=tC[0][:], op=OP.mult),
                              r=[("tA",), ("tC",)], w=[("wkv", co)], tag="wkv")
                em.op(S, lambda e: preload_set(e, "exp_and_others"), tag="tbl_t")
                for co in range(HC):
                    em.op(S, lambda e, co=co: e.activation(out=rp[co][:], in_=rp[co][:],
                                                           func=AF.Tanh,
                                                           scale=cst[:, C_HALF:C_HALF + 1]),
                          r=[("rp", co), ("cst",)], w=[("rp", co)], tag="tanh")
                    em.op(V, lambda e, co=co: e.tensor_scalar(out=tA[:], in0=rp[co][:],
                                                              scalar1=0.5, scalar2=0.5,
                                                              op0=OP.mult, op1=OP.add),
                          r=[("rp", co)], w=[("tA",)])
                    em.op(V, lambda e, co=co: e.tensor_tensor(out=rwk[co][:], in0=tA[:],
                                                              in1=wkv[co][:], op=OP.mult),
                          r=[("tA",), ("wkv", co)], w=[("xk", co)], tag="rwkv")
                em.op(S, lambda e: preload_set(e, "natural_log_exp_and_others"), tag="tbl_b")
                for cq in range(max(1, HC // 4)):
                    nm = min(4, HC)
                    for hc in range(HC):
                        b = 4 + hc % 4
                        load_piece(b, w_in["woT"], ("win", "woT"), hc * 128, cq * nm * 128,
                                   ncols=nm * 128)
                        for m in range(nm):
                            em.op(TE, lambda e, hc=hc, m=m, b=b, st=(hc == 0): e.matmul(
                                psum[m][:, 0:TC], wp[b][:, m * 128:(m + 1) * 128],
                                rwk[hc][:, 0:TC], start=st, stop=(hc == HC - 1)),
                                r=[("wp", b), ("xk", hc)], w=[("psum", m)], tag="mm_wo")
                    for m in range(nm):
                        co = cq * nm + m
                        em.op(V, lambda e, co=co, tsl=tsl: e.tensor_tensor(
                            out=tA[:], in0=res[co][:, tsl], in1=mu_b[:], op=OP.subtract),
                            r=[("res", co), ("mu_b",)], w=[("tA",)], tag="ln0a")
                        em.op(V, lambda e: e.tensor_tensor(out=tB[:], in0=tA[:],
                                                           in1=rc_b[:], op=OP.mult),
                              r=[("tA",), ("rc_b",)], w=[("tB",)], tag="ln0b")
                        em.op(V, lambda e, m=m, co=co: e.tensor_tensor(
                            out=ybf[co][:], in0=tB[:], in1=psum[m][:, 0:TC], op=OP.add),
                            r=[("tB",), ("psum", m)], w=[("xr", co)], tag="y0")
                ln_out_add(ybf, [("xr", c) for c in range(HC)], rp,
                           [("rp", c) for c in range(HC)], tj, 0, TC)
        em.barrier("l0a")

    def l0_ffn():
        with contextlib.ExitStack() as pes:
            psb = lambda n, sh, dt=BF16: pes.enter_context(nc.sbuf_tensor(n, sh, dt))
            xf = [psb(f"xf{c}", [128, TC]) for c in range(HC)]
            hsp = [psb(f"hsp{i}", [128, TC]) for i in range(IC)]
            sg4 = [psb(f"sg{m}", [128, TC]) for m in range(2)]
            ybf = xf
            tD_ = pes.enter_context(nc.sbuf_tensor("tDf", [128, TC], F32))
            tCf = pes.enter_context(nc.sbuf_tensor("tCf", [128, TC], F32))
            tE = pes.enter_context(nc.sbuf_tensor("tE", [128, TC], F32))
            tF = pes.enter_context(nc.sbuf_tensor("tF", [128, TC], F32))
            tG = pes.enter_context(nc.sbuf_tensor("tG", [128, TC], F32))
            tH = pes.enter_context(nc.sbuf_tensor("tH", [128, TC], F32))
            scN = pes.enter_context(nc.sbuf_tensor("scN2", [128, TC], F32))
            scD = pes.enter_context(nc.sbuf_tensor("scD2", [128, TC], F32))
            wb_t = pes.enter_context(nc.sbuf_tensor("wb_t2", [128, TC], F32))
            t2q = pes.enter_context(nc.sbuf_tensor("t2q", [128, TC], F32))
            for tj in range(TN):
                tsl = slice(tj * TC, (tj + 1) * TC)
                src, skeys = res_src(tj)
                ln_rows(src, skeys, hsp, [("hsp", c) for c in range(HC)], H, TC, C_ELN)
                for c in range(HC):
                    em.op(V, lambda e, c=c, tsl=tsl: e.tensor_tensor(
                        out=tA[:], in0=res[c][:, tsl], in1=mu_b[:], op=OP.subtract),
                        r=[("res", c), ("mu_b",)], w=[("tA",)])
                    em.op(V, lambda e, c=c: e.tensor_tensor(out=xf[c][:], in0=tA[:],
                                                            in1=rr_b[:], op=OP.mult),
                          r=[("tA",), ("rr_b",)], w=[("xf", c)], tag="xffn")
                em.op(S, lambda e: preload_set(e, "silu_and_others"), tag="tbl_s")
                for icq in range(IC // 4):
                    for hc in range(HC):
                        b = hc % 4
                        load_piece(b, w_in["w_gate"], ("win", "wg"), hc * 128, icq * 512)
                        for m in range(4):
                            em.op(TE, lambda e, hc=hc, m=m, b=b, st=(hc == 0): e.matmul(
                                psum[m][:, 0:TC], wp[b][:, m * 128:(m + 1) * 128],
                                xf[hc][:, 0:TC], start=st, stop=(hc == HC - 1)),
                                r=[("wp", b), ("xf", hc)], w=[("psum", m)], tag="mm_g")
                    for m in (0, 1):
                        em.op(S, lambda e, m=m: e.activation(out=sg4[m][:], in_=psum[m][:, 0:TC],
                                                             func=AF.Silu),
                              r=[("psum", m)], w=[("sg", m)], tag="silu")
                    ub = [4, 5, 0, 1]
                    for hc in range(HC):
                        b = 4 + hc % 4
                        load_piece(b, w_in["w_up"], ("win", "wu"), hc * 128, icq * 512)
                        for m in range(4):
                            em.op(TE, lambda e, hc=hc, m=m, b=b, st=(hc == 0): e.matmul(
                                psum[ub[m]][:, 0:TC], wp[b][:, m * 128:(m + 1) * 128],
                                xf[hc][:, 0:TC], start=st, stop=(hc == HC - 1)),
                                r=[("wp", b), ("xf", hc)], w=[("psum", ub[m])], tag="mm_u")
                    hts = (tA, tB, tCf, tD_)
                    ats = (tE, tF, tG, tH)
                    tkeys = [("tA",), ("tB",), ("tCf",), ("tD_",)]
                    akeys = [("tE",), ("tF",), ("tG",), ("tH",)]
                    for m in range(4):
                        if m == 2:
                            em.op(S, lambda e: e.activation(out=sg4[0][:], in_=psum[2][:, 0:TC],
                                                            func=AF.Silu),
                                  r=[("psum", 2)], w=[("sg", 0)], tag="silu")
                            em.op(S, lambda e: e.activation(out=sg4[1][:], in_=psum[3][:, 0:TC],
                                                            func=AF.Silu),
                                  r=[("psum", 3)], w=[("sg", 1)], tag="silu")
                        em.op(V, lambda e, m=m: e.tensor_tensor(out=hts[m][:], in0=sg4[m % 2][:],
                                                                in1=psum[ub[m]][:, 0:TC],
                                                                op=OP.mult),
                              r=[("sg", m % 2), ("psum", ub[m])], w=[tkeys[m]], tag="h")
                        em.op(S, lambda e, m=m: e.activation(out=ats[m][:], in_=hts[m][:],
                                                             func=AF.Abs),
                              r=[tkeys[m]], w=[akeys[m]], tag="abs")
                    em.op(V, lambda e: e.tensor_tensor(out=wb_t[:], in0=ats[0][:],
                                                       in1=ats[1][:], op=OP.max),
                          r=[akeys[0], akeys[1]], w=[("wb_t",)])
                    em.op(V, lambda e: e.tensor_tensor(out=scN[:], in0=ats[2][:],
                                                       in1=ats[3][:], op=OP.max),
                          r=[akeys[2], akeys[3]], w=[("scN",)])
                    em.op(V, lambda e: e.tensor_tensor(out=t2q[:], in0=wb_t[:],
                                                       in1=scN[:], op=OP.min),
                          r=[("wb_t",), ("scN",)], w=[("t2q",)])
                    em.op(V, lambda e: e.tensor_tensor(out=wb_t[:], in0=ats[0][:],
                                                       in1=ats[1][:], op=OP.min),
                          r=[akeys[0], akeys[1]], w=[("wb_t",)])
                    em.op(V, lambda e: e.tensor_tensor(out=scN[:], in0=ats[2][:],
                                                       in1=ats[3][:], op=OP.min),
                          r=[akeys[2], akeys[3]], w=[("scN",)])
                    em.op(V, lambda e: e.tensor_tensor(out=scD[:], in0=wb_t[:],
                                                       in1=scN[:], op=OP.max),
                          r=[("wb_t",), ("scN",)], w=[("scD",)])
                    em.op(V, lambda e: e.tensor_tensor(out=wb_t[:], in0=t2q[:],
                                                       in1=scD[:], op=OP.max),
                          r=[("t2q",), ("scD",)], w=[("wb_t",)], tag="t2")
                    for m in range(4):
                        it = icq * 4 + m
                        em.op(V, lambda e, m=m: e.tensor_tensor(out=ats[m][:], in0=ats[m][:],
                                                                in1=wb_t[:], op=OP.is_ge),
                              r=[akeys[m], ("wb_t",)], w=[akeys[m]], tag="mask")
                        em.op(V, lambda e, m=m, it=it: e.tensor_tensor(
                            out=hsp[it][:], in0=ats[m][:], in1=hts[m][:], op=OP.mult),
                            r=[akeys[m], tkeys[m]], w=[("hsp", it)], tag="hsp")
                em.op(S, lambda e: preload_set(e, "natural_log_exp_and_others"), tag="tbl_b2")
                for oc in range(HC):
                    for icq in range(IC // 4):
                        b = icq % 4
                        a = wbf["w_down"][icq * 512:(icq + 1) * 512,
                                          oc * 128:(oc + 1) * 128].rearrange(
                            "(kc p) n -> p kc n", p=128)
                        rkeys = [("wbf", "wd", icq * 4 + d, (oc * 128) // 1024)
                                 for d in range(4)]
                        em.op(SY, lambda e, a=a, b=b: e.dma_start(
                            out=wp[b][:].rearrange("p (kc n) -> p kc n", n=128), in_=a),
                            r=rkeys, w=[("wp", b)], dma=True, tag="wd_ld")
                        for k4 in range(4):
                            ic = icq * 4 + k4
                            em.op(TE, lambda e, b=b, k4=k4, ic=ic, st=(ic == 0): e.matmul(
                                psum[2][:, 0:TC], wp[b][:, k4 * 128:(k4 + 1) * 128],
                                hsp[ic][:, 0:TC], start=st, stop=(ic == IC - 1)),
                                r=[("wp", b), ("hsp", ic)], w=[("psum", 2)], tag="mm_d")
                    em.op(S, lambda e, oc=oc: e.copy(out=ybf[oc][:], in_=psum[2][:, 0:TC]),
                          r=[("psum", 2)], w=[("xf", oc)], tag="yffn")
                ln_out_add(ybf, [("xf", c) for c in range(HC)], hsp,
                           [("hsp", c) for c in range(HC)], tj, 0, TC)
        em.barrier("l0f")

    def l1_attn():
        with contextlib.ExitStack() as pes:
            psb = lambda n, sh, dt=BF16: pes.enter_context(nc.sbuf_tensor(n, sh, dt))
            stb = [psb(f"stb{r}", [128, T]) for r in range(RC)]
            acb = [psb(f"acb{r}", [128, T]) for r in range(RC)]
            xtok = [psb(f"xt{j}", [128, H]) for j in range(JB)]
            em.op(S, lambda e: preload_set(e, "natural_log_exp_and_others"), tag="tblA")
            aes = contextlib.ExitStack()
            xf = [aes.enter_context(nc.sbuf_tensor(f"xa{c}", [128, TC], BF16))
                  for c in range(HC)]
            for tj in range(TN):
                tsl = slice(tj * TC, (tj + 1) * TC)
                src, skeys = res_src(tj)
                ln_rows(src, skeys, xf, [("xfa", c) for c in range(HC)], H, TC, C_ELN)
                for c in range(HC):
                    em.op(V, lambda e, c=c, tsl=tsl: e.tensor_tensor(
                        out=tA[:], in0=res[c][:, tsl], in1=mu_b[:], op=OP.subtract),
                        r=[("res", c), ("mu_b",)], w=[("tA",)])
                    em.op(V, lambda e, c=c: e.tensor_tensor(out=xf[c][:], in0=tA[:],
                                                            in1=rr_b[:], op=OP.mult),
                          r=[("tA",), ("rr_b",)], w=[("xfa", c)], tag="xl3")
                for rc in range(RC):
                    gsl = slice(tj * TC, (tj + 1) * TC)
                    for hc in range(HC):
                        b = hc % 4
                        load_piece(b, w_in["ws"], ("win", "ws"), hc * 128, rc * 128,
                                   ncols=128)
                        em.op(TE, lambda e, hc=hc, b=b, st=(hc == 0): e.matmul(
                            psum[0][:, 0:TC], wp[b][:, 0:128],
                            xf[hc][:, 0:TC], start=st, stop=(hc == HC - 1)),
                            r=[("wp", b), ("xfa", hc)], w=[("psum", 0)], tag="mm_s")
                    em.op(S, lambda e, gsl=gsl, rc=rc: e.copy(out=stb[rc][:, gsl],
                                                              in_=psum[0][:, 0:TC]),
                          r=[("psum", 0)], w=[("stb", rc)])
                    for hc in range(HC):
                        b = 4 + hc % 4
                        load_piece(b, w_in["wa"], ("win", "wa"), hc * 128, rc * 128,
                                   ncols=128)
                        em.op(TE, lambda e, hc=hc, b=b, st=(hc == 0): e.matmul(
                            psum[1][:, 0:TC], wp[b][:, 0:128],
                            xf[hc][:, 0:TC], start=st, stop=(hc == HC - 1)),
                            r=[("wp", b), ("xfa", hc)], w=[("psum", 1)], tag="mm_a")
                    em.op(S, lambda e, gsl=gsl, rc=rc: e.copy(out=acb[rc][:, gsl],
                                                              in_=psum[1][:, 0:TC]),
                          r=[("psum", 1)], w=[("acb", rc)])
                for c in range(HC):
                    for q in range(TC // 128):
                        jb = (tj * TC) // 128 + q
                        em.op(SY, lambda e, c=c, q=q, jb=jb: e.dma_start(
                            out=xtok[jb][:, c * 128:(c + 1) * 128],
                            in_=xf[c][:, q * 128:(q + 1) * 128], transpose=True),
                            r=[("xfa", c)], w=[("xtok", jb)], dma=True, tag="xtok")
            L1B = cfg.get("l1b_upto", 5)
            aes.close()
            em.barrier("l1ab")
            Ep = [psb(f"Ep{j}", [128, TCB]) for j in range(JB)]
            anb = [psb(f"anb{c}", [128, TCB]) for c in range(HC)]
            ybf = [psb(f"yba{c}", [128, TCB]) for c in range(HC)]
            xsq = Ep
            for iq in range(0 if cfg.get("l1a_only_A") else JB):
                isl = slice(iq * 128, (iq + 1) * 128)
                for jb in range(iq + 1):
                    for rc in range(RC):
                        em.op(TE, lambda e, jb=jb, rc=rc, isl=isl, st=(rc == 0): e.matmul(
                            psum[0][:, 0:TCB], acb[rc][:, jb * 128:(jb + 1) * 128],
                            stb[rc][:, isl], start=st, stop=(rc == RC - 1)),
                            r=[("acb", rc), ("stb", rc)], w=[("psum", 0)], tag="mm_sc")
                    if jb == iq:
                        em.op(V, lambda e: e.tensor_tensor(out=tA[:, 0:TCB],
                                                           in0=psum[0][:, 0:TCB],
                                                           in1=amask[:], op=OP.add),
                              r=[("psum", 0), ("amask",)], w=[("tA",)], tag="mask")
                        em.op(S, lambda e, jb=jb: e.activation(out=Ep[jb][:], in_=tA[:, 0:TCB],
                                                               func=AF.Exp,
                                                               scale=cst[:, C_I16:C_I16 + 1]),
                              r=[("tA",), ("cst",)], w=[("Ep", jb)], tag="expS")
                    else:
                        em.op(S, lambda e, jb=jb: e.activation(out=Ep[jb][:],
                                                               in_=psum[0][:, 0:TCB],
                                                               func=AF.Exp,
                                                               scale=cst[:, C_I16:C_I16 + 1]),
                              r=[("psum", 0), ("cst",)], w=[("Ep", jb)], tag="expS")
                if L1B < 2:
                    continue
                for jb in range(iq + 1):
                    em.op(TE, lambda e, jb=jb, iq=iq, st=(jb == 0): e.matmul(
                        prow[0:1, 0:TCB], ones_bf[:], Ep[jb][:],
                        start=st, stop=(jb == iq)),
                        r=[("ones_bf",), ("Ep", jb)], w=[("prow",)], tag="rs")
                em.op(V, lambda e: e.tensor_copy(out=rw["s1"][:, 0:TCB], in_=prow[0:1, 0:TCB]),
                      r=[("prow",)], w=[("rw", "s1")])
                em.op(V, lambda e: e.reciprocal(out=rw["tmp"][:, 0:TCB],
                                                in_=rw["s1"][:, 0:TCB]),
                      r=[("rw", "s1")], w=[("rw", "tmp")], tag="rcp")
                bcast_row(rw["tmp"][:], ("rw", "tmp"), TCB, rc_b, ("rc_b",), psum[4], 4)
                if L1B < 3:
                    continue
                for hc in range(HC):
                    for jb in range(iq + 1):
                        em.op(TE, lambda e, jb=jb, hc=hc, iq=iq, st=(jb == 0): e.matmul(
                            psum[1][:, 0:TCB], xtok[jb][:, hc * 128:(hc + 1) * 128],
                            Ep[jb][:], start=st, stop=(jb == iq)),
                            r=[("xtok", jb), ("Ep", jb)], w=[("psum", 1)], tag="mm_av")
                    em.op(V, lambda e, hc=hc: e.tensor_tensor(out=anb[hc][:],
                                                              in0=psum[1][:, 0:TCB],
                                                              in1=rc_b[:, 0:TCB], op=OP.mult),
                          r=[("psum", 1), ("rc_b",)], w=[("anb", hc)], tag="attn")
                if L1B < 4:
                    continue
                nm = min(4, HC)
                for cq in range(max(1, HC // nm)):
                    for hc in range(HC):
                        b = 4 + hc % 4
                        load_piece(b, w_in["woT1"], ("win", "woT1"), hc * 128,
                                   cq * nm * 128, ncols=nm * 128)
                        for m in range(nm):
                            em.op(TE, lambda e, hc=hc, m=m, b=b, st=(hc == 0): e.matmul(
                                psum[m][:, 0:TCB], wp[b][:, m * 128:(m + 1) * 128],
                                anb[hc][:], start=st, stop=(hc == HC - 1)),
                                r=[("wp", b), ("anb", hc)], w=[("psum", m)], tag="mm_o1")
                    for m in range(nm):
                        oc = cq * nm + m
                        em.op(V, lambda e, m=m, oc=oc: e.tensor_scalar(
                            out=ybf[oc][:], in0=psum[m][:, 0:TCB],
                            scalar1=pc["bo"][:, oc:oc + 1], scalar2=None, op0=OP.add),
                            r=[("psum", m), ("pc", "bo")], w=[("yba", oc)], tag="y_att")
                if L1B < 5:
                    continue
                tj, off = (iq * 128) // TC, (iq * 128) % TC
                ln_out_add(ybf, [("yba", c) for c in range(HC)], xsq,
                           [("Ep", c) for c in range(HC)], tj, off, TCB)
        em.barrier("l1a")

    def l1_ffn():
        with contextlib.ExitStack() as pes:
            psb = lambda n, sh, dt=BF16: pes.enter_context(nc.sbuf_tensor(n, sh, dt))
            xf = [psb(f"x5{c}", [128, TC]) for c in range(HC)]
            fbufs = [pes.enter_context(nc.sbuf_tensor(f"fb{i}", [128, TC], F32))
                     for i in range(2)]
            hbf = [psb(f"hb{i}", [128, TC]) for i in range(IC)]
            ybf = xf
            xsq = [psb(f"q5{c}", [128, TC]) for c in range(HC)]
            for tj in range(TN):
                tsl = slice(tj * TC, (tj + 1) * TC)
                src, skeys = res_src(tj)
                ln_rows(src, skeys, xsq, [("xsq5", c) for c in range(HC)], H, TC, C_ELN)
                for c in range(HC):
                    em.op(V, lambda e, c=c, tsl=tsl: e.tensor_tensor(
                        out=tA[:], in0=res[c][:, tsl], in1=mu_b[:], op=OP.subtract),
                        r=[("res", c), ("mu_b",)], w=[("tA",)])
                    em.op(V, lambda e, c=c: e.tensor_tensor(out=xf[c][:], in0=tA[:],
                                                            in1=rr_b[:], op=OP.mult),
                          r=[("tA",), ("rr_b",)], w=[("x5", c)], tag="x5")
                for icq in range(IC // 4):
                    for hc in range(HC):
                        b = hc % 4
                        em.op(SY, lambda e, icq=icq, hc=hc, b=b: e.dma_start(
                            out=wp[b][:], in_=qfc[icq * 512:(icq + 1) * 512,
                                                 hc * 128:(hc + 1) * 128], transpose=True),
                            r=[("qfc",)], w=[("wp", b)], dma=True, tag="qfc_tr")
                        for m in range(4):
                            em.op(TE, lambda e, hc=hc, m=m, b=b, st=(hc == 0): e.matmul(
                                psum[m][:, 0:TC], wp[b][:, m * 128:(m + 1) * 128],
                                xf[hc][:, 0:TC], start=st, stop=(hc == HC - 1)),
                                r=[("wp", b), ("x5", hc)], w=[("psum", m)], tag="mm_fc")
                    for m in range(4):
                        ic = icq * 4 + m
                        em.op(S, lambda e, m=m: e.activation(out=tA[:], in_=psum[m][:, 0:TC],
                                                             func=AF.Relu),
                              r=[("psum", m)], w=[("tA",)], tag="relu")
                        em.op(V, lambda e, ic=ic: e.tensor_tensor(out=hbf[ic][:], in0=tA[:],
                                                                  in1=tA[:], op=OP.mult),
                              r=[("tA",)], w=[("hb", ic)], tag="hsq")
                for t0 in range(0, TC, 512):
                    n = min(512, TC - t0)
                    for ic in range(IC):
                        em.op(S, lambda e, ic=ic, t0=t0, n=n: e.activation(
                            out=xsq[ic % HC][:, t0:t0 + n], in_=hbf[ic][:, t0:t0 + n],
                            func=AF.Square),
                            r=[("hb", ic)], w=[("xsq5", ic % HC)], tag="h2")
                        em.op(TE, lambda e, ic=ic, t0=t0, n=n, st=(ic == 0): e.matmul(
                            prow[0:1, 0:n], ones_bf[:], xsq[ic % HC][:, t0:t0 + n],
                            start=st, stop=(ic == IC - 1)),
                            r=[("ones_bf",), ("xsq5", ic % HC)], w=[("prow",)], tag="rsum")
                    em.op(V, lambda e, t0=t0, n=n: e.tensor_copy(out=rw["s1"][:, t0:t0 + n],
                                                                 in_=prow[0:1, 0:n]),
                          r=[("prow",)], w=[("rw", "s1")])
                em.op(V, lambda e: e.tensor_scalar(out=rw["var"][:], in0=rw["s1"][:],
                                                   scalar1=1.0 / I, scalar2=None,
                                                   op0=OP.mult),
                      r=[("rw", "s1")], w=[("rw", "var")])
                em.op(S, lambda e: e.activation(out=rw["l0"][:], in_=rw["var"][:], func=AF.Ln,
                                                bias=cst[0:1, C_ERMS:C_ERMS + 1]),
                      r=[("rw", "var"), ("cst",)], w=[("rw", "l0")])
                em.op(S, lambda e: e.activation(out=rw["rr"][:], in_=rw["l0"][:], func=AF.Exp,
                                                scale=cst[0:1, C_NHALF:C_NHALF + 1]),
                      r=[("rw", "l0"), ("cst",)], w=[("rw", "rr")], tag="rms")
                bcast_row(rw["rr"][:], ("rw", "rr"), TC, rr_b, ("rr_b",), psum[5], 5)
                for ocq in range(max(1, HC // 4)):
                    nm = min(4, HC)
                    for icc in range(IC):
                        b = 4 + icc % 4
                        em.op(SY, lambda e, ocq=ocq, icc=icc, b=b, nm=nm: e.dma_start(
                            out=wp[b][0:128, 0:nm * 128],
                            in_=qpr[ocq * nm * 128:(ocq + 1) * nm * 128,
                                    icc * 128:(icc + 1) * 128], transpose=True),
                            r=[("qpr",)], w=[("wp", b)], dma=True, tag="qpr_tr")
                        for m in range(nm):
                            em.op(TE, lambda e, m=m, b=b, icc=icc, st=(icc == 0): e.matmul(
                                psum[m][:, 0:TC], wp[b][:, m * 128:(m + 1) * 128],
                                hbf[icc][:, 0:TC], start=st, stop=(icc == IC - 1)),
                                r=[("wp", b), ("hb", icc)], w=[("psum", m)], tag="mm_pr")
                    for m in range(nm):
                        oc = ocq * nm + m
                        em.op(V, lambda e, m=m, oc=oc: e.tensor_tensor(
                            out=ybf[oc][:], in0=psum[m][:, 0:TC], in1=rr_b[:, 0:TC],
                            op=OP.mult),
                            r=[("psum", m), ("rr_b",)], w=[("x5", oc)], tag="y5")
                ln_out_add(ybf, [("x5", c) for c in range(HC)], xsq,
                           [("xsq5", c) for c in range(HC)], tj, 0, TC, final=True,
                           fin_bufs=fbufs)
        em.barrier("l1f")

    phases = cfg.get("phases", "qaf1AF")
    if "q" in phases:
        with contextlib.ExitStack() as qscope:
            qpump, _nq = make_quant(qscope, "fcpr", "Q")
            qpump(10 ** 9)
        em.barrier("q")
    if "a" in phases:
        l0_attn()
    if "f" in phases:
        l0_ffn()
    if "A" in phases:
        l1_attn()
    if "F" in phases:
        l1_ffn()

    with nc.Block() as block:
        em.finalize(block, sems)
    es.close()
    return nc


def make_wmap(inputs, I):
    jj = np.arange(I)
    g = jj // 4
    m = jj % 4
    newpos = (g // 128) * 512 + m * 128 + (g % 128)
    perm = np.empty(I, dtype=np.int64)
    perm[newpos] = jj

    def C(a):
        return np.ascontiguousarray(a, dtype=np.float32)

    return {
        "wkT": C(inputs["L0_attn_Wk"].T), "wvT": C(inputs["L0_attn_Wv"].T),
        "wrT": C(inputs["L0_attn_Wr"].T), "woT": C(inputs["L0_attn_Wo"].T),
        "ln_w": C(inputs["L0_attn_ln_w"]), "ln_b": C(inputs["L0_attn_ln_b"]),
        "mix_k": C(inputs["L0_attn_mix_k"]), "mix_v": C(inputs["L0_attn_mix_v"]),
        "mix_r": C(inputs["L0_attn_mix_r"]), "td": C(inputs["L0_attn_time_decay"]),
        "cw": C(inputs["L0_attn_current_token_weight"]),
        "w_gate": C(inputs["L0_ffn_w_gate"][:, perm]),
        "w_up": C(inputs["L0_ffn_w_up"][:, perm]),
        "w_down": C(inputs["L0_ffn_w_down"][perm, :]),
        "ws": C(inputs["L1_attn_Ws"]), "wa": C(inputs["L1_attn_Wa"]),
        "woT1": C(inputs["L1_attn_Wo"].T), "bo": C(inputs["L1_attn_bo"]),
        "w_fc": C(inputs["L1_ffn_W_fc"]), "w_proj": C(inputs["L1_ffn_W_proj"]),
        "tri": np.where(np.arange(128)[None, :] < np.arange(128)[:, None],
                        np.float32(NEG_BIG), np.float32(0.0)),
    }


def kernel(**inputs):
    cfg = full_cfg()
    H, T, I = cfg["H"], cfg["T"], cfg["I"]
    B = inputs["L0_attn_hidden_state"].shape[0]
    x = np.ascontiguousarray(inputs["L0_attn_hidden_state"])
    wmap = make_wmap(inputs, I)
    nc = build_nc(cfg)
    in_maps = []
    for b in range(B):
        mm = dict(wmap)
        mm["x_in"] = np.ascontiguousarray(x[b].T, dtype=np.float32)
        in_maps.append(mm)
    res_ = run_bass_kernel_spmd(nc, in_maps, core_ids=list(range(B)))
    out = np.empty((B, T, H), dtype=np.float32)
    for b in range(B):
        out[b] = res_.results[b]["y_out"].T
    return out


# version 32
# speedup vs baseline: 1.2736x; 1.1279x over previous
import contextlib
import numpy as np

import concourse.bass as bass
import concourse.mybir as mybir
from concourse.bass_utils import run_bass_kernel_spmd

F32 = mybir.dt.float32
BF16 = mybir.dt.bfloat16
AF = mybir.ActivationFunctionType
OP = mybir.AluOpType
AX = mybir.AxisListType

C_T = float(np.float32(np.float32(0.5009765625) * np.float32(1.0 + 2.0 ** -20)))
V45C = float(2.0 ** 45)
V29C = float(2.0 ** 29)
K0 = 1.000244140625
NEG_BIG = -1e38
LN_EPS = 1e-5
RMS_EPS = 1.1920929e-7
ENGINES = ("tensor", "vector", "scalar", "gpsimd", "sync")


def full_cfg():
    return dict(H=2048, T=2048, I=8192, R=256, TC=512)


class Emitter:

    KQ = 8

    def __init__(self):
        self.ops = []
        self.last_writer = {}
        self.readers = {}
        self.qcount = {}
        self.qlast = {}

    def op(self, eng, fn, r=(), w=(), dma=False, tag="", extra_deps=()):
        idx = len(self.ops)
        deps = set(extra_deps)
        same = False

        def classify(j):
            nonlocal same
            rec = self.ops[j]
            if rec["eng"] == eng and rec["dma"] == dma and not dma:
                same = True
            else:
                deps.add(j)

        for reg in r:
            for wj in self.last_writer.get(reg, ()):
                classify(wj)
        for reg in w:
            for wj in self.last_writer.get(reg, ()):
                classify(wj)
            for rd in self.readers.get(reg, ()):
                classify(rd)
        self.ops.append(dict(eng=eng, fn=fn, dma=dma, deps=deps, tag=tag,
                             need_inc=False, drain_before=same))
        if dma:
            qi = self.qcount.get(eng, 0)
            self.qcount[eng] = qi + 1
            self.qlast[(eng, qi % self.KQ)] = idx
        for reg in r:
            self.readers.setdefault(reg, []).append(idx)
        for reg in w:
            if dma and self.last_writer.get(reg):
                self.last_writer[reg].append(idx)
            else:
                self.last_writer[reg] = [idx]
            self.readers[reg] = []
        return idx

    def barrier(self, tag="bar"):
        last = {}
        for i, rec in enumerate(self.ops):
            if not rec["dma"]:
                last[rec["eng"]] = i
        deps = tuple(last.values()) + tuple(self.qlast.values())
        for e in ENGINES:
            self.op(e, lambda en: en.nop(), extra_deps=deps, tag=tag + "_" + e)
        self.last_writer = {}
        self.readers = {}

    def finalize(self, block, sems):
        for rec in self.ops:
            for d in rec["deps"]:
                self.ops[d]["need_inc"] = True
        for rec in self.ops:
            if rec["dma"]:
                rec["need_inc"] = True
        counts = {}
        qidx = {}
        for rec in self.ops:
            if rec["dma"]:
                q = rec["eng"]
                i = qidx.get(q, 0)
                qidx[q] = i + 1
                sub = i % self.KQ
                key = (q, True, sub)
                counts[key] = counts.get(key, 0) + 16
                rec["sem_key"] = key
                rec["sem_val"] = counts[key]
            elif rec["need_inc"]:
                key = (rec["eng"], False)
                counts[key] = counts.get(key, 0) + 1
                rec["sem_key"] = key
                rec["sem_val"] = counts[key]
            else:
                rec["sem_key"] = rec["sem_val"] = None
        for rec in self.ops:
            waits = {}
            for d in rec["deps"]:
                dep = self.ops[d]
                waits[dep["sem_key"]] = max(waits.get(dep["sem_key"], 0),
                                            dep["sem_val"])
            if not rec["dma"]:
                waits.pop((rec["eng"], False), None)
            rec["waits"] = waits
        self.final_counts = counts

        def run(eng_name, e):
            seen = {}
            for rec in self.ops:
                if rec["eng"] != eng_name:
                    continue
                for key in sorted(rec["waits"], key=str):
                    v = rec["waits"][key]
                    if seen.get(key, -1) < v:
                        e.wait_ge(sems[key], v)
                        seen[key] = v
                if rec["dma"] and rec["sem_val"] > 16:
                    k2, v2 = rec["sem_key"], rec["sem_val"] - 16
                    if seen.get(k2, -1) < v2:
                        e.wait_ge(sems[k2], v2)
                        seen[k2] = v2
                if rec["drain_before"] and eng_name in ("vector", "scalar", "gpsimd"):
                    e.drain()
                inst = rec["fn"](e)
                if rec["sem_key"] is not None:
                    if inst is None:
                        inst = e.nop()
                    inst.then_inc(sems[rec["sem_key"]], 16 if rec["dma"] else 1)

        @block.tensor
        def _(e):
            run("tensor", e)

        @block.vector
        def _(e):
            run("vector", e)

        @block.scalar
        def _(e):
            run("scalar", e)

        @block.gpsimd
        def _(e):
            run("gpsimd", e)

        @block.sync
        def _(e):
            run("sync", e)
            for key in sorted(self.final_counts, key=str):
                if key[1] is True:
                    e.wait_ge(sems[key], self.final_counts[key])


def build_nc(cfg):
    H, T, I, R, TC = cfg["H"], cfg["T"], cfg["I"], cfg["R"], cfg["TC"]
    HC = H // 128
    TN = T // TC
    IC = I // 128
    RC = R // 128
    JB = T // 128
    TCB = 128
    GRP = 64
    QW = 1024 if H >= 1024 else H
    assert TC <= 512

    nc = bass.Bass("TRN2", target_bir_lowering=False)
    x_in = nc.dram_tensor("x_in", [H, T], F32, kind="ExternalInput").ap()
    y_out = nc.dram_tensor("y_out", [H, T], F32, kind="ExternalOutput").ap()
    w_in = {}
    for name, shape in [
        ("wkT", [H, H]), ("wvT", [H, H]), ("wrT", [H, H]), ("woT", [H, H]),
        ("ln_w", [H]), ("ln_b", [H]), ("mix_k", [H]), ("mix_v", [H]),
        ("mix_r", [H]), ("td", [H]), ("cw", [H]),
        ("w_gate", [H, I]), ("w_up", [H, I]), ("w_down", [I, H]),
        ("ws", [H, R]), ("wa", [H, R]), ("woT1", [H, H]), ("bo", [H]),
        ("w_fc", [I, H]), ("w_proj", [H, I]),
    ]:
        w_in[name] = nc.dram_tensor(name, shape, F32, kind="ExternalInput").ap()
    tri_in = nc.dram_tensor("tri", [128, 128], F32, kind="ExternalInput").ap()
    qfc = nc.dram_tensor("qfc", [I, H], BF16, kind="Internal").ap()
    qpr = nc.dram_tensor("qpr", [H, I], BF16, kind="Internal").ap()
    PRECONV = ["wkT", "wvT", "wrT", "woT", "w_gate", "w_up", "w_down", "woT1",
               "ws", "wa"]
    wbf = {n: nc.dram_tensor("bf_" + n, list(w_in[n].shape), BF16,
                             kind="Internal").ap() for n in PRECONV}

    em = Emitter()
    es = contextlib.ExitStack()

    def sb(name, shape, dtype=F32):
        return es.enter_context(nc.sbuf_tensor(name, shape, dtype))

    res = [sb(f"res{c}", [128, T], BF16) for c in range(HC)]
    pc = {n: sb("pc_" + n, [128, HC]) for n in
          ["ln_w", "ln_b", "mix_k", "mix_v", "mix_r", "cw", "wdec", "bo", "ecw"]}
    ones_bf = sb("ones_bf", [128, 1], BF16)
    ones1_f = sb("ones1_f", [1, 128], F32)
    cst = sb("cst", [128, 8])
    C_HALF, C_NHALF, C_ELN, C_ERMS, C_N1, C_S1, C_S2, C_I16 = range(8)
    dcol = sb("dcol", [128, 2])
    stN = sb("stN", [128, HC])
    stD = sb("stD", [128, HC])
    xprev = sb("xprev", [128, HC])
    rw = {n: sb("rw_" + n, [1, TC]) for n in
          ["s1", "s2", "mu", "var", "l0", "rr", "tmp"]}
    rw["r0"] = rw["s1"]
    rw["vy"] = rw["tmp"]
    rw["l1"] = rw["s2"]
    mu_b = sb("mu_b", [128, TC], BF16)
    rr_b = sb("rr_b", [128, TC])
    rc_b = sb("rc_b", [128, TC], BF16)
    amask = sb("amask", [128, 128], BF16)
    tA = sb("tA", [128, TC])
    tB = sb("tB", [128, TC])
    wp = [sb(f"wp{i}", [128, 512], BF16) for i in range(8)]
    cv = [None, None]
    tC = [None]

    psum = [es.enter_context(nc.psum_tensor(f"ps{i}", [128, 512], F32))
            for i in range(6)]
    prow = es.enter_context(nc.psum_tensor("prow", [1, 512], F32))
    prow2 = es.enter_context(nc.psum_tensor("prow2", [1, 512], F32))

    sems = {}
    for eng in ENGINES:
        sems[(eng, False)] = es.enter_context(nc.semaphore("s_" + eng))
    for q in ("sync", "gpsimd"):
        for sub in range(8):
            sems[(q, True, sub)] = es.enter_context(nc.semaphore(f"sd_{q}{sub}"))

    V, S, G, TE, SY = "vector", "scalar", "gpsimd", "tensor", "sync"

    from concourse.hw_specs import get_activation_tables
    _tab_names = list(get_activation_tables(nc.m.arch).keys())

    def preload_set(e, name):
        idx = _tab_names.index(name)
        return e.add_instruction(mybir.InstLoadActFuncSet(
            name=e.bass.get_next_instruction_name(), act_func_set_id=idx,
            ins=[], outs=[]))

    for c in range(HC):
        em.op(G, lambda e, c=c: e.dma_start(out=res[c][:], in_=x_in[c * 128:(c + 1) * 128, :]),
              r=[("x_in",)], w=[("res", c)], dma=True, tag="ld_x")
    for name in ["ln_w", "ln_b", "mix_k", "mix_v", "mix_r", "cw", "bo"]:
        ap = w_in[name].rearrange("(c p) -> p c", p=128)
        em.op(SY, lambda e, n=name, a=ap: e.dma_start(out=pc[n][:], in_=a, allow_slow_non_contiguous=True),
              r=[("win", name)], w=[("pc", name)], dma=True)
    em.op(SY, lambda e: e.dma_start(out=pc["wdec"][:],
                                    in_=w_in["td"].rearrange("(c p) -> p c", p=128),
                                    allow_slow_non_contiguous=True),
          r=[("win", "td")], w=[("pc", "wdec")], dma=True)
    for i, v in [(C_HALF, 0.5), (C_NHALF, -0.5), (C_ELN, LN_EPS),
                 (C_ERMS, RMS_EPS), (C_N1, -1.0), (C_I16, 1.0 / 16.0)]:
        em.op(V, lambda e, i=i, v=v: e.memset(cst[:, i:i + 1], v), w=[("cst",)])
    em.op(V, lambda e: e.memset(ones_bf[:], 1.0), w=[("ones_bf",)])
    em.op(V, lambda e: e.memset(ones1_f[:], 1.0), w=[("ones1",)])
    em.op(V, lambda e: e.memset(stN[:], 0.0), w=[("stN",)])
    em.op(V, lambda e: e.memset(stD[:], 0.0), w=[("stD",)])
    em.op(V, lambda e: e.memset(xprev[:], 0.0), w=[("xprev",)])
    em.op(V, lambda e: e.memset(dcol[:, 0:1], K0), w=[("dcol",)])
    em.op(V, lambda e: e.tensor_scalar(out=cst[:, C_S1:C_S1 + 1], in0=dcol[:, 0:1],
                                       scalar1=V45C, scalar2=None, op0=OP.mult),
          r=[("dcol",)], w=[("cst",)])
    em.op(V, lambda e: e.tensor_tensor(out=cst[:, C_S2:C_S2 + 1], in0=cst[:, C_S1:C_S1 + 1],
                                       in1=dcol[:, 0:1], op=OP.add),
          r=[("dcol",), ("cst",)], w=[("cst",)])
    em.op(V, lambda e: e.tensor_tensor(out=cst[:, C_S1:C_S1 + 1], in0=cst[:, C_S2:C_S2 + 1],
                                       in1=dcol[:, 0:1], op=OP.subtract),
          r=[("dcol",), ("cst",)], w=[("cst",)])
    em.op(V, lambda e: e.tensor_tensor(out=dcol[:, 1:2], in0=cst[:, C_S2:C_S2 + 1],
                                       in1=cst[:, C_S1:C_S1 + 1], op=OP.subtract),
          r=[("cst",)], w=[("dcol",)])
    em.op(V, lambda e: e.tensor_scalar(out=dcol[:, 0:1], in0=dcol[:, 1:2], scalar1=-1.0,
                                       scalar2=1.0, op0=OP.mult, op1=OP.add),
          r=[("dcol",)], w=[("dcol",)], tag="D")
    em.op(S, lambda e: preload_set(e, "natural_log_exp_and_others"))
    em.op(S, lambda e: e.activation(out=pc["wdec"][:], in_=pc["wdec"][:], func=AF.Exp),
          r=[("pc", "wdec")], w=[("pc", "wdec")])
    em.op(S, lambda e: e.activation(out=pc["wdec"][:], in_=pc["wdec"][:], func=AF.Exp,
                                    scale=cst[:, C_N1:C_N1 + 1]),
          r=[("pc", "wdec"), ("cst",)], w=[("pc", "wdec")], tag="wdec")
    em.op(S, lambda e: e.activation(out=pc["ecw"][:], in_=pc["cw"][:], func=AF.Exp),
          r=[("pc", "cw")], w=[("pc", "ecw")], tag="ecw")
    em.op(G, lambda e: e.dma_start(out=amask[:], in_=tri_in[:]),
          r=[("tri_in",)], w=[("amask",)], dma=True, tag="ld_tri")

    def load_piece(b, src, skey, row0, col0, ncols=512, nrows=128):
        name = skey[1]
        bsrc = wbf[{"wg": "w_gate", "wu": "w_up", "wd": "w_down"}.get(name, name)]
        rkeys = [("wbf", name, (row0 + d) // 128, cp)
                 for d in range(0, nrows, 128)
                 for cp in range(col0 // 1024, (col0 + ncols - 1) // 1024 + 1)]
        em.op(SY, lambda e: e.dma_start(out=wp[b][0:nrows, 0:ncols],
                                        in_=bsrc[row0:row0 + nrows, col0:col0 + ncols]),
              r=rkeys, w=[("wp", b)], dma=True, tag="wld")

    def bcast_row(row, rkey, width, dst, dkey, bank, bidx):
        for t0 in range(0, width, 512):
            n = min(512, width - t0)
            em.op(TE, lambda e, t0=t0, n=n: e.matmul(bank[0:128, 0:n], ones1_f[:],
                                                     row[0:1, t0:t0 + n],
                                                     start=True, stop=True),
                  r=[rkey, ("ones1",)], w=[("psum", bidx)], tag="bc")
            em.op(S, lambda e, t0=t0, n=n: e.copy(out=dst[:, t0:t0 + n],
                                                  in_=bank[0:128, 0:n]),
                  r=[("psum", bidx)], w=[dkey], tag="bc_e")

    def ln_rows(src_of, src_keys, sq_tiles, sq_keys, n_feat, w, eps_idx,
                fused=False, rr0=False):
        nch = len(src_keys)
        for c in range(nch):
            em.op(S, lambda e, c=c: e.activation(out=sq_tiles[c][:, 0:w],
                                                 in_=src_of(c, 0, w), func=AF.Square),
                  r=[src_keys[c]], w=[sq_keys[c]], tag="sq")
        for t0 in range(0, w, 512):
            n = min(512, w - t0)
            for c in range(nch):
                em.op(TE, lambda e, c=c, t0=t0, n=n, st=(c == 0): e.matmul(
                    prow[0:1, 0:n], ones_bf[:], src_of(c, t0, t0 + n),
                    start=st, stop=(c == nch - 1)),
                    r=[("ones_bf",), src_keys[c]], w=[("prow",)], tag="sum1")
            for c in range(nch):
                em.op(TE, lambda e, c=c, t0=t0, n=n, st=(c == 0): e.matmul(
                    prow2[0:1, 0:n], ones_bf[:], sq_tiles[c][:, t0:t0 + n],
                    start=st, stop=(c == nch - 1)),
                    r=[("ones_bf",), sq_keys[c]], w=[("prow2",)], tag="sum2")
            em.op(V, lambda e, t0=t0, n=n: e.tensor_copy(out=rw["s1"][:, t0:t0 + n],
                                                         in_=prow[0:1, 0:n]),
                  r=[("prow",)], w=[("rw", "s1")])
            em.op(V, lambda e, t0=t0, n=n: e.tensor_copy(out=rw["s2"][:, t0:t0 + n],
                                                         in_=prow2[0:1, 0:n]),
                  r=[("prow2",)], w=[("rw", "s2")])
        inv = 1.0 / n_feat
        em.op(V, lambda e: e.tensor_scalar(out=rw["mu"][:, 0:w], in0=rw["s1"][:, 0:w],
                                           scalar1=inv, scalar2=None, op0=OP.mult),
              r=[("rw", "s1")], w=[("rw", "mu")])
        em.op(V, lambda e: e.tensor_tensor(out=rw["tmp"][:, 0:w], in0=rw["mu"][:, 0:w],
                                           in1=rw["mu"][:, 0:w], op=OP.mult),
              r=[("rw", "mu")], w=[("rw", "tmp")])
        em.op(V, lambda e: e.tensor_scalar(out=rw["var"][:, 0:w], in0=rw["s2"][:, 0:w],
                                           scalar1=inv, scalar2=None, op0=OP.mult),
              r=[("rw", "s2")], w=[("rw", "var")])
        em.op(V, lambda e: e.tensor_tensor(out=rw["var"][:, 0:w], in0=rw["var"][:, 0:w],
                                           in1=rw["tmp"][:, 0:w], op=OP.subtract),
              r=[("rw", "var"), ("rw", "tmp")], w=[("rw", "var")])
        em.op(S, lambda e: e.activation(out=rw["l0"][:, 0:w], in_=rw["var"][:, 0:w],
                                        func=AF.Ln, bias=cst[0:1, eps_idx:eps_idx + 1]),
              r=[("rw", "var"), ("cst",)], w=[("rw", "l0")])
        if fused:
            em.op(S, lambda e: e.activation(out=rw["r0"][:, 0:w], in_=rw["l0"][:, 0:w],
                                            func=AF.Exp, scale=cst[0:1, C_N1:C_N1 + 1]),
                  r=[("rw", "l0"), ("cst",), ("rw", "s1")], w=[("rw", "s1")])
            em.op(V, lambda e: e.tensor_tensor(out=rw["vy"][:, 0:w], in0=rw["var"][:, 0:w],
                                               in1=rw["r0"][:, 0:w], op=OP.mult),
                  r=[("rw", "var"), ("rw", "s1"), ("rw", "tmp")], w=[("rw", "tmp")])
            em.op(S, lambda e: e.activation(out=rw["l1"][:, 0:w], in_=rw["vy"][:, 0:w],
                                            func=AF.Ln, bias=cst[0:1, eps_idx:eps_idx + 1]),
                  r=[("rw", "tmp"), ("cst",)], w=[("rw", "s2")])
            em.op(V, lambda e: e.tensor_tensor(out=rw["tmp"][:, 0:w], in0=rw["l1"][:, 0:w],
                                               in1=rw["l0"][:, 0:w], op=OP.add),
                  r=[("rw", "s2"), ("rw", "l0")], w=[("rw", "tmp")])
            em.op(S, lambda e: e.activation(out=rw["rr"][:, 0:w], in_=rw["tmp"][:, 0:w],
                                            func=AF.Exp, scale=cst[0:1, C_NHALF:C_NHALF + 1]),
                  r=[("rw", "tmp"), ("cst",)], w=[("rw", "rr")])
        else:
            em.op(S, lambda e: e.activation(out=rw["rr"][:, 0:w], in_=rw["l0"][:, 0:w],
                                            func=AF.Exp, scale=cst[0:1, C_NHALF:C_NHALF + 1]),
                  r=[("rw", "l0"), ("cst",)], w=[("rw", "rr")])
        bcast_row(rw["mu"][:], ("rw", "mu"), w, mu_b, ("mu_b",), psum[4], 4)
        bcast_row(rw["rr"][:], ("rw", "rr"), w, rr_b, ("rr_b",), psum[5], 5)
        if rr0:
            em.op(S, lambda e: e.activation(out=rw["tmp"][:, 0:w], in_=rw["l0"][:, 0:w],
                                            func=AF.Exp, scale=cst[0:1, C_NHALF:C_NHALF + 1]),
                  r=[("rw", "l0"), ("cst",)], w=[("rw", "tmp")])
            bcast_row(rw["tmp"][:], ("rw", "tmp"), w, rc_b, ("rc_b",), psum[4], 4)

    def res_src(tj):
        def f(c, a, b):
            return res[c][:, tj * TC + a: tj * TC + b]
        return f, [("res", c) for c in range(HC)]

    def ln_out_add(y_tiles, y_keys, sq_tiles, sq_keys, tj, off, w, final=False,
                   fin_bufs=None):
        ln_rows(lambda c, a, b: y_tiles[c][:, a:b], y_keys,
                sq_tiles, sq_keys, H, w, C_ELN)
        for c in range(HC):
            tsl = slice(tj * TC + off, tj * TC + off + w)
            em.op(V, lambda e, c=c: e.tensor_tensor(out=tA[:, 0:w],
                                                    in0=y_tiles[c][:, 0:w],
                                                    in1=mu_b[:, 0:w], op=OP.subtract),
                  r=[y_keys[c], ("mu_b",)], w=[("tA",)])
            em.op(V, lambda e: e.tensor_tensor(out=tB[:, 0:w], in0=tA[:, 0:w],
                                               in1=rr_b[:, 0:w], op=OP.mult),
                  r=[("tA",), ("rr_b",)], w=[("tB",)])
            if final:
                fb = fin_bufs[c % len(fin_bufs)]
                fk = ("finb", c % len(fin_bufs))
                em.op(V, lambda e, c=c, tsl=tsl, fb=fb: e.tensor_tensor(
                    out=fb[:, 0:w], in0=res[c][:, tsl], in1=tB[:, 0:w], op=OP.add),
                    r=[("res", c), ("tB",)], w=[fk])
                em.op(SY, lambda e, c=c, tsl=tsl, fb=fb: e.dma_start(
                    out=y_out[c * 128:(c + 1) * 128, tsl], in_=fb[:, 0:w]),
                    r=[fk], w=[("y_out",)], dma=True, tag="store_y")
            else:
                em.op(V, lambda e, c=c, tsl=tsl: e.tensor_tensor(
                    out=res[c][:, tsl], in0=res[c][:, tsl], in1=tB[:, 0:w], op=OP.add),
                    r=[("res", c), ("tB",)], w=[("res", c)], tag="res_upd")

    def preconv_all():
        i = 0
        for n in PRECONV:
            srcd = w_in[n]
            rows, cols = srcd.shape
            key = {"w_gate": "wg", "w_up": "wu", "w_down": "wd"}.get(n, n)
            for r0 in range(0, rows, 128):
                for c0 in range(0, cols, 1024):
                    cw = min(1024, cols - c0)
                    b = i % 2
                    i += 1
                    em.op(G, lambda e, b=b, srcd=srcd, r0=r0, c0=c0, cw=cw: e.dma_start(
                        out=cv[b][:, 0:cw], in_=srcd[r0:r0 + 128, c0:c0 + cw]),
                        r=[("win", n)], w=[("cv", b)], dma=True, tag="pc_ld")
                    em.op(G, lambda e, b=b, n=n, r0=r0, c0=c0, cw=cw: e.dma_start(
                        out=wbf[n][r0:r0 + 128, c0:c0 + cw], in_=cv[b][:, 0:cw]),
                        r=[("cv", b)], w=[("wbf", key, r0 // 128, c0 // 1024)],
                        dma=True, tag="pc_st")

    def make_quant(qes, which, pfx=""):
        qsb = lambda n, sh, dt=F32: qes.enter_context(nc.sbuf_tensor(pfx + n, sh, dt))
        NG = QW // GRP
        S2 = 2
        qw = [qsb(f"qw{i}", [128, QW]) for i in range(2)]
        qc1 = [qsb(f"qc1{i}", [128, QW]) for i in range(S2)]
        qd1 = [qsb(f"qd1{i}", [128, QW]) for i in range(S2)]
        qV = qd1
        qwb = [qsb(f"qwb{i}", [128, QW], BF16) for i in range(S2)]
        qg1 = [qsb(f"qg1{i}", [128, QW], BF16) for i in range(S2)]
        qout = [qsb(f"qout{i}", [128, QW], BF16) for i in range(S2)]
        qg2 = qout
        qs_ = [qsb(f"qs{i}", [128, NG]) for i in range(S2)]
        qm = [qsb(f"qm{i}", [128, NG]) for i in range(S2)]
        qt_ = [qsb(f"qt{i}", [128, NG]) for i in range(S2)]
        qnt = [qsb(f"qnt{i}", [128, NG]) for i in range(S2)]
        qsc = [qsb(f"qsc{i}", [128, NG], BF16) for i in range(S2)]
        qtm = [qsb(f"qtm{i}", [128, NG]) for i in range(S2)]

        def bcg(t):
            a = t[:]
            return bass.AP(a.tensor, a.offset, [a.ap[0], [1, NG], [0, GRP]])

        def chunk(src_, skey, dst, dkey, r0, c0, qwi):
            s = qwi
            qwt, qwk = qw[qwi], (pfx, "qw", qwi)
            em.op(SY, lambda e: e.dma_start(out=qwt[:], in_=src_[r0:r0 + 128, c0:c0 + QW]),
                  r=[skey], w=[qwk], dma=True)
            em.op(V, lambda e: e.scalar_tensor_tensor(out=qc1[s][:], in0=qwt[:], scalar=V45C,
                                                      in1=qwt[:], op0=OP.mult, op1=OP.add),
                  r=[qwk], w=[(pfx, "qc1", s)])
            em.op(V, lambda e: e.tensor_tensor(out=qd1[s][:], in0=qc1[s][:], in1=qwt[:],
                                               op=OP.subtract),
                  r=[(pfx, "qc1", s), qwk], w=[(pfx, "qd1", s)])
            em.op(V, lambda e: e.tensor_tensor(out=qV[s][:], in0=qc1[s][:], in1=qd1[s][:],
                                               op=OP.subtract),
                  r=[(pfx, "qc1", s), (pfx, "qd1", s)], w=[(pfx, "qd1", s)])
            em.op(V, lambda e: e.scalar_tensor_tensor(out=qwb[s][:], in0=qwt[:],
                                                      scalar=dcol[:, 0:1], in1=qV[s][:],
                                                      op0=OP.mult, op1=OP.add),
                  r=[qwk, (pfx, "qd1", s), ("dcol",)], w=[(pfx, "qwb", s)], tag="q_wb")
            em.op(V, lambda e: e.tensor_reduce(
                out=qs_[s][:], in_=qwb[s][:].rearrange("p (g k) -> p g k", k=GRP),
                axis=AX.X, op=OP.add, apply_absolute_value=True),
                r=[(pfx, "qwb", s)], w=[(pfx, "qs_", s)], tag="q_red")
            em.op(V, lambda e: e.tensor_scalar(out=qs_[s][:], in0=qs_[s][:], scalar1=1.0 / GRP,
                                               scalar2=None, op0=OP.mult),
                  r=[(pfx, "qs_", s)], w=[(pfx, "qs_", s)])
            em.op(V, lambda e: e.scalar_tensor_tensor(out=qm[s][:], in0=qs_[s][:], scalar=V29C,
                                                      in1=qs_[s][:], op0=OP.mult, op1=OP.add),
                  r=[(pfx, "qs_", s)], w=[(pfx, "qm", s)])
            em.op(V, lambda e: e.tensor_tensor(out=qtm[s][:], in0=qm[s][:], in1=qs_[s][:],
                                               op=OP.subtract),
                  r=[(pfx, "qm", s), (pfx, "qs_", s)], w=[(pfx, "qtm", s)])
            em.op(V, lambda e: e.tensor_tensor(out=qm[s][:], in0=qm[s][:], in1=qtm[s][:],
                                               op=OP.subtract),
                  r=[(pfx, "qm", s), (pfx, "qtm", s)], w=[(pfx, "qm", s)])
            em.op(V, lambda e: e.scalar_tensor_tensor(out=qm[s][:], in0=qs_[s][:],
                                                      scalar=dcol[:, 0:1], in1=qm[s][:],
                                                      op0=OP.mult, op1=OP.add),
                  r=[(pfx, "qs_", s), (pfx, "qm", s), ("dcol",)], w=[(pfx, "qm", s)])
            em.op(V, lambda e: e.tensor_scalar(out=qm[s][:], in0=qm[s][:], scalar1=1e-8,
                                               scalar2=None, op0=OP.max),
                  r=[(pfx, "qm", s)], w=[(pfx, "qm", s)])
            em.op(V, lambda e: e.scalar_tensor_tensor(out=qtm[s][:], in0=qm[s][:], scalar=V45C,
                                                      in1=qm[s][:], op0=OP.mult, op1=OP.add),
                  r=[(pfx, "qm", s)], w=[(pfx, "qtm", s)])
            em.op(V, lambda e: e.tensor_tensor(out=qt_[s][:], in0=qtm[s][:], in1=qm[s][:],
                                               op=OP.subtract),
                  r=[(pfx, "qtm", s), (pfx, "qm", s)], w=[(pfx, "qt_", s)])
            em.op(V, lambda e: e.tensor_tensor(out=qt_[s][:], in0=qtm[s][:], in1=qt_[s][:],
                                               op=OP.subtract),
                  r=[(pfx, "qtm", s), (pfx, "qt_", s)], w=[(pfx, "qt_", s)])
            em.op(V, lambda e: e.scalar_tensor_tensor(out=qsc[s][:], in0=qm[s][:],
                                                      scalar=dcol[:, 0:1], in1=qt_[s][:],
                                                      op0=OP.mult, op1=OP.add),
                  r=[(pfx, "qm", s), (pfx, "qt_", s), ("dcol",)], w=[(pfx, "qsc", s)], tag="q_sc")
            em.op(V, lambda e: e.tensor_scalar(out=qt_[s][:], in0=qsc[s][:], scalar1=C_T,
                                               scalar2=None, op0=OP.mult),
                  r=[(pfx, "qsc", s)], w=[(pfx, "qt_", s)])
            em.op(V, lambda e: e.tensor_scalar(out=qnt[s][:], in0=qt_[s][:], scalar1=-1.0,
                                               scalar2=None, op0=OP.mult),
                  r=[(pfx, "qt_", s)], w=[(pfx, "qnt", s)])
            em.op(V, lambda e: e.tensor_tensor(
                out=qg1[s][:].rearrange("p (g k) -> p g k", k=GRP),
                in0=qwb[s][:].rearrange("p (g k) -> p g k", k=GRP),
                in1=bcg(qt_[s]), op=OP.is_gt),
                r=[(pfx, "qwb", s), (pfx, "qt_", s)], w=[(pfx, "qg1", s)], tag="q_g1")
            em.op(V, lambda e: e.tensor_tensor(
                out=qg2[s][:].rearrange("p (g k) -> p g k", k=GRP),
                in0=qwb[s][:].rearrange("p (g k) -> p g k", k=GRP),
                in1=bcg(qnt[s]), op=OP.is_lt),
                r=[(pfx, "qwb", s), (pfx, "qnt", s)], w=[(pfx, "qout", s)], tag="q_g2")
            em.op(V, lambda e: e.tensor_tensor(out=qg1[s][:], in0=qg1[s][:], in1=qg2[s][:],
                                               op=OP.subtract),
                  r=[(pfx, "qg1", s), (pfx, "qout", s)], w=[(pfx, "qg1", s)])
            em.op(V, lambda e: e.tensor_tensor(
                out=qout[s][:].rearrange("p (g k) -> p g k", k=GRP),
                in0=qg1[s][:].rearrange("p (g k) -> p g k", k=GRP),
                in1=bcg(qsc[s]), op=OP.mult),
                r=[(pfx, "qg1", s), (pfx, "qsc", s)], w=[(pfx, "qout", s)], tag="q_qs")
            em.op(SY, lambda e: e.dma_start(out=dst[r0:r0 + 128, c0:c0 + QW],
                                            in_=qout[s][:]),
                  r=[(pfx, "qout", s)], w=[dkey], dma=True, tag="q_st")

        chunks = []
        if "fc" in which:
            for r0 in range(0, I, 128):
                for c0 in range(0, H, QW):
                    chunks.append((w_in["w_fc"], ("win", "w_fc"), qfc, ("qfc",), r0, c0))
        if "pr" in which:
            for r0 in range(0, H, 128):
                for c0 in range(0, I, QW):
                    chunks.append((w_in["w_proj"], ("win", "w_proj"), qpr, ("qpr",), r0, c0))
        state = {"i": 0}

        def pump(n):
            while n > 0 and state["i"] < len(chunks):
                chunk(*chunks[state["i"]], qwi=state["i"] % 2)
                state["i"] += 1
                n -= 1
        return pump, len(chunks)

    def l0_attn():
        with contextlib.ExitStack() as pes:
            psb = lambda n, sh, dt=BF16: pes.enter_context(nc.sbuf_tensor(n, sh, dt))
            xk = [psb(f"xk{c}", [128, TC]) for c in range(HC)]
            xv = [psb(f"xv{c}", [128, TC]) for c in range(HC)]
            xr = [psb(f"xr{c}", [128, TC]) for c in range(HC)]
            rp = [psb(f"rp{c}", [128, TC]) for c in range(HC)]
            wkv = [psb(f"wkv{c}", [128, TC]) for c in range(HC)]
            rwk = xk
            ybf = xr
            ek4 = [pes.enter_context(nc.sbuf_tensor(f"ek4_{m}", [128, TC], F32))
                   for m in range(4)]
            vv4 = [pes.enter_context(nc.sbuf_tensor(f"vv4_{m}", [128, TC], F32))
                   for m in range(4)]
            xl1 = pes.enter_context(nc.sbuf_tensor("xl1", [128, TC + 1], BF16))
            cv[0] = pes.enter_context(nc.sbuf_tensor("cvp0", [128, 1024], BF16))
            cv[1] = pes.enter_context(nc.sbuf_tensor("cvp1", [128, 1024], BF16))
            preconv_all()
            scN = pes.enter_context(nc.sbuf_tensor("scN", [128, TC + 1], F32))
            scD = pes.enter_context(nc.sbuf_tensor("scD", [128, TC + 1], F32))
            tC[0] = pes.enter_context(nc.sbuf_tensor("tCa", [128, TC], F32))
            tD_ = tC[0]
            em.op(S, lambda e: preload_set(e, "natural_log_exp_and_others"), tag="tbl")
            for tj in range(TN):
                tsl = slice(tj * TC, (tj + 1) * TC)
                src, skeys = res_src(tj)
                ln_rows(src, skeys, rp, [("rp", c) for c in range(HC)], H, TC,
                        C_ELN, fused=True, rr0=True)
                for c in range(HC):
                    em.op(V, lambda e, c=c, tsl=tsl: e.tensor_tensor(
                        out=tA[:], in0=res[c][:, tsl], in1=mu_b[:], op=OP.subtract),
                        r=[("res", c), ("mu_b",)], w=[("tA",)])
                    em.op(V, lambda e: e.tensor_tensor(out=tB[:], in0=tA[:], in1=rr_b[:],
                                                       op=OP.mult),
                          r=[("tA",), ("rr_b",)], w=[("tB",)])
                    em.op(V, lambda e, c=c: e.tensor_copy(out=xl1[:, 0:1],
                                                          in_=xprev[:, c:c + 1]),
                          r=[("xprev",)], w=[("xl1",)])
                    em.op(V, lambda e, c=c: e.tensor_scalar(
                        out=xl1[:, 1:TC + 1], in0=tB[:], scalar1=pc["ln_w"][:, c:c + 1],
                        scalar2=pc["ln_b"][:, c:c + 1], op0=OP.mult, op1=OP.add),
                        r=[("tB",), ("pc", "ln_w"), ("pc", "ln_b")], w=[("xl1",)], tag="xl1")
                    em.op(V, lambda e, c=c: e.tensor_copy(out=xprev[:, c:c + 1],
                                                          in_=xl1[:, TC:TC + 1]),
                          r=[("xl1",)], w=[("xprev",)])
                    em.op(V, lambda e: e.tensor_tensor(out=tC[0][:], in0=xl1[:, 1:TC + 1],
                                                       in1=xl1[:, 0:TC], op=OP.subtract),
                          r=[("xl1",)], w=[("tC",)])
                    for mix, dst, dk in (("mix_k", xk, "xk"), ("mix_v", xv, "xv"),
                                         ("mix_r", xr, "xr")):
                        em.op(V, lambda e, c=c, mix=mix, dst=dst: e.scalar_tensor_tensor(
                            out=dst[c][:], in0=tC[0][:], scalar=pc[mix][:, c:c + 1],
                            in1=xl1[:, 0:TC], op0=OP.mult, op1=OP.add),
                            r=[("tC",), ("xl1",), ("pc", mix)], w=[(dk, c)], tag="xi")
                nm = min(4, HC)
                for cq in range(max(1, HC // nm)):
                    for hc in range(HC):
                        b = hc % 4
                        load_piece(b, w_in["wkT"], ("win", "wkT"), hc * 128, cq * nm * 128,
                                   ncols=nm * 128)
                        for m in range(nm):
                            em.op(TE, lambda e, hc=hc, m=m, b=b, st=(hc == 0): e.matmul(
                                psum[m][:, 0:TC], wp[b][:, m * 128:(m + 1) * 128],
                                xk[hc][:, 0:TC], start=st, stop=(hc == HC - 1)),
                                r=[("wp", b), ("xk", hc)], w=[("psum", m)], tag="mm_k")
                    for m in range(nm):
                        em.op(S, lambda e, m=m: e.activation(out=ek4[m][:], in_=psum[m][:, 0:TC],
                                                             func=AF.Exp),
                              r=[("psum", m)], w=[("ek4", m)], tag="ek")
                    for hc in range(HC):
                        b = 4 + hc % 4
                        load_piece(b, w_in["wvT"], ("win", "wvT"), hc * 128, cq * nm * 128,
                                   ncols=nm * 128)
                        for m in range(nm):
                            em.op(TE, lambda e, hc=hc, m=m, b=b, st=(hc == 0): e.matmul(
                                psum[m][:, 0:TC], wp[b][:, m * 128:(m + 1) * 128],
                                xv[hc][:, 0:TC], start=st, stop=(hc == HC - 1)),
                                r=[("wp", b), ("xv", hc)], w=[("psum", m)], tag="mm_v")
                    for m in range(nm):
                        em.op(S, lambda e, m=m: e.copy(out=vv4[m][:], in_=psum[m][:, 0:TC]),
                              r=[("psum", m)], w=[("vv4", m)], tag="v_ev")
                    for hc in range(HC):
                        b = hc % 4
                        load_piece(b, w_in["wrT"], ("win", "wrT"), hc * 128, cq * nm * 128,
                                   ncols=nm * 128)
                        for m in range(nm):
                            em.op(TE, lambda e, hc=hc, m=m, b=b, st=(hc == 0): e.matmul(
                                psum[m][:, 0:TC], wp[b][:, m * 128:(m + 1) * 128],
                                xr[hc][:, 0:TC], start=st, stop=(hc == HC - 1)),
                                r=[("wp", b), ("xr", hc)], w=[("psum", m)], tag="mm_r")
                    for m in range(nm):
                        co = cq * nm + m
                        em.op(S, lambda e, m=m, co=co: e.copy(out=rp[co][:],
                                                              in_=psum[m][:, 0:TC]),
                              r=[("psum", m)], w=[("rp", co)], tag="rp_ev")
                    for m in range(nm):
                        co = cq * nm + m
                        em.op(V, lambda e, m=m: e.tensor_tensor(out=tD_[:], in0=ek4[m][:],
                                                                in1=vv4[m][:], op=OP.mult),
                              r=[("ek4", m), ("vv4", m)], w=[("tC",)])
                        em.op(V, lambda e, co=co: e.tensor_copy(out=scN[:, 0:1],
                                                                in_=stN[:, co:co + 1]),
                              r=[("stN",)], w=[("scN",)])
                        em.op(V, lambda e, co=co: e.tensor_tensor_scan(
                            out=scN[:, 1:TC + 1],
                            data0=bass.AP(pc["wdec"][:].tensor,
                                          pc["wdec"][:].offset + co,
                                          [pc["wdec"][:].ap[0], [0, TC]]),
                            data1=tD_[:],
                            initial=stN[:, co:co + 1], op0=OP.mult, op1=OP.add),
                            r=[("pc", "wdec"), ("tC",), ("stN",)], w=[("scN",)], tag="scanN")
                        em.op(V, lambda e, co=co: e.tensor_copy(out=stN[:, co:co + 1],
                                                                in_=scN[:, TC:TC + 1]),
                              r=[("scN",)], w=[("stN",)])
                        em.op(V, lambda e, co=co: e.tensor_copy(out=scD[:, 0:1],
                                                                in_=stD[:, co:co + 1]),
                              r=[("stD",)], w=[("scD",)])
                        em.op(V, lambda e, co=co, m=m: e.tensor_tensor_scan(
                            out=scD[:, 1:TC + 1],
                            data0=bass.AP(pc["wdec"][:].tensor,
                                          pc["wdec"][:].offset + co,
                                          [pc["wdec"][:].ap[0], [0, TC]]),
                            data1=ek4[m][:],
                            initial=stD[:, co:co + 1], op0=OP.mult, op1=OP.add),
                            r=[("pc", "wdec"), ("ek4", m), ("stD",)], w=[("scD",)], tag="scanD")
                        em.op(V, lambda e, co=co: e.tensor_copy(out=stD[:, co:co + 1],
                                                                in_=scD[:, TC:TC + 1]),
                              r=[("scD",)], w=[("stD",)])
                        em.op(V, lambda e, co=co: e.tensor_scalar(
                            out=tA[:], in0=tD_[:], scalar1=pc["ecw"][:, co:co + 1],
                            scalar2=None, op0=OP.mult),
                            r=[("tC",), ("pc", "ecw")], w=[("tA",)], tag="ecurv")
                        em.op(V, lambda e: e.tensor_tensor(out=tA[:], in0=tA[:],
                                                           in1=scN[:, 0:TC], op=OP.add),
                              r=[("tA",), ("scN",)], w=[("tA",)])
                        em.op(V, lambda e, m=m, co=co: e.tensor_scalar(
                            out=tB[:], in0=ek4[m][:], scalar1=pc["ecw"][:, co:co + 1],
                            scalar2=None, op0=OP.mult),
                            r=[("ek4", m), ("pc", "ecw")], w=[("tB",)], tag="ecur")
                        em.op(V, lambda e: e.tensor_tensor(out=tB[:], in0=tB[:],
                                                           in1=scD[:, 0:TC], op=OP.add),
                              r=[("tB",), ("scD",)], w=[("tB",)])
                        em.op(V, lambda e: e.reciprocal(out=tC[0][:], in_=tB[:]),
                              r=[("tB",)], w=[("tC",)], tag="recip")
                        em.op(V, lambda e, co=co: e.tensor_tensor(out=wkv[co][:], in0=tA[:],
                                                                  in1=tC[0][:], op=OP.mult),
                              r=[("tA",), ("tC",)], w=[("wkv", co)], tag="wkv")
                em.op(S, lambda e: preload_set(e, "exp_and_others"), tag="tbl_t")
                for co in range(HC):
                    em.op(S, lambda e, co=co: e.activation(out=rp[co][:], in_=rp[co][:],
                                                           func=AF.Tanh,
                                                           scale=cst[:, C_HALF:C_HALF + 1]),
                          r=[("rp", co), ("cst",)], w=[("rp", co)], tag="tanh")
                    em.op(V, lambda e, co=co: e.tensor_scalar(out=tA[:], in0=rp[co][:],
                                                              scalar1=0.5, scalar2=0.5,
                                                              op0=OP.mult, op1=OP.add),
                          r=[("rp", co)], w=[("tA",)])
                    em.op(V, lambda e, co=co: e.tensor_tensor(out=rwk[co][:], in0=tA[:],
                                                              in1=wkv[co][:], op=OP.mult),
                          r=[("tA",), ("wkv", co)], w=[("xk", co)], tag="rwkv")
                em.op(S, lambda e: preload_set(e, "natural_log_exp_and_others"), tag="tbl_b")
                for cq in range(max(1, HC // 4)):
                    nm = min(4, HC)
                    for hc in range(HC):
                        b = 4 + hc % 4
                        load_piece(b, w_in["woT"], ("win", "woT"), hc * 128, cq * nm * 128,
                                   ncols=nm * 128)
                        for m in range(nm):
                            em.op(TE, lambda e, hc=hc, m=m, b=b, st=(hc == 0): e.matmul(
                                psum[m][:, 0:TC], wp[b][:, m * 128:(m + 1) * 128],
                                rwk[hc][:, 0:TC], start=st, stop=(hc == HC - 1)),
                                r=[("wp", b), ("xk", hc)], w=[("psum", m)], tag="mm_wo")
                    for m in range(nm):
                        co = cq * nm + m
                        em.op(V, lambda e, co=co, tsl=tsl: e.tensor_tensor(
                            out=tA[:], in0=res[co][:, tsl], in1=mu_b[:], op=OP.subtract),
                            r=[("res", co), ("mu_b",)], w=[("tA",)], tag="ln0a")
                        em.op(V, lambda e: e.tensor_tensor(out=tB[:], in0=tA[:],
                                                           in1=rc_b[:], op=OP.mult),
                              r=[("tA",), ("rc_b",)], w=[("tB",)], tag="ln0b")
                        em.op(V, lambda e, m=m, co=co: e.tensor_tensor(
                            out=ybf[co][:], in0=tB[:], in1=psum[m][:, 0:TC], op=OP.add),
                            r=[("tB",), ("psum", m)], w=[("xr", co)], tag="y0")
                ln_out_add(ybf, [("xr", c) for c in range(HC)], rp,
                           [("rp", c) for c in range(HC)], tj, 0, TC)
        em.barrier("l0a")

    def l0_ffn():
        with contextlib.ExitStack() as pes:
            psb = lambda n, sh, dt=BF16: pes.enter_context(nc.sbuf_tensor(n, sh, dt))
            xf = [psb(f"xf{c}", [128, TC]) for c in range(HC)]
            hsp = [psb(f"hsp{i}", [128, TC]) for i in range(IC)]
            sg4 = [psb(f"sg{m}", [128, TC]) for m in range(2)]
            ybf = xf
            tD_ = pes.enter_context(nc.sbuf_tensor("tDf", [128, TC], F32))
            tCf = pes.enter_context(nc.sbuf_tensor("tCf", [128, TC], F32))
            tE = pes.enter_context(nc.sbuf_tensor("tE", [128, TC], F32))
            tF = pes.enter_context(nc.sbuf_tensor("tF", [128, TC], F32))
            tG = pes.enter_context(nc.sbuf_tensor("tG", [128, TC], F32))
            tH = pes.enter_context(nc.sbuf_tensor("tH", [128, TC], F32))
            scN = pes.enter_context(nc.sbuf_tensor("scN2", [128, TC], F32))
            scD = pes.enter_context(nc.sbuf_tensor("scD2", [128, TC], F32))
            wb_t = pes.enter_context(nc.sbuf_tensor("wb_t2", [128, TC], F32))
            t2q = pes.enter_context(nc.sbuf_tensor("t2q", [128, TC], F32))
            for tj in range(TN):
                tsl = slice(tj * TC, (tj + 1) * TC)
                src, skeys = res_src(tj)
                ln_rows(src, skeys, hsp, [("hsp", c) for c in range(HC)], H, TC, C_ELN)
                for c in range(HC):
                    em.op(V, lambda e, c=c, tsl=tsl: e.tensor_tensor(
                        out=tA[:], in0=res[c][:, tsl], in1=mu_b[:], op=OP.subtract),
                        r=[("res", c), ("mu_b",)], w=[("tA",)])
                    em.op(V, lambda e, c=c: e.tensor_tensor(out=xf[c][:], in0=tA[:],
                                                            in1=rr_b[:], op=OP.mult),
                          r=[("tA",), ("rr_b",)], w=[("xf", c)], tag="xffn")
                em.op(S, lambda e: preload_set(e, "silu_and_others"), tag="tbl_s")
                for icq in range(IC // 4):
                    for hc in range(HC):
                        b = hc % 4
                        load_piece(b, w_in["w_gate"], ("win", "wg"), hc * 128, icq * 512)
                        for m in range(4):
                            em.op(TE, lambda e, hc=hc, m=m, b=b, st=(hc == 0): e.matmul(
                                psum[m][:, 0:TC], wp[b][:, m * 128:(m + 1) * 128],
                                xf[hc][:, 0:TC], start=st, stop=(hc == HC - 1)),
                                r=[("wp", b), ("xf", hc)], w=[("psum", m)], tag="mm_g")
                    for m in (0, 1):
                        em.op(S, lambda e, m=m: e.activation(out=sg4[m][:], in_=psum[m][:, 0:TC],
                                                             func=AF.Silu),
                              r=[("psum", m)], w=[("sg", m)], tag="silu")
                    ub = [4, 5, 0, 1]
                    for hc in range(HC):
                        b = 4 + hc % 4
                        load_piece(b, w_in["w_up"], ("win", "wu"), hc * 128, icq * 512)
                        for m in range(4):
                            em.op(TE, lambda e, hc=hc, m=m, b=b, st=(hc == 0): e.matmul(
                                psum[ub[m]][:, 0:TC], wp[b][:, m * 128:(m + 1) * 128],
                                xf[hc][:, 0:TC], start=st, stop=(hc == HC - 1)),
                                r=[("wp", b), ("xf", hc)], w=[("psum", ub[m])], tag="mm_u")
                    hts = (tA, tB, tCf, tD_)
                    ats = (tE, tF, tG, tH)
                    tkeys = [("tA",), ("tB",), ("tCf",), ("tD_",)]
                    akeys = [("tE",), ("tF",), ("tG",), ("tH",)]
                    for m in range(4):
                        if m == 2:
                            em.op(S, lambda e: e.activation(out=sg4[0][:], in_=psum[2][:, 0:TC],
                                                            func=AF.Silu),
                                  r=[("psum", 2)], w=[("sg", 0)], tag="silu")
                            em.op(S, lambda e: e.activation(out=sg4[1][:], in_=psum[3][:, 0:TC],
                                                            func=AF.Silu),
                                  r=[("psum", 3)], w=[("sg", 1)], tag="silu")
                        em.op(V, lambda e, m=m: e.tensor_tensor(out=hts[m][:], in0=sg4[m % 2][:],
                                                                in1=psum[ub[m]][:, 0:TC],
                                                                op=OP.mult),
                              r=[("sg", m % 2), ("psum", ub[m])], w=[tkeys[m]], tag="h")
                        em.op(S, lambda e, m=m: e.activation(out=ats[m][:], in_=hts[m][:],
                                                             func=AF.Abs),
                              r=[tkeys[m]], w=[akeys[m]], tag="abs")
                    em.op(V, lambda e: e.tensor_tensor(out=wb_t[:], in0=ats[0][:],
                                                       in1=ats[1][:], op=OP.max),
                          r=[akeys[0], akeys[1]], w=[("wb_t",)])
                    em.op(V, lambda e: e.tensor_tensor(out=scN[:], in0=ats[2][:],
                                                       in1=ats[3][:], op=OP.max),
                          r=[akeys[2], akeys[3]], w=[("scN",)])
                    em.op(V, lambda e: e.tensor_tensor(out=t2q[:], in0=wb_t[:],
                                                       in1=scN[:], op=OP.min),
                          r=[("wb_t",), ("scN",)], w=[("t2q",)])
                    em.op(V, lambda e: e.tensor_tensor(out=wb_t[:], in0=ats[0][:],
                                                       in1=ats[1][:], op=OP.min),
                          r=[akeys[0], akeys[1]], w=[("wb_t",)])
                    em.op(V, lambda e: e.tensor_tensor(out=scN[:], in0=ats[2][:],
                                                       in1=ats[3][:], op=OP.min),
                          r=[akeys[2], akeys[3]], w=[("scN",)])
                    em.op(V, lambda e: e.tensor_tensor(out=scD[:], in0=wb_t[:],
                                                       in1=scN[:], op=OP.max),
                          r=[("wb_t",), ("scN",)], w=[("scD",)])
                    em.op(V, lambda e: e.tensor_tensor(out=wb_t[:], in0=t2q[:],
                                                       in1=scD[:], op=OP.max),
                          r=[("t2q",), ("scD",)], w=[("wb_t",)], tag="t2")
                    for m in range(4):
                        it = icq * 4 + m
                        em.op(V, lambda e, m=m: e.tensor_tensor(out=ats[m][:], in0=ats[m][:],
                                                                in1=wb_t[:], op=OP.is_ge),
                              r=[akeys[m], ("wb_t",)], w=[akeys[m]], tag="mask")
                        em.op(V, lambda e, m=m, it=it: e.tensor_tensor(
                            out=hsp[it][:], in0=ats[m][:], in1=hts[m][:], op=OP.mult),
                            r=[akeys[m], tkeys[m]], w=[("hsp", it)], tag="hsp")
                em.op(S, lambda e: preload_set(e, "natural_log_exp_and_others"), tag="tbl_b2")
                for oc in range(HC):
                    for icq in range(IC // 4):
                        b = icq % 4
                        a = wbf["w_down"][icq * 512:(icq + 1) * 512,
                                          oc * 128:(oc + 1) * 128].rearrange(
                            "(kc p) n -> p kc n", p=128)
                        rkeys = [("wbf", "wd", icq * 4 + d, (oc * 128) // 1024)
                                 for d in range(4)]
                        em.op(SY, lambda e, a=a, b=b: e.dma_start(
                            out=wp[b][:].rearrange("p (kc n) -> p kc n", n=128), in_=a),
                            r=rkeys, w=[("wp", b)], dma=True, tag="wd_ld")
                        for k4 in range(4):
                            ic = icq * 4 + k4
                            em.op(TE, lambda e, b=b, k4=k4, ic=ic, st=(ic == 0): e.matmul(
                                psum[2][:, 0:TC], wp[b][:, k4 * 128:(k4 + 1) * 128],
                                hsp[ic][:, 0:TC], start=st, stop=(ic == IC - 1)),
                                r=[("wp", b), ("hsp", ic)], w=[("psum", 2)], tag="mm_d")
                    em.op(S, lambda e, oc=oc: e.copy(out=ybf[oc][:], in_=psum[2][:, 0:TC]),
                          r=[("psum", 2)], w=[("xf", oc)], tag="yffn")
                ln_out_add(ybf, [("xf", c) for c in range(HC)], hsp,
                           [("hsp", c) for c in range(HC)], tj, 0, TC)
        em.barrier("l0f")

    def l1_attn():
        with contextlib.ExitStack() as pes:
            psb = lambda n, sh, dt=BF16: pes.enter_context(nc.sbuf_tensor(n, sh, dt))
            stb = [psb(f"stb{r}", [128, T]) for r in range(RC)]
            acb = [psb(f"acb{r}", [128, T]) for r in range(RC)]
            xtok = [psb(f"xt{j}", [128, H]) for j in range(JB)]
            em.op(S, lambda e: preload_set(e, "natural_log_exp_and_others"), tag="tblA")
            aes = contextlib.ExitStack()
            xf = [aes.enter_context(nc.sbuf_tensor(f"xa{c}", [128, TC], BF16))
                  for c in range(HC)]
            for tj in range(TN):
                tsl = slice(tj * TC, (tj + 1) * TC)
                src, skeys = res_src(tj)
                ln_rows(src, skeys, xf, [("xfa", c) for c in range(HC)], H, TC, C_ELN)
                for c in range(HC):
                    em.op(V, lambda e, c=c, tsl=tsl: e.tensor_tensor(
                        out=tA[:], in0=res[c][:, tsl], in1=mu_b[:], op=OP.subtract),
                        r=[("res", c), ("mu_b",)], w=[("tA",)])
                    em.op(V, lambda e, c=c: e.tensor_tensor(out=xf[c][:], in0=tA[:],
                                                            in1=rr_b[:], op=OP.mult),
                          r=[("tA",), ("rr_b",)], w=[("xfa", c)], tag="xl3")
                for rc in range(RC):
                    gsl = slice(tj * TC, (tj + 1) * TC)
                    for hc in range(HC):
                        b = hc % 4
                        load_piece(b, w_in["ws"], ("win", "ws"), hc * 128, rc * 128,
                                   ncols=128)
                        em.op(TE, lambda e, hc=hc, b=b, st=(hc == 0): e.matmul(
                            psum[0][:, 0:TC], wp[b][:, 0:128],
                            xf[hc][:, 0:TC], start=st, stop=(hc == HC - 1)),
                            r=[("wp", b), ("xfa", hc)], w=[("psum", 0)], tag="mm_s")
                    em.op(S, lambda e, gsl=gsl, rc=rc: e.copy(out=stb[rc][:, gsl],
                                                              in_=psum[0][:, 0:TC]),
                          r=[("psum", 0)], w=[("stb", rc)])
                    for hc in range(HC):
                        b = 4 + hc % 4
                        load_piece(b, w_in["wa"], ("win", "wa"), hc * 128, rc * 128,
                                   ncols=128)
                        em.op(TE, lambda e, hc=hc, b=b, st=(hc == 0): e.matmul(
                            psum[1][:, 0:TC], wp[b][:, 0:128],
                            xf[hc][:, 0:TC], start=st, stop=(hc == HC - 1)),
                            r=[("wp", b), ("xfa", hc)], w=[("psum", 1)], tag="mm_a")
                    em.op(S, lambda e, gsl=gsl, rc=rc: e.copy(out=acb[rc][:, gsl],
                                                              in_=psum[1][:, 0:TC]),
                          r=[("psum", 1)], w=[("acb", rc)])
                for c in range(HC):
                    for q in range(TC // 128):
                        jb = (tj * TC) // 128 + q
                        em.op(SY, lambda e, c=c, q=q, jb=jb: e.dma_start(
                            out=xtok[jb][:, c * 128:(c + 1) * 128],
                            in_=xf[c][:, q * 128:(q + 1) * 128], transpose=True),
                            r=[("xfa", c)], w=[("xtok", jb)], dma=True, tag="xtok")
            L1B = cfg.get("l1b_upto", 5)
            aes.close()
            em.barrier("l1ab")
            Ep = [psb(f"Ep{j}", [128, TCB]) for j in range(JB)]
            anb = [psb(f"anb{c}", [128, TCB]) for c in range(HC)]
            ybf = [psb(f"yba{c}", [128, TCB]) for c in range(HC)]
            xsq = Ep
            for iq in range(0 if cfg.get("l1a_only_A") else JB):
                isl = slice(iq * 128, (iq + 1) * 128)
                for jb in range(iq + 1):
                    for rc in range(RC):
                        em.op(TE, lambda e, jb=jb, rc=rc, isl=isl, st=(rc == 0): e.matmul(
                            psum[0][:, 0:TCB], acb[rc][:, jb * 128:(jb + 1) * 128],
                            stb[rc][:, isl], start=st, stop=(rc == RC - 1)),
                            r=[("acb", rc), ("stb", rc)], w=[("psum", 0)], tag="mm_sc")
                    if jb == iq:
                        em.op(V, lambda e: e.tensor_tensor(out=tA[:, 0:TCB],
                                                           in0=psum[0][:, 0:TCB],
                                                           in1=amask[:], op=OP.add),
                              r=[("psum", 0), ("amask",)], w=[("tA",)], tag="mask")
                        em.op(S, lambda e, jb=jb: e.activation(out=Ep[jb][:], in_=tA[:, 0:TCB],
                                                               func=AF.Exp,
                                                               scale=cst[:, C_I16:C_I16 + 1]),
                              r=[("tA",), ("cst",)], w=[("Ep", jb)], tag="expS")
                    else:
                        em.op(S, lambda e, jb=jb: e.activation(out=Ep[jb][:],
                                                               in_=psum[0][:, 0:TCB],
                                                               func=AF.Exp,
                                                               scale=cst[:, C_I16:C_I16 + 1]),
                              r=[("psum", 0), ("cst",)], w=[("Ep", jb)], tag="expS")
                if L1B < 2:
                    continue
                for jb in range(iq + 1):
                    em.op(TE, lambda e, jb=jb, iq=iq, st=(jb == 0): e.matmul(
                        prow[0:1, 0:TCB], ones_bf[:], Ep[jb][:],
                        start=st, stop=(jb == iq)),
                        r=[("ones_bf",), ("Ep", jb)], w=[("prow",)], tag="rs")
                em.op(V, lambda e: e.tensor_copy(out=rw["s1"][:, 0:TCB], in_=prow[0:1, 0:TCB]),
                      r=[("prow",)], w=[("rw", "s1")])
                em.op(V, lambda e: e.reciprocal(out=rw["tmp"][:, 0:TCB],
                                                in_=rw["s1"][:, 0:TCB]),
                      r=[("rw", "s1")], w=[("rw", "tmp")], tag="rcp")
                bcast_row(rw["tmp"][:], ("rw", "tmp"), TCB, rc_b, ("rc_b",), psum[4], 4)
                if L1B < 3:
                    continue
                for hc in range(HC):
                    for jb in range(iq + 1):
                        em.op(TE, lambda e, jb=jb, hc=hc, iq=iq, st=(jb == 0): e.matmul(
                            psum[1][:, 0:TCB], xtok[jb][:, hc * 128:(hc + 1) * 128],
                            Ep[jb][:], start=st, stop=(jb == iq)),
                            r=[("xtok", jb), ("Ep", jb)], w=[("psum", 1)], tag="mm_av")
                    em.op(V, lambda e, hc=hc: e.tensor_tensor(out=anb[hc][:],
                                                              in0=psum[1][:, 0:TCB],
                                                              in1=rc_b[:, 0:TCB], op=OP.mult),
                          r=[("psum", 1), ("rc_b",)], w=[("anb", hc)], tag="attn")
                if L1B < 4:
                    continue
                nm = min(4, HC)
                for cq in range(max(1, HC // nm)):
                    for hc in range(HC):
                        b = 4 + hc % 4
                        load_piece(b, w_in["woT1"], ("win", "woT1"), hc * 128,
                                   cq * nm * 128, ncols=nm * 128)
                        for m in range(nm):
                            em.op(TE, lambda e, hc=hc, m=m, b=b, st=(hc == 0): e.matmul(
                                psum[m][:, 0:TCB], wp[b][:, m * 128:(m + 1) * 128],
                                anb[hc][:], start=st, stop=(hc == HC - 1)),
                                r=[("wp", b), ("anb", hc)], w=[("psum", m)], tag="mm_o1")
                    for m in range(nm):
                        oc = cq * nm + m
                        em.op(V, lambda e, m=m, oc=oc: e.tensor_scalar(
                            out=ybf[oc][:], in0=psum[m][:, 0:TCB],
                            scalar1=pc["bo"][:, oc:oc + 1], scalar2=None, op0=OP.add),
                            r=[("psum", m), ("pc", "bo")], w=[("yba", oc)], tag="y_att")
                if L1B < 5:
                    continue
                tj, off = (iq * 128) // TC, (iq * 128) % TC
                ln_out_add(ybf, [("yba", c) for c in range(HC)], xsq,
                           [("Ep", c) for c in range(HC)], tj, off, TCB)
        em.barrier("l1a")

    def l1_ffn():
        with contextlib.ExitStack() as pes:
            psb = lambda n, sh, dt=BF16: pes.enter_context(nc.sbuf_tensor(n, sh, dt))
            xf = [psb(f"x5{c}", [128, TC]) for c in range(HC)]
            fbufs = [pes.enter_context(nc.sbuf_tensor(f"fb{i}", [128, TC], F32))
                     for i in range(2)]
            hbf = [psb(f"hb{i}", [128, TC]) for i in range(IC)]
            ybf = xf
            xsq = [psb(f"q5{c}", [128, TC]) for c in range(HC)]
            for tj in range(TN):
                tsl = slice(tj * TC, (tj + 1) * TC)
                src, skeys = res_src(tj)
                ln_rows(src, skeys, xsq, [("xsq5", c) for c in range(HC)], H, TC, C_ELN)
                for c in range(HC):
                    em.op(V, lambda e, c=c, tsl=tsl: e.tensor_tensor(
                        out=tA[:], in0=res[c][:, tsl], in1=mu_b[:], op=OP.subtract),
                        r=[("res", c), ("mu_b",)], w=[("tA",)])
                    em.op(V, lambda e, c=c: e.tensor_tensor(out=xf[c][:], in0=tA[:],
                                                            in1=rr_b[:], op=OP.mult),
                          r=[("tA",), ("rr_b",)], w=[("x5", c)], tag="x5")
                for icq in range(IC // 4):
                    for hc in range(HC):
                        b = hc % 4
                        em.op(SY, lambda e, icq=icq, hc=hc, b=b: e.dma_start(
                            out=wp[b][:], in_=qfc[icq * 512:(icq + 1) * 512,
                                                 hc * 128:(hc + 1) * 128], transpose=True),
                            r=[("qfc",)], w=[("wp", b)], dma=True, tag="qfc_tr")
                        for m in range(4):
                            em.op(TE, lambda e, hc=hc, m=m, b=b, st=(hc == 0): e.matmul(
                                psum[m][:, 0:TC], wp[b][:, m * 128:(m + 1) * 128],
                                xf[hc][:, 0:TC], start=st, stop=(hc == HC - 1)),
                                r=[("wp", b), ("x5", hc)], w=[("psum", m)], tag="mm_fc")
                    for m in range(4):
                        ic = icq * 4 + m
                        em.op(S, lambda e, m=m: e.activation(out=tA[:], in_=psum[m][:, 0:TC],
                                                             func=AF.Relu),
                              r=[("psum", m)], w=[("tA",)], tag="relu")
                        em.op(V, lambda e, ic=ic: e.tensor_tensor(out=hbf[ic][:], in0=tA[:],
                                                                  in1=tA[:], op=OP.mult),
                              r=[("tA",)], w=[("hb", ic)], tag="hsq")
                for t0 in range(0, TC, 512):
                    n = min(512, TC - t0)
                    for ic in range(IC):
                        em.op(S, lambda e, ic=ic, t0=t0, n=n: e.activation(
                            out=xsq[ic % HC][:, t0:t0 + n], in_=hbf[ic][:, t0:t0 + n],
                            func=AF.Square),
                            r=[("hb", ic)], w=[("xsq5", ic % HC)], tag="h2")
                        em.op(TE, lambda e, ic=ic, t0=t0, n=n, st=(ic == 0): e.matmul(
                            prow[0:1, 0:n], ones_bf[:], xsq[ic % HC][:, t0:t0 + n],
                            start=st, stop=(ic == IC - 1)),
                            r=[("ones_bf",), ("xsq5", ic % HC)], w=[("prow",)], tag="rsum")
                    em.op(V, lambda e, t0=t0, n=n: e.tensor_copy(out=rw["s1"][:, t0:t0 + n],
                                                                 in_=prow[0:1, 0:n]),
                          r=[("prow",)], w=[("rw", "s1")])
                em.op(V, lambda e: e.tensor_scalar(out=rw["var"][:], in0=rw["s1"][:],
                                                   scalar1=1.0 / I, scalar2=None,
                                                   op0=OP.mult),
                      r=[("rw", "s1")], w=[("rw", "var")])
                em.op(S, lambda e: e.activation(out=rw["l0"][:], in_=rw["var"][:], func=AF.Ln,
                                                bias=cst[0:1, C_ERMS:C_ERMS + 1]),
                      r=[("rw", "var"), ("cst",)], w=[("rw", "l0")])
                em.op(S, lambda e: e.activation(out=rw["rr"][:], in_=rw["l0"][:], func=AF.Exp,
                                                scale=cst[0:1, C_NHALF:C_NHALF + 1]),
                      r=[("rw", "l0"), ("cst",)], w=[("rw", "rr")], tag="rms")
                bcast_row(rw["rr"][:], ("rw", "rr"), TC, rr_b, ("rr_b",), psum[5], 5)
                for ocq in range(max(1, HC // 4)):
                    nm = min(4, HC)
                    for icc in range(IC):
                        b = 4 + icc % 4
                        em.op(SY, lambda e, ocq=ocq, icc=icc, b=b, nm=nm: e.dma_start(
                            out=wp[b][0:128, 0:nm * 128],
                            in_=qpr[ocq * nm * 128:(ocq + 1) * nm * 128,
                                    icc * 128:(icc + 1) * 128], transpose=True),
                            r=[("qpr",)], w=[("wp", b)], dma=True, tag="qpr_tr")
                        for m in range(nm):
                            em.op(TE, lambda e, m=m, b=b, icc=icc, st=(icc == 0): e.matmul(
                                psum[m][:, 0:TC], wp[b][:, m * 128:(m + 1) * 128],
                                hbf[icc][:, 0:TC], start=st, stop=(icc == IC - 1)),
                                r=[("wp", b), ("hb", icc)], w=[("psum", m)], tag="mm_pr")
                    for m in range(nm):
                        oc = ocq * nm + m
                        em.op(V, lambda e, m=m, oc=oc: e.tensor_tensor(
                            out=ybf[oc][:], in0=psum[m][:, 0:TC], in1=rr_b[:, 0:TC],
                            op=OP.mult),
                            r=[("psum", m), ("rr_b",)], w=[("x5", oc)], tag="y5")
                ln_out_add(ybf, [("x5", c) for c in range(HC)], xsq,
                           [("xsq5", c) for c in range(HC)], tj, 0, TC, final=True,
                           fin_bufs=fbufs)
        em.barrier("l1f")

    phases = cfg.get("phases", "qaf1AF")
    if "q" in phases:
        with contextlib.ExitStack() as qscope:
            qpump, _nq = make_quant(qscope, "fcpr", "Q")
            qpump(10 ** 9)
        em.barrier("q")
    if "a" in phases:
        l0_attn()
    if "f" in phases:
        l0_ffn()
    if "A" in phases:
        l1_attn()
    if "F" in phases:
        l1_ffn()

    with nc.Block() as block:
        em.finalize(block, sems)
    es.close()
    return nc


def make_wmap(inputs, I):
    jj = np.arange(I)
    g = jj // 4
    m = jj % 4
    newpos = (g // 128) * 512 + m * 128 + (g % 128)
    perm = np.empty(I, dtype=np.int64)
    perm[newpos] = jj

    def C(a):
        return np.ascontiguousarray(a, dtype=np.float32)

    return {
        "wkT": C(inputs["L0_attn_Wk"].T), "wvT": C(inputs["L0_attn_Wv"].T),
        "wrT": C(inputs["L0_attn_Wr"].T), "woT": C(inputs["L0_attn_Wo"].T),
        "ln_w": C(inputs["L0_attn_ln_w"]), "ln_b": C(inputs["L0_attn_ln_b"]),
        "mix_k": C(inputs["L0_attn_mix_k"]), "mix_v": C(inputs["L0_attn_mix_v"]),
        "mix_r": C(inputs["L0_attn_mix_r"]), "td": C(inputs["L0_attn_time_decay"]),
        "cw": C(inputs["L0_attn_current_token_weight"]),
        "w_gate": C(inputs["L0_ffn_w_gate"][:, perm]),
        "w_up": C(inputs["L0_ffn_w_up"][:, perm]),
        "w_down": C(inputs["L0_ffn_w_down"][perm, :]),
        "ws": C(inputs["L1_attn_Ws"]), "wa": C(inputs["L1_attn_Wa"]),
        "woT1": C(inputs["L1_attn_Wo"].T), "bo": C(inputs["L1_attn_bo"]),
        "w_fc": C(inputs["L1_ffn_W_fc"]), "w_proj": C(inputs["L1_ffn_W_proj"]),
        "tri": np.where(np.arange(128)[None, :] < np.arange(128)[:, None],
                        np.float32(NEG_BIG), np.float32(0.0)),
    }


def kernel(**inputs):
    cfg = full_cfg()
    H, T, I = cfg["H"], cfg["T"], cfg["I"]
    B = inputs["L0_attn_hidden_state"].shape[0]
    x = np.ascontiguousarray(inputs["L0_attn_hidden_state"])
    wmap = make_wmap(inputs, I)
    nc = build_nc(cfg)
    in_maps = []
    for b in range(B):
        mm = dict(wmap)
        mm["x_in"] = np.ascontiguousarray(x[b].T, dtype=np.float32)
        in_maps.append(mm)
    res_ = run_bass_kernel_spmd(nc, in_maps, core_ids=list(range(B)))
    out = np.empty((B, T, H), dtype=np.float32)
    for b in range(B):
        out[b] = res_.results[b]["y_out"].T
    return out
